# version 23
# baseline (speedup 1.0000x reference)
import numpy as np
import concourse.bass as bass
import concourse.mybir as mybir
from concourse.bass_utils import run_bass_kernel_spmd

F32 = mybir.dt.float32
BF16 = mybir.dt.bfloat16
AF = mybir.ActivationFunctionType
ALU = mybir.AluOpType
AX = mybir.AxisListType

D = 1024
H = 16
DK = 64
E = 8
HALF = 1536
KD = D // 128
MD = D // 128
MF = HALF // 128
NEXP = E + 1


class Ctr:
    __slots__ = ("sem", "n")

    def __init__(self, sem):
        self.sem = sem
        self.n = 0

    def inc(self, k=1):
        self.n += k
        return self.n


class WaitFilter:

    def __init__(self, eng, state):
        self._e = eng
        self._state = state

    def wait_ge(self, sem, val):
        key = sem.num
        if self._state.get(key, -1) >= val:
            return None
        self._state[key] = val
        return self._e.wait_ge(sem, val)

    def __getattr__(self, k):
        return getattr(self._e, k)


class Phase:

    WSTATE = {}

    def __init__(self, nc):
        self.nc = nc
        self.q = {k: [] for k in ("tensor", "vector", "scalar", "sync", "gpsimd")}

    def add(self, eng, fn):
        self.q[eng].append(fn)

    def commit(self):
        with self.nc.Block() as blk:
            for name, items in self.q.items():
                if not items:
                    continue

                def runner(e, items=items, name=name):
                    wf = WaitFilter(e, Phase.WSTATE.setdefault((id(self.nc), name), {}))
                    for fn in items:
                        fn(wf)
                getattr(blk, name)(runner)


def chain(ph, eng, ctr, builders, waits=None):
    vals = [ctr.inc() for _ in builders]

    def run(e, builders=builders, vals=vals, waits=waits):
        if waits:
            for sm, v in waits:
                e.wait_ge(sm, v)
        last = len(builders) - 1
        for i, (f, val) in enumerate(zip(builders, vals)):
            f(e).then_inc(ctr.sem, 1)
            if i < last:
                e.wait_ge(ctr.sem, val)
    ph.add(eng, run)
    return vals[-1]


def build_nc(S, dbg=False):
    NT = S // 128
    QS = S // 4
    NW = min(512, QS)
    NB = QS // NW
    ST = min(512, S)
    NST = S // ST
    QT = ST // 128
    QU = S // 4
    TQ = QU // 128
    NU = S // QU
    NQ512 = max(1, QU // 512)
    QW = min(512, QU)

    nc = bass.Bass("TRN2", target_bir_lowering=False, detect_race_conditions=False)

    x_h = nc.dram_tensor("x", [S, D], F32, kind="ExternalInput")
    wq_h = nc.dram_tensor("wq", [128, KD, MD, 128], F32, kind="ExternalInput")
    wk_h = nc.dram_tensor("wk", [128, KD, DK], F32, kind="ExternalInput")
    wv_h = nc.dram_tensor("wv", [128, KD, DK], F32, kind="ExternalInput")
    wo_h = nc.dram_tensor("wo", [128, KD, MD, 128], F32, kind="ExternalInput")
    bq_h = nc.dram_tensor("bq", [128, MD], F32, kind="ExternalInput")
    bkv_h = nc.dram_tensor("bkv", [128, 2], F32, kind="ExternalInput")
    bo_h = nc.dram_tensor("bo", [1, D], F32, kind="ExternalInput")
    gw_h = nc.dram_tensor("gw", [128, KD, E], F32, kind="ExternalInput")
    gu_h = nc.dram_tensor("gu", [NEXP, 2 * MF, 128, KD, 128], F32,
                          kind="ExternalInput")
    wd_h = nc.dram_tensor("wd", [NEXP, 128, MF, D], F32, kind="ExternalInput")
    id_h = nc.dram_tensor("ident", [128, 128], F32, kind="ExternalInput")
    ones_h = nc.dram_tensor("ones", [1, 128], F32, kind="ExternalInput")
    r1n_h = nc.dram_tensor("r1n", [S, D], F32,
                           kind="ExternalOutput" if dbg else "Internal")
    y_h = nc.dram_tensor("y", [S, D], F32, kind="ExternalOutput")

    ctx = []

    def sem(name):
        cm = nc.semaphore(name)
        ctx.append(cm)
        return Ctr(cm.__enter__())

    def alloc(shape, dtype, name, ctx=ctx):
        cm = nc.sbuf_tensor(name, shape, dtype)
        ctx.append(cm)
        return cm.__enter__()

    s_in = sem("s_in")
    s_w = sem("s_w")
    s_out = sem("s_out")
    s_pe = sem("s_pe")
    s_vec = sem("s_vec")
    s_act = sem("s_act")

    ps_cm = nc.psum_tensor("ps", [128, 4096], F32)
    ctx.append(ps_cm)
    ps = ps_cm.__enter__()
    A = ps[:, 0:1024]
    B = ps[:, 1024:2048]
    C = ps[:, 2048:3072]
    Dr = ps[:, 3072:4096]

    ident_bf = alloc([128, 128], BF16, "ident_bf")
    ident_f = alloc([128, 128], F32, "ident_f")
    ones_bf = alloc([1, 128], BF16, "ones_bf")
    junk_bf = alloc([128, D], BF16, "junk_bf")
    lnst = alloc([128, 16], F32, "lnst")

    def lnv(p, a, b):
        return lnst[:, p * 8 + a:p * 8 + b]
    eps_sb = alloc([128, 1], F32, "eps_sb")
    x_f32 = alloc([128, 2, D], F32, "x_f32")
    x_bf = alloc([128, 1, D], BF16, "x_bf")

    actx = []
    xT = alloc([128, KD, S], BF16, "xT", actx)
    kT = alloc([128, S], BF16, "kT", actx)
    vT_bf = alloc([DK, S], BF16, "vT_bf", actx)
    v_sb = alloc([128, NT, DK], BF16, "v_sb", actx)
    wq_sb = alloc([128, KD, MD, 128], BF16, "wq_sb", actx)
    wo_sb = alloc([128, KD, MD, 128], BF16, "wo_sb", actx)
    wk_sb = alloc([128, KD, DK], BF16, "wk_sb", actx)
    wv_sb = alloc([128, KD, DK], BF16, "wv_sb", actx)
    bq_sb = alloc([128, MD], F32, "bq_sb", actx)
    bkv_sb = alloc([128, 2], F32, "bkv_sb", actx)
    bkv_sc = alloc([128, 2], F32, "bkv_sc", actx)
    bo_bf = alloc([1, D], BF16, "bo_bf", actx)
    qT = alloc([128, MD, ST], BF16, "qT", actx)
    pv_sb = alloc([128, KD, ST], BF16, "pv_sb", actx)
    P_sb = alloc([128, 2, S], BF16, "P_sb", actx)
    wT_sb = alloc([128, NT, ST], BF16, "wT_sb", actx)
    D_sb = alloc([128, 2, 4, 128], BF16, "D_sb", actx)
    nm4 = alloc([128, 2, 4], F32, "nm4", actx)
    sums4 = alloc([128, 2, 4], F32, "sums4", actx)
    st4 = alloc([128, 2, 4], F32, "st4", actx)
    al4 = alloc([128, 2, 4], F32, "al4", actx)
    nm1 = alloc([128, 2, 1], F32, "nm1", actx)
    sto = alloc([128, 2, 2], F32, "sto", actx)
    r1_f32 = alloc([128, 1, D], F32, "r1_f32", actx)
    r1n_f32 = alloc([128, 1, D], F32, "r1n_f32", actx)

    ph = Phase(nc)

    def gload(dst, src):
        v = s_w.inc(16)
        ph.add("gpsimd", lambda g, dst=dst, src=src:
               g.dma_start(dst, src).then_inc(s_w.sem, 16))
        return v

    def sload(dst, src):
        v = s_in.inc(16)
        ph.add("sync", lambda g, dst=dst, src=src:
               g.dma_start(dst, src).then_inc(s_in.sem, 16))
        return v

    gload(ident_bf[:], id_h[:])
    gload(ones_bf[:], ones_h[:])
    gload(bo_bf[:], bo_h[:])
    gload(wq_sb[:], wq_h[:])
    gload(wo_sb[:], wo_h[:])
    gload(wk_sb[:], wk_h[:])
    gload(wv_sb[:], wv_h[:])
    sload(ident_f[:], id_h[:])
    sload(bq_sb[:], bq_h[:])
    v_bkv = sload(bkv_sb[:], bkv_h[:])

    def _bkv(v, w=v_bkv):
        v.wait_ge(s_in.sem, w)
        v.memset(eps_sb[:], 1e-5)
        v.tensor_scalar(bkv_sc[:, 0:1], bkv_sb[:, 0:1], 0.125, None, ALU.mult)
        v.tensor_copy(bkv_sc[:, 1:2], bkv_sb[:, 1:2]).then_inc(s_vec.sem, 1)
    ph.add("vector", _bkv)
    s_vec.inc()
    w_const = s_w.n

    for t in range(NT):
        p = t % 2
        v_in = s_in.inc(16)
        ph.add("sync", lambda g, t=t, p=p, w=s_act.n:
               (g.wait_ge(s_act.sem, w),
                g.dma_start(x_f32[:, p, :], x_h[t * 128:(t + 1) * 128, :])
                .then_inc(s_in.sem, 16)))
        v_act = s_act.inc()
        ph.add("scalar", lambda sc, p=p, w=v_in, wpe=s_pe.n:
               (sc.wait_ge(s_in.sem, w),
                sc.wait_ge(s_pe.sem, wpe),
                sc.copy(x_bf[:, 0, :], x_f32[:, p, :]).then_inc(s_act.sem, 1)))
        for half in range(2):
            reg = A[:, half * 512:(half + 1) * 512]
            v_pe = None
            for f4 in range(4):
                f = half * 4 + f4
                last = (f4 == 3)
                if last:
                    v_pe = s_pe.inc()

                def mk(te, p=p, f=f, f4=f4, w=v_act, wc=w_const,
                       wv=s_vec.n, last=last, reg=reg):
                    te.wait_ge(s_act.sem, w)
                    te.wait_ge(s_w.sem, wc)
                    te.wait_ge(s_vec.sem, wv)
                    ins = te.matmul(reg[:, f4 * 128:(f4 + 1) * 128],
                                    x_bf[:, 0, f * 128:(f + 1) * 128],
                                    ident_bf[:], start=True, stop=True)
                    if last:
                        ins.then_inc(s_pe.sem, 1)
                ph.add("tensor", mk)
            s_vec.inc()
            ph.add("vector", lambda v, t=t, half=half, w=v_pe, reg=reg:
                   (v.wait_ge(s_pe.sem, w),
                    v.tensor_copy(
                        xT[:, half * 4:half * 4 + 4, t * 128:(t + 1) * 128],
                        reg.rearrange("p (a b) -> p a b", a=4))
                    .then_inc(s_vec.sem, 1)))
    ph.commit()

    ph = Phase(nc)
    NS512 = S // 512
    for which, wsb, dst in ((0, wk_sb, kT), (1, wv_sb, vT_bf)):
        for n in range(NS512):
            reg = A[:, (n % 2) * 512:((n % 2) + 1) * 512]
            v_pe = None
            for tp in ((0, (0, 0)), (1, (0, 64))) if which == 0 else ((0, (0, 0)),):
                ti, tpos = tp
                for k in range(KD):
                    last = (k == KD - 1 and ti == (1 if which == 0 else 0))
                    if last:
                        v_pe = s_pe.inc()

                    def mk(te, wsb=wsb, k=k, n=n, reg=reg, ti=ti, tpos=tpos,
                           wact=s_act.n, last=last):
                        te.wait_ge(s_act.sem, wact)
                        out = reg[ti * 64:(ti + 1) * 64, :] if wsb is wk_sb \
                            else reg[0:64, :]
                        ins = te.matmul(out, wsb[:, k, :],
                                        xT[:, k, n * 512:(n + 1) * 512],
                                        start=(k == 0), stop=(k == KD - 1),
                                        tile_position=tpos)
                        if last:
                            ins.then_inc(s_pe.sem, 1)
                    ph.add("tensor", mk)
            s_act.inc()
            if which == 0:
                ph.add("scalar", lambda sc, n=n, reg=reg, w=v_pe:
                       (sc.wait_ge(s_pe.sem, w),
                        sc.activation(kT[:, n * 512:(n + 1) * 512], reg,
                                      AF.Identity, bias=bkv_sc[:, 0:1], scale=0.125)
                        .then_inc(s_act.sem, 1)))
            else:
                ph.add("scalar", lambda sc, n=n, reg=reg, w=v_pe:
                       (sc.wait_ge(s_pe.sem, w),
                        sc.activation(vT_bf[:, n * 512:(n + 1) * 512],
                                      reg[0:64, :], AF.Identity,
                                      bias=bkv_sc[0:64, 1:2])
                        .then_inc(s_act.sem, 1)))
    for t2 in range((NT + 1) // 2):
        p = t2 % 2
        reg = B[:, p * 512:(p + 1) * 512]
        v_pe = None
        njt = min(2, NT - t2 * 2)
        for j in range(njt):
            t = t2 * 2 + j
            last = (j == njt - 1)
            if last:
                v_pe = s_pe.inc()

            def mk(te, t=t, j=j, reg=reg, w=s_act.n, wv=s_vec.n, last=last):
                te.wait_ge(s_act.sem, w)
                te.wait_ge(s_vec.sem, wv)
                ins = te.matmul(reg[:, j * 64:(j + 1) * 64],
                                vT_bf[:, t * 128:(t + 1) * 128],
                                ident_bf[0:64, 0:64], start=True, stop=True)
                if last:
                    ins.then_inc(s_pe.sem, 1)
            ph.add("tensor", mk)
        s_vec.inc()
        ph.add("vector", lambda v, t2=t2, reg=reg, w=v_pe, njt=njt:
               (v.wait_ge(s_pe.sem, w),
                v.tensor_copy(v_sb[:, t2 * 2:t2 * 2 + njt, :],
                              reg[:, 0:njt * 64]
                              .rearrange("p (a b) -> p a b", a=njt))
                .then_inc(s_vec.sem, 1)))
    ph.commit()

    if dbg:
        kT_d = nc.dram_tensor("kT_d", [128, S], BF16, kind="ExternalOutput")
        v_d = nc.dram_tensor("v_d", [128, NT * DK], BF16, kind="ExternalOutput")
        xT_d = nc.dram_tensor("xT_d", [128, KD * S], BF16, kind="ExternalOutput")
        ph = Phase(nc)
        s_out.inc(48)
        ph.add("sync", lambda g:
               (g.dma_start(kT_d[:], kT[:]).then_inc(s_out.sem, 16),
                g.dma_start(v_d[:], v_sb[:].rearrange("p a b -> p (a b)"))
                .then_inc(s_out.sem, 16),
                g.dma_start(xT_d[:], xT[:].rearrange("p a b -> p (a b)"))
                .then_inc(s_out.sem, 16),
                g.wait_ge(s_out.sem, s_out.n)))
        ph.commit()

    NC128 = S // 128
    CPQ = QS // 128
    r1n_dma_hist = {0: 0, 1: 0}
    out_dma_hist = {0: 0, 1: 0}
    tr_hist = {0: 0, 1: 0}
    ph = Phase(nc)
    for st in range(NST):
        exp_acts_prev = None
        if dbg and st > 0:
            ph = Phase(nc)
        for m in range(MD):
            reg = (A, B)[m % 2][:, 0:ST]
            v_pe = None
            for k in range(KD):
                last = (k == KD - 1)
                if last:
                    v_pe = s_pe.inc()

                def mk(te, m=m, k=k, reg=reg, wact=s_act.n, wv=s_vec.n,
                       last=last, st=st):
                    te.wait_ge(s_act.sem, wact)
                    te.wait_ge(s_vec.sem, wv)
                    ins = te.matmul(reg, wq_sb[:, k, m, :],
                                    xT[:, k, st * ST:(st + 1) * ST],
                                    start=(k == 0), stop=last)
                    if last:
                        ins.then_inc(s_pe.sem, 1)
                ph.add("tensor", mk)
            s_act.inc()
            ph.add("scalar", lambda sc, m=m, reg=reg, w=v_pe:
                   (sc.wait_ge(s_pe.sem, w),
                    sc.activation(qT[:, m, :], reg, AF.Identity,
                                  bias=bq_sb[:, m:m + 1]).then_inc(s_act.sem, 1)))

        def emit_tr_group(info, g4):
            pb_, vD_, qt_ = info["pb"], info["vD"], info["qt"]
            reg = C[:, (g4 % 2) * 512:((g4 % 2) + 1) * 512]
            v_pe = None
            for j in range(4):
                ci = g4 * 4 + j
                cq = ci // CPQ
                last = (j == 3)
                if last:
                    v_pe = s_pe.inc()

                def mk(te, pb=pb_, ci=ci, cq=cq, j=j, reg=reg,
                       w=vD_, wact=ev_hist["act"][g4 % 2],
                       wv=ev_hist["vec"][g4 % 2], last=last):
                    te.wait_ge(s_vec.sem, w)
                    te.wait_ge(s_act.sem, wact)
                    te.wait_ge(s_vec.sem, wv)
                    ins = te.matmul(reg[:, j * 128:(j + 1) * 128],
                                    P_sb[:, pb, ci * 128:(ci + 1) * 128],
                                    D_sb[:, pb, cq, :],
                                    start=True, stop=True)
                    if last:
                        ins.then_inc(s_pe.sem, 1)
                ph.add("tensor", mk)
            if g4 % 2 == 0:
                v_ev = s_act.inc()
                ph.add("scalar", lambda sc, g4=g4, qt=qt_, reg=reg, w=v_pe:
                       (sc.wait_ge(s_pe.sem, w),
                        sc.copy(wT_sb[:, g4 * 4:g4 * 4 + 4,
                                      qt * 128:(qt + 1) * 128],
                                reg.rearrange("p (a b) -> p a b", a=4))
                        .then_inc(s_act.sem, 1)))
                ev_hist["act"][0] = v_ev
            else:
                v_ev = s_vec.inc()
                ph.add("vector", lambda v, g4=g4, qt=qt_, reg=reg, w=v_pe:
                       (v.wait_ge(s_pe.sem, w),
                        v.tensor_copy(wT_sb[:, g4 * 4:g4 * 4 + 4,
                                            qt * 128:(qt + 1) * 128],
                                      reg.rearrange("p (a b) -> p a b", a=4))
                        .then_inc(s_vec.sem, 1)))
                ev_hist["vec"][1] = v_ev
            if g4 == NC128 // 4 - 1:
                tr_hist[pb_] = v_pe

        def emit_pv(hprev):
            hbp = (hprev % 2) * 64
            pvreg = Dr[hbp:hbp + DK, 0:ST]
            v_pe = None
            for ci in range(NT):
                last = (ci == NT - 1)
                if last:
                    v_pe = s_pe.inc()

                def mk(te, ci=ci, hbp=hbp, w=s_act.n, wv=s_vec.n, last=last,
                       pvreg=pvreg):
                    te.wait_ge(s_act.sem, w)
                    te.wait_ge(s_vec.sem, wv)
                    ins = te.matmul(pvreg, v_sb[:, ci, :], wT_sb[:, ci, :],
                                    start=(ci == 0), stop=last,
                                    tile_position=(0, hbp))
                    if last:
                        ins.then_inc(s_pe.sem, 1)
                ph.add("tensor", mk)
            s_vec.inc()
            ph.add("vector", lambda v, hprev=hprev, hbp=hbp, w=v_pe,
                   pvreg=pvreg:
                   (v.wait_ge(s_pe.sem, w),
                    v.tensor_copy(pv_sb[hbp:hbp + 64, hprev // 2, :], pvreg)
                    .then_inc(s_vec.sem, 1)))

        ev_hist = {"act": [s_act.n, s_act.n], "vec": [s_vec.n, s_vec.n]}
        pend = None
        for h in range(H):
            hb = (h % 2) * 64
            for qt in range(QT):
                pb = (h * QT + qt) % 2
                exp_acts = []
                for c in range(4):
                    sreg = (A, B)[c % 2]
                    v_pe = None
                    for nb in range(NB):
                        last = (nb == NB - 1)
                        if last:
                            v_pe = s_pe.inc()
                        if c >= 2:
                            wait_act = exp_acts[c - 2]
                        elif exp_acts_prev is None:
                            wait_act = s_act.n
                        else:
                            wait_act = exp_acts_prev[c + 2]

                        def mk(te, h=h, hb=hb, qt=qt, c=c, nb=nb, sreg=sreg,
                               wact=wait_act, last=last):
                            te.wait_ge(s_act.sem, wact)
                            ins = te.matmul(
                                sreg[:, nb * NW:(nb + 1) * NW],
                                qT[hb:hb + 64, h // 2, qt * 128:(qt + 1) * 128],
                                kT[hb:hb + 64, c * QS + nb * NW:
                                   c * QS + (nb + 1) * NW],
                                start=True, stop=True,
                                tile_position=(hb, 0))
                            if last:
                                ins.then_inc(s_pe.sem, 1)
                        ph.add("tensor", mk)
                    v_vec = s_vec.inc()
                    ph.add("vector", lambda v, c=c, pb=pb, sreg=sreg, w=v_pe:
                           (v.wait_ge(s_pe.sem, w),
                            v.tensor_reduce(nm4[:, pb, c:c + 1], sreg[:, 0:QS],
                                            AX.X, ALU.max, negate=True)
                            .then_inc(s_vec.sem, 1)))
                    v_act = s_act.inc()
                    exp_acts.append(v_act)
                    ph.add("scalar", lambda sc, c=c, pb=pb, sreg=sreg, w=v_vec,
                           wtr=tr_hist[pb]:
                           (sc.wait_ge(s_vec.sem, w),
                            sc.wait_ge(s_pe.sem, wtr),
                            sc.activation(P_sb[:, pb, c * QS:(c + 1) * QS],
                                          sreg[:, 0:QS], AF.Exp,
                                          bias=nm4[:, pb, c:c + 1],
                                          accum_out=sums4[:, pb, c:c + 1])
                            .then_inc(s_act.sem, 1)))
                    if pend is not None:
                        G = NC128 // 4
                        for g4 in range(G * c // 4, G * (c + 1) // 4):
                            emit_tr_group(pend, g4)
                exp_acts_prev = exp_acts

                v_vec = chain(ph, "vector", s_vec, [
                    lambda v, pb=pb: v.tensor_reduce(
                        nm1[:, pb, :], nm4[:, pb, 0:4], AX.X, ALU.min),
                    lambda v, pb=pb: v.tensor_scalar(
                        st4[:, pb, :], nm4[:, pb, :], -1.0, nm1[:, pb, :],
                        ALU.mult, ALU.add),
                ], waits=[(s_act.sem, exp_acts[3])])
                v_act = s_act.inc()
                ph.add("scalar", lambda sc, pb=pb, w=v_vec:
                       (sc.wait_ge(s_vec.sem, w),
                        sc.activation(al4[:, pb, :], st4[:, pb, :], AF.Exp)
                        .then_inc(s_act.sem, 1)))
                ch = chain(ph, "vector", s_vec, [
                    lambda v, pb=pb: v.scalar_tensor_tensor(
                        st4[:, pb, :], al4[:, pb, :], 1.0, sums4[:, pb, :],
                        ALU.mult, ALU.mult, accum_out=sto[:, pb, 0:1]),
                    lambda v, pb=pb: v.reciprocal(
                        sto[:, pb, 1:2], sto[:, pb, 0:1]),
                ], waits=[(s_act.sem, v_act)])
                v_vecD = s_vec.inc()
                ph.add("vector", lambda v, pb=pb, w=ch, wtr=tr_hist[pb]:
                       (v.wait_ge(s_vec.sem, w),
                        v.wait_ge(s_pe.sem, wtr),
                        v.tensor_scalar(D_sb[:, pb, 0, :], ident_bf[:],
                                        al4[:, pb, 0:1], sto[:, pb, 1:2],
                                        ALU.mult, ALU.mult),
                        v.tensor_scalar(D_sb[:, pb, 1, :], ident_bf[:],
                                        al4[:, pb, 1:2], sto[:, pb, 1:2],
                                        ALU.mult, ALU.mult),
                        v.tensor_scalar(D_sb[:, pb, 2, :], ident_bf[:],
                                        al4[:, pb, 2:3], sto[:, pb, 1:2],
                                        ALU.mult, ALU.mult),
                        v.tensor_scalar(D_sb[:, pb, 3, :], ident_bf[:],
                                        al4[:, pb, 3:4], sto[:, pb, 1:2],
                                        ALU.mult, ALU.mult)
                        .then_inc(s_vec.sem, 1)))

                if pend is not None and pend["head_end"]:
                    emit_pv(pend["h"])
                pend = {"pb": pb, "vD": v_vecD, "qt": qt,
                        "head_end": qt == QT - 1, "h": h}

        for g4 in range(NC128 // 4):
            emit_tr_group(pend, g4)
        emit_pv(pend["h"])
        pend = None

        for t in range(QT):
            tok = st * QT + t
            p = t % 2
            oreg = (A, B)[p]
            v_in = s_in.inc(16)
            ph.add("sync", lambda g, tok=tok, p=p, w=s_vec.n:
                   (g.wait_ge(s_vec.sem, w),
                    g.dma_start(x_f32[:, p, :], x_h[tok * 128:(tok + 1) * 128, :])
                    .then_inc(s_in.sem, 16)))
            v_pe = None
            for n in range(2):
                for k in range(KD):
                    ph.add("tensor", lambda te, t=t, n=n, k=k, oreg=oreg,
                           wv=s_vec.n, wact=s_act.n:
                           (te.wait_ge(s_vec.sem, wv),
                            te.wait_ge(s_act.sem, wact),
                            te.matmul(oreg[:, n * 512:(n + 1) * 512],
                                      pv_sb[:, k, t * 128:(t + 1) * 128],
                                      wo_sb[:, k, n * 4:(n + 1) * 4, :],
                                      start=(k == 0), stop=False)))
                last = (n == 1)
                if last:
                    v_pe = s_pe.inc()

                def mk(te, n=n, oreg=oreg, last=last):
                    ins = te.matmul(oreg[:, n * 512:(n + 1) * 512], ones_bf[:],
                                    bo_bf[:, n * 512:(n + 1) * 512],
                                    start=False, stop=True)
                    if last:
                        ins.then_inc(s_pe.sem, 1)
                ph.add("tensor", mk)
            v_vec = s_vec.inc()
            ph.add("vector", lambda v, p=p, oreg=oreg, w=v_pe, win=v_in:
                   (v.wait_ge(s_pe.sem, w),
                    v.wait_ge(s_in.sem, win),
                    v.scalar_tensor_tensor(r1_f32[:, 0, :], oreg, 1.0,
                                           x_f32[:, p, :], ALU.mult, ALU.add,
                                           accum_out=lnv(p, 0, 1))
                    .then_inc(s_vec.sem, 1)))
            v_act = s_act.inc()
            ph.add("scalar", lambda sc, p=p, w=v_vec:
                   (sc.wait_ge(s_vec.sem, w),
                    sc.activation(junk_bf[:], r1_f32[:, 0, :], AF.Square,
                                  accum_out=lnv(p, 1, 2)).then_inc(s_act.sem, 1)))
            v_vec2 = chain(ph, "vector", s_vec, [
                lambda v, p=p: v.tensor_scalar(lnv(p, 2, 3), lnv(p, 0, 1),
                                               1.0 / D, None, ALU.mult),
                lambda v, p=p: v.tensor_scalar(lnv(p, 3, 4), lnv(p, 1, 2),
                                               1.0 / D, None, ALU.mult),
                lambda v, p=p: v.tensor_mul(lnv(p, 4, 5), lnv(p, 2, 3),
                                            lnv(p, 2, 3)),
                lambda v, p=p: v.tensor_sub(lnv(p, 3, 4), lnv(p, 3, 4),
                                            lnv(p, 4, 5)),
            ], waits=[(s_act.sem, v_act)])
            v_act2 = chain(ph, "scalar", s_act, [
                lambda sc, p=p: sc.activation(lnv(p, 5, 6), lnv(p, 3, 4),
                                              AF.Ln, bias=eps_sb[:]),
                lambda sc, p=p: sc.activation(lnv(p, 6, 7), lnv(p, 5, 6),
                                              AF.Exp, scale=-0.5),
            ], waits=[(s_vec.sem, v_vec2)])
            v_vec3 = chain(ph, "vector", s_vec, [
                lambda v, p=p: v.tensor_mul(lnv(p, 7, 8), lnv(p, 2, 3),
                                            lnv(p, 6, 7)),
                lambda v, p=p: v.tensor_scalar(lnv(p, 7, 8), lnv(p, 7, 8),
                                               -1.0, None, ALU.mult),
                lambda v, p=p: v.tensor_scalar(r1n_f32[:, 0, :], r1_f32[:, 0, :],
                                               lnv(p, 6, 7), lnv(p, 7, 8),
                                               ALU.mult, ALU.add),
            ], waits=[(s_act.sem, v_act2), (s_out.sem, r1n_dma_hist[p])])
            r1n_dma_hist[p] = s_out.inc(16)
            ph.add("sync", lambda g, tok=tok, p=p, w=v_vec3:
                   (g.wait_ge(s_vec.sem, w),
                    g.dma_start(r1n_h[tok * 128:(tok + 1) * 128, :],
                                r1n_f32[:, 0, :]).then_inc(s_out.sem, 16)))
        if dbg:
            ph.commit()
        if dbg and st == 0:
            qT_d = nc.dram_tensor("qT_d", [128, MD * ST], BF16, kind="ExternalOutput")
            stt_d = nc.dram_tensor("stt_d", [128, 2 * (4 * 4 + 1 + 2)], F32, kind="ExternalOutput")
            Dd_d = nc.dram_tensor("Dd_d", [128, 2 * 4 * 128], BF16, kind="ExternalOutput")
            P_d = nc.dram_tensor("P_d", [128, S], BF16, kind="ExternalOutput")
            wT_d = nc.dram_tensor("wT_d", [128, NT * ST], BF16, kind="ExternalOutput")
            pv_d = nc.dram_tensor("pv_d", [128, KD * ST], BF16, kind="ExternalOutput")
            ph = Phase(nc)
            s_out.inc(96)
            ph.add("sync", lambda g:
                   (g.dma_start(stt_d[:, 0:8], nm4[:].rearrange("p a b -> p (a b)"))
                    .then_inc(s_out.sem, 16),
                    g.dma_start(stt_d[:, 8:16], sums4[:].rearrange("p a b -> p (a b)"))
                    .then_inc(s_out.sem, 16),
                    g.dma_start(stt_d[:, 16:24], al4[:].rearrange("p a b -> p (a b)"))
                    .then_inc(s_out.sem, 16),
                    g.dma_start(stt_d[:, 24:32], st4[:].rearrange("p a b -> p (a b)"))
                    .then_inc(s_out.sem, 16),
                    g.dma_start(stt_d[:, 32:36],
                                sto[:].rearrange("p a b -> p (a b)"))
                    .then_inc(s_out.sem, 16),
                    g.dma_start(Dd_d[:], D_sb[:].rearrange("p a b c -> p (a b c)"))
                    .then_inc(s_out.sem, 16),
                    g.dma_start(qT_d[:], qT[:].rearrange("p a b -> p (a b)"))
                    .then_inc(s_out.sem, 16),
                    g.dma_start(P_d[:], P_sb[:, 1, :]).then_inc(s_out.sem, 16),
                    g.dma_start(wT_d[:], wT_sb[:].rearrange("p a b -> p (a b)"))
                    .then_inc(s_out.sem, 16),
                    g.dma_start(pv_d[:], pv_sb[:].rearrange("p a b -> p (a b)"))
                    .then_inc(s_out.sem, 16),
                    g.wait_ge(s_out.sem, s_out.n)))
            ph.commit()

    if not dbg:
        ph.commit()
    for cm in reversed(actx):
        cm.__exit__(None, None, None)

    fctx = []
    gw_bf = alloc([128, KD, E], BF16, "gw_bf", fctx)
    rq_f32 = alloc([128, TQ, D], F32, "rq_f32", fctx)
    rqT = alloc([128, KD, QU], BF16, "rqT", fctx)
    acc = alloc([128, TQ, D], F32, "acc", fctx)
    su_sb = alloc([128, MF, QU], BF16, "su_sb", fctx)
    gu_sb = alloc([128, 6, KD, 128], BF16, "gu_sb", fctx)
    wd_sb = alloc([128, 2, MF, D], BF16, "wd_sb", fctx)
    gst = alloc([128, QU], BF16, "gst", fctx)
    lg_fm = alloc([E, QU], F32, "lg_fm", fctx)
    lg_tm = alloc([128, TQ, E], F32, "lg_tm", fctx)
    comb = alloc([128, TQ, E], F32, "comb", fctx)
    cwk = alloc([128, 4 * E], F32, "cwk", fctx)
    out_sb = alloc([128, 2, D], F32, "out_sb", fctx)
    rq_bf = alloc([128, 2, D], BF16, "rq_bf", fctx)

    r1n_done = s_out.n
    ln2_v3 = {t: 0 for t in range(TQ)}

    first_q = True
    ph = Phase(nc)
    for qu in range(NU):
        if first_q:
            gload(gw_bf[:], gw_h[:])
            first_q = False
        for t in range(TQ):
            p = t % 2
            v_in = s_in.inc(16)
            ph.add("sync", lambda g, qu=qu, t=t, w=r1n_done,
                   wv=ln2_v3[t]:
                   (g.wait_ge(s_out.sem, w),
                    g.wait_ge(s_vec.sem, wv),
                    g.dma_start(rq_f32[:, t, :],
                                r1n_h[(qu * TQ + t) * 128:
                                      (qu * TQ + t + 1) * 128, :])
                    .then_inc(s_in.sem, 16)))
            v_act = s_act.inc()
            ph.add("scalar", lambda sc, t=t, p=p, w=v_in, wpe=s_pe.n:
                   (sc.wait_ge(s_in.sem, w),
                    sc.wait_ge(s_pe.sem, wpe),
                    sc.copy(rq_bf[:, p, :], rq_f32[:, t, :])
                    .then_inc(s_act.sem, 1)))
            for half in range(2):
                reg = C[:, half * 512:(half + 1) * 512]
                v_pe = None
                for f4 in range(4):
                    f = half * 4 + f4
                    last = (f4 == 3)
                    if last:
                        v_pe = s_pe.inc()

                    def mk(te, p=p, f=f, f4=f4, reg=reg, w=v_act,
                           wv=s_vec.n, wg=s_w.n, last=last):
                        te.wait_ge(s_act.sem, w)
                        te.wait_ge(s_vec.sem, wv)
                        te.wait_ge(s_w.sem, wg)
                        ins = te.matmul(reg[:, f4 * 128:(f4 + 1) * 128],
                                        rq_bf[:, p, f * 128:(f + 1) * 128],
                                        ident_bf[:], start=True, stop=True)
                        if last:
                            ins.then_inc(s_pe.sem, 1)
                    ph.add("tensor", mk)
                s_vec.inc()
                ph.add("vector", lambda v, t=t, half=half, reg=reg, w=v_pe:
                       (v.wait_ge(s_pe.sem, w),
                        v.tensor_copy(
                            rqT[:, half * 4:half * 4 + 4, t * 128:(t + 1) * 128],
                            reg.rearrange("p (a b) -> p a b", a=4))
                        .then_inc(s_vec.sem, 1)))

        for n in range(NQ512):
            lreg = Dr[0:E, n * QW:(n + 1) * QW]
            v_pe = None
            for k in range(KD):
                last = (k == KD - 1)
                if last:
                    v_pe = s_pe.inc()

                def mk(te, n=n, k=k, lreg=lreg, wv=s_vec.n, wg=s_w.n, last=last):
                    te.wait_ge(s_vec.sem, wv)
                    te.wait_ge(s_w.sem, wg)
                    ins = te.matmul(lreg, gw_bf[:, k, :],
                                    rqT[:, k, n * QW:(n + 1) * QW],
                                    start=(k == 0), stop=last)
                    if last:
                        ins.then_inc(s_pe.sem, 1)
                ph.add("tensor", mk)
            s_vec.inc()
            ph.add("vector", lambda v, n=n, lreg=lreg, w=v_pe:
                   (v.wait_ge(s_pe.sem, w),
                    v.tensor_copy(lg_fm[:, n * QW:(n + 1) * QW], lreg)
                    .then_inc(s_vec.sem, 1)))
        for t in range(TQ):
            treg = C[:, (t % 2) * 512:(t % 2) * 512 + 128]
            v_pe = s_pe.inc()
            ph.add("tensor", lambda te, t=t, treg=treg, w=s_vec.n:
                   (te.wait_ge(s_vec.sem, w),
                    te.matmul(treg[:, 0:E], lg_fm[:, t * 128:(t + 1) * 128],
                              ident_f[0:E, 0:E], start=True, stop=True)
                    .then_inc(s_pe.sem, 1)))
            v_vec = s_vec.inc()
            ph.add("vector", lambda v, t=t, treg=treg, w=v_pe:
                   (v.wait_ge(s_pe.sem, w),
                    v.tensor_copy(lg_tm[:, t, :], treg[:, 0:E])
                    .then_inc(s_vec.sem, 1)))
            c0, c1, c2 = cwk[:, 0:E], cwk[:, E:2 * E], cwk[:, 2 * E:3 * E]
            m1, m2, dw = lnst[:, 0:1], lnst[:, 1:2], lnst[:, 2:3]
            w1, w2 = lnst[:, 3:4], lnst[:, 4:5]
            v_vec2 = chain(ph, "vector", s_vec, [
                lambda v, t=t, m1=m1: v.tensor_reduce(
                    m1, lg_tm[:, t, :], AX.X, ALU.max),
                lambda v, t=t, c0=c0, m1=m1: v.tensor_scalar(
                    c0, lg_tm[:, t, :], m1, None, ALU.is_equal),
                lambda v, t=t, c0=c0, c1=c1: v.scalar_tensor_tensor(
                    c1, c0, -1e30, lg_tm[:, t, :], ALU.mult, ALU.add),
                lambda v, c1=c1, m2=m2: v.tensor_reduce(m2, c1, AX.X, ALU.max),
                lambda v, m1=m1, m2=m2, dw=dw: v.tensor_sub(dw, m1, m2),
            ], waits=[(s_vec.sem, v_vec)])
            v_act = s_act.inc()
            ph.add("scalar", lambda sc, w=v_vec2, w1=w1, dw=dw:
                   (sc.wait_ge(s_vec.sem, w),
                    sc.activation(w1, dw, AF.Exp, scale=-1.0)
                    .then_inc(s_act.sem, 1)))
            chain(ph, "vector", s_vec, [
                lambda v, w1=w1: v.tensor_scalar(
                    w1, w1, 1.0, None, ALU.add),
                lambda v, w1=w1: v.reciprocal(w1, w1),
                lambda v, w1=w1, w2=w2: v.tensor_scalar(
                    w2, w1, -1.0, 1.0, ALU.mult, ALU.add),
                lambda v, c1=c1, c2=c2, m2=m2: v.tensor_scalar(
                    c2, c1, m2, None, ALU.is_equal),
                lambda v, c0=c0, w1=w1: v.tensor_scalar(
                    c0, c0, w1, None, ALU.mult),
                lambda v, t=t, c0=c0, c2=c2, w2=w2: v.scalar_tensor_tensor(
                    comb[:, t, :], c2, w2, c0, ALU.mult, ALU.add),
            ], waits=[(s_act.sem, v_act)])

        lg_done = s_vec.n
        gu_pe_hist = [0] * 6
        rd = {k: (s_vec.sem, lg_done) for k in "ABCD"}
        regs = {"A": A, "B": B, "C": C, "D": Dr}
        acc_hist = {0: 0, 1: 0}
        gst_hist = 0
        wd_v = {}
        for ei in range(NEXP):
            if ei == 0:
                wd_v[0] = s_w.inc(16)
                ph.add("gpsimd", lambda g, w=s_pe.n:
                       (g.wait_ge(s_pe.sem, w),
                        g.dma_start(wd_sb[:, 0, :, :], wd_h[0])
                        .then_inc(s_w.sem, 16)))
            v_wd = wd_v[ei]
            for mp in range(MF):
                v_pe = None
                v_pe_g = None
                for gi in range(2):
                    ms = mp * 2 + gi
                    bufi = ms % 6
                    v_g = s_w.inc(16)
                    ph.add("gpsimd", lambda g, ei=ei, ms=ms, bufi=bufi,
                           w=gu_pe_hist[bufi]:
                           (g.wait_ge(s_pe.sem, w),
                            g.dma_start(gu_sb[:, bufi, :, :], gu_h[ei, ms])
                            .then_inc(s_w.sem, 16)))
                    if mp == 3 and gi == 0 and ei + 1 < NEXP:
                        wd_v[ei + 1] = s_w.inc(16)
                        ph.add("gpsimd", lambda g, ei=ei, w=s_pe.n:
                               (g.wait_ge(s_pe.sem, w),
                                g.dma_start(wd_sb[:, (ei + 1) % 2, :, :],
                                            wd_h[ei + 1]).then_inc(s_w.sem, 16)))
                    rname = ("A", "C")[mp % 2] if gi == 0 else ("B", "D")[mp % 2]
                    greg = regs[rname]
                    rsem, rval = rd[rname]
                    for n in range(NQ512):
                        for k in range(KD):
                            last = (k == KD - 1 and n == NQ512 - 1)
                            if last:
                                v_pe = s_pe.inc()
                                if gi == 0:
                                    v_pe_g = v_pe

                            def mk(te, bufi=bufi, n=n, k=k, greg=greg,
                                   w=v_g, rsem=rsem, rval=rval, last=last):
                                te.wait_ge(s_w.sem, w)
                                te.wait_ge(rsem, rval)
                                ins = te.matmul(
                                    greg[:, n * QW:(n + 1) * QW],
                                    gu_sb[:, bufi, k, :],
                                    rqT[:, k, n * QW:(n + 1) * QW],
                                    start=(k == 0), stop=(k == KD - 1))
                                if last:
                                    ins.then_inc(s_pe.sem, 1)
                            ph.add("tensor", mk)
                    gu_pe_hist[bufi] = v_pe
                rname_g = ("A", "C")[mp % 2]
                rname_u = ("B", "D")[mp % 2]
                v_act = s_act.inc()
                ph.add("scalar", lambda sc, w=v_pe_g, wg=gst_hist,
                       greg_g=regs[rname_g]:
                       (sc.wait_ge(s_pe.sem, w),
                        sc.wait_ge(s_vec.sem, wg),
                        sc.activation(gst[:], greg_g[:, 0:QU], AF.Silu)
                        .then_inc(s_act.sem, 1)))
                rd[rname_g] = (s_act.sem, v_act)
                v_vec = s_vec.inc()
                ph.add("vector", lambda v, mp=mp, w=v_act, wpe=v_pe,
                       greg_u=regs[rname_u]:
                       (v.wait_ge(s_act.sem, w),
                        v.wait_ge(s_pe.sem, wpe),
                        v.tensor_mul(su_sb[:, mp, :], greg_u[:, 0:QU], gst[:])
                        .then_inc(s_vec.sem, 1)))
                rd[rname_u] = (s_vec.sem, v_vec)
                gst_hist = v_vec
            su_done = s_vec.n
            for t in range(TQ):
                rname = ("A", "B")[t % 2]
                yreg = regs[rname]
                rsem, rval = rd[rname]
                v_pe = None
                for k in range(MF):
                    for n in range(2):
                        last = (k == MF - 1 and n == 1)
                        if last:
                            v_pe = s_pe.inc()

                        def mk(te, t=t, k=k, n=n, ei=ei, yreg=yreg,
                               w=max(su_done, acc_hist[t % 2]), wwd=v_wd,
                               rsem=rsem, rval=rval, last=last):
                            te.wait_ge(s_vec.sem, w)
                            te.wait_ge(rsem, rval)
                            te.wait_ge(s_w.sem, wwd)
                            ins = te.matmul(yreg[:, n * 512:(n + 1) * 512],
                                            su_sb[:, k, t * 128:(t + 1) * 128],
                                            wd_sb[:, ei % 2, k,
                                                  n * 512:(n + 1) * 512],
                                            start=(k == 0), stop=(k == MF - 1))
                            if last:
                                ins.then_inc(s_pe.sem, 1)
                        ph.add("tensor", mk)
                v_y = s_vec.inc()
                if ei == 0:
                    ph.add("vector", lambda v, t=t, yreg=yreg, w=v_pe:
                           (v.wait_ge(s_pe.sem, w),
                            v.tensor_copy(acc[:, t, :], yreg)
                            .then_inc(s_vec.sem, 1)))
                else:
                    ph.add("vector", lambda v, t=t, yreg=yreg, ei=ei, w=v_pe:
                           (v.wait_ge(s_pe.sem, w),
                            v.scalar_tensor_tensor(acc[:, t, :], yreg,
                                                   comb[:, t, ei - 1:ei],
                                                   acc[:, t, :],
                                                   ALU.mult, ALU.add)
                            .then_inc(s_vec.sem, 1)))
                acc_hist[t % 2] = v_y
                rd[rname] = (s_vec.sem, v_y)

        acc_last = s_vec.n
        for t in range(TQ):
            p = t % 2
            v_vec = s_vec.inc()
            ph.add("vector", lambda v, t=t, wa=acc_last:
                   (v.wait_ge(s_vec.sem, wa),
                    v.scalar_tensor_tensor(rq_f32[:, t, :], acc[:, t, :], 1.0,
                                           rq_f32[:, t, :], ALU.mult, ALU.add,
                                           accum_out=lnv(p, 0, 1))
                    .then_inc(s_vec.sem, 1)))
            v_act = s_act.inc()
            ph.add("scalar", lambda sc, t=t, p=p, w=v_vec:
                   (sc.wait_ge(s_vec.sem, w),
                    sc.activation(junk_bf[:], rq_f32[:, t, :], AF.Square,
                                  accum_out=lnv(p, 1, 2)).then_inc(s_act.sem, 1)))
            v_vec2 = chain(ph, "vector", s_vec, [
                lambda v, p=p: v.tensor_scalar(lnv(p, 2, 3), lnv(p, 0, 1),
                                               1.0 / D, None, ALU.mult),
                lambda v, p=p: v.tensor_scalar(lnv(p, 3, 4), lnv(p, 1, 2),
                                               1.0 / D, None, ALU.mult),
                lambda v, p=p: v.tensor_mul(lnv(p, 4, 5), lnv(p, 2, 3),
                                            lnv(p, 2, 3)),
                lambda v, p=p: v.tensor_sub(lnv(p, 3, 4), lnv(p, 3, 4),
                                            lnv(p, 4, 5)),
            ], waits=[(s_act.sem, v_act)])
            v_act2 = chain(ph, "scalar", s_act, [
                lambda sc, p=p: sc.activation(lnv(p, 5, 6), lnv(p, 3, 4),
                                              AF.Ln, bias=eps_sb[:]),
                lambda sc, p=p: sc.activation(lnv(p, 6, 7), lnv(p, 5, 6),
                                              AF.Exp, scale=-0.5),
            ], waits=[(s_vec.sem, v_vec2)])
            v_vec3 = chain(ph, "vector", s_vec, [
                lambda v, p=p: v.tensor_mul(lnv(p, 7, 8), lnv(p, 2, 3),
                                            lnv(p, 6, 7)),
                lambda v, p=p: v.tensor_scalar(lnv(p, 7, 8), lnv(p, 7, 8),
                                               -1.0, None, ALU.mult),
                lambda v, t=t, p=p: v.tensor_scalar(
                    out_sb[:, p, :], rq_f32[:, t, :], lnv(p, 6, 7),
                    lnv(p, 7, 8), ALU.mult, ALU.add),
            ], waits=[(s_act.sem, v_act2), (s_out.sem, out_dma_hist[p])])
            ln2_v3[t] = v_vec3
            out_dma_hist[p] = s_out.inc(16)
            ph.add("sync", lambda g, qu=qu, t=t, p=p, w=v_vec3:
                   (g.wait_ge(s_vec.sem, w),
                    g.dma_start(y_h[(qu * TQ + t) * 128:
                                    (qu * TQ + t + 1) * 128, :],
                                out_sb[:, p, :]).then_inc(s_out.sem, 16)))
    ph.commit()

    ph = Phase(nc)
    ph.add("sync", lambda g, w=s_out.n: g.wait_ge(s_out.sem, w))
    ph.commit()

    for cm in reversed(fctx):
        cm.__exit__(None, None, None)
    for cm in reversed(ctx):
        cm.__exit__(None, None, None)
    return nc


def prep_core_inputs(inputs, S):
    f = np.ascontiguousarray
    Wq = np.asarray(inputs["L0_attn_Wq"], dtype=np.float32)
    Wk = np.asarray(inputs["L0_attn_Wk"], dtype=np.float32)
    Wv = np.asarray(inputs["L0_attn_Wv"], dtype=np.float32)
    Wo = np.asarray(inputs["L0_attn_Wo"], dtype=np.float32)
    bq = np.asarray(inputs["L0_attn_bq"], dtype=np.float32)
    bk = np.asarray(inputs["L0_attn_bk"], dtype=np.float32)
    bv = np.asarray(inputs["L0_attn_bv"], dtype=np.float32)
    bo = np.asarray(inputs["L0_attn_bo"], dtype=np.float32)
    gate_w = np.asarray(inputs["L0_ffn_gate_w"], dtype=np.float32)
    w_up = np.asarray(inputs["L0_ffn_w_up"], dtype=np.float32)
    w_down = np.asarray(inputs["L0_ffn_w_down"], dtype=np.float32)
    sg_gate = np.asarray(inputs["L0_ffn_sg_gate"], dtype=np.float32)
    sg_up = np.asarray(inputs["L0_ffn_sg_up"], dtype=np.float32)
    sg_down = np.asarray(inputs["L0_ffn_sg_down"], dtype=np.float32)

    def tile_lhst(W):
        i_, o_ = W.shape
        return f(W.reshape(i_ // 128, 128, o_ // 128, 128).transpose(1, 0, 2, 3))

    bkv = np.stack([np.concatenate([bk, bk]), np.concatenate([bv, bv])], axis=1)
    common = {
        "wq": tile_lhst(Wq),
        "wo": tile_lhst(Wo),
        "wk": f(Wk.reshape(KD, 128, DK).transpose(1, 0, 2)),
        "wv": f(Wv.reshape(KD, 128, DK).transpose(1, 0, 2)),
        "bq": f(bq.reshape(MD, 128).T),
        "bkv": f(bkv.astype(np.float32)),
        "bo": bo.reshape(1, D),
        "gw": f(gate_w.T.reshape(KD, 128, E).transpose(1, 0, 2)),
        "ident": np.eye(128, dtype=np.float32),
        "ones": np.ones((1, 128), dtype=np.float32),
    }

    gu_list, wd_list = [], []
    order = np.empty(2 * MF, dtype=np.int64)
    order[0::2] = np.arange(MF)
    order[1::2] = np.arange(MF) + MF
    for e in range(NEXP):
        if e == 0:
            wu = np.concatenate([sg_gate, sg_up], axis=0)
            wdn = sg_down
        else:
            wu = w_up[e - 1]
            wdn = w_down[e - 1]
        g4 = wu.reshape(2 * MF, 128, KD, 128).transpose(0, 3, 2, 1)
        gu_list.append(g4[order])
        wd_list.append(wdn.T.reshape(MF, 128, D).transpose(1, 0, 2))
    common["gu"] = f(np.stack(gu_list, axis=0))
    common["wd"] = f(np.stack(wd_list, axis=0))

    x = np.asarray(inputs["L0_attn_x"], dtype=np.float32)
    in_maps = []
    for c in range(x.shape[0]):
        m = dict(common)
        m["x"] = f(x[c, :S, :])
        in_maps.append(m)
    return in_maps


def kernel(**inputs):
    S = 4096
    nc = build_nc(S)
    in_maps = prep_core_inputs(inputs, S)
    res = run_bass_kernel_spmd(nc, in_maps, core_ids=list(range(8)))
    out = np.stack([np.asarray(res.results[c]["y"]) for c in range(8)], axis=0)
    return out


# version 24
# speedup vs baseline: 1.0364x; 1.0364x over previous
import numpy as np
import concourse.bass as bass
import concourse.mybir as mybir
from concourse.bass_utils import run_bass_kernel_spmd

F32 = mybir.dt.float32
BF16 = mybir.dt.bfloat16
AF = mybir.ActivationFunctionType
ALU = mybir.AluOpType
AX = mybir.AxisListType

D = 1024
H = 16
DK = 64
E = 8
HALF = 1536
KD = D // 128
MD = D // 128
MF = HALF // 128
NEXP = E + 1


class Ctr:
    __slots__ = ("sem", "n")

    def __init__(self, sem):
        self.sem = sem
        self.n = 0

    def inc(self, k=1):
        self.n += k
        return self.n


class WaitFilter:

    def __init__(self, eng, state):
        self._e = eng
        self._state = state

    def wait_ge(self, sem, val):
        key = sem.num
        if self._state.get(key, -1) >= val:
            return None
        self._state[key] = val
        return self._e.wait_ge(sem, val)

    def __getattr__(self, k):
        return getattr(self._e, k)


class Phase:

    WSTATE = {}

    def __init__(self, nc):
        self.nc = nc
        self.q = {k: [] for k in ("tensor", "vector", "scalar", "sync", "gpsimd")}

    def add(self, eng, fn):
        self.q[eng].append(fn)

    def commit(self):
        with self.nc.Block() as blk:
            for name, items in self.q.items():
                if not items:
                    continue

                def runner(e, items=items, name=name):
                    wf = WaitFilter(e, Phase.WSTATE.setdefault((id(self.nc), name), {}))
                    for fn in items:
                        fn(wf)
                getattr(blk, name)(runner)


def chain(ph, eng, ctr, builders, waits=None):
    vals = [ctr.inc() for _ in builders]

    def run(e, builders=builders, vals=vals, waits=waits):
        if waits:
            for sm, v in waits:
                e.wait_ge(sm, v)
        last = len(builders) - 1
        for i, (f, val) in enumerate(zip(builders, vals)):
            f(e).then_inc(ctr.sem, 1)
            if i < last:
                e.wait_ge(ctr.sem, val)
    ph.add(eng, run)
    return vals[-1]


def build_nc(S, dbg=False):
    NT = S // 128
    QS = S // 4
    NW = min(512, QS)
    NB = QS // NW
    ST = min(512, S)
    NST = S // ST
    QT = ST // 128
    QU = S // 4
    TQ = QU // 128
    NU = S // QU
    NQ512 = max(1, QU // 512)
    QW = min(512, QU)

    nc = bass.Bass("TRN2", target_bir_lowering=False, detect_race_conditions=False)

    x_h = nc.dram_tensor("x", [S, D], F32, kind="ExternalInput")
    wq_h = nc.dram_tensor("wq", [128, KD, MD, 128], F32, kind="ExternalInput")
    wk_h = nc.dram_tensor("wk", [128, KD, DK], F32, kind="ExternalInput")
    wv_h = nc.dram_tensor("wv", [128, KD, DK], F32, kind="ExternalInput")
    wo_h = nc.dram_tensor("wo", [128, KD, MD, 128], F32, kind="ExternalInput")
    bq_h = nc.dram_tensor("bq", [128, MD], F32, kind="ExternalInput")
    bkv_h = nc.dram_tensor("bkv", [128, 2], F32, kind="ExternalInput")
    bo_h = nc.dram_tensor("bo", [1, D], F32, kind="ExternalInput")
    gw_h = nc.dram_tensor("gw", [128, KD, E], F32, kind="ExternalInput")
    gu_h = nc.dram_tensor("gu", [NEXP, 2 * MF, 128, KD, 128], F32,
                          kind="ExternalInput")
    wd_h = nc.dram_tensor("wd", [NEXP, 128, MF, D], F32, kind="ExternalInput")
    id_h = nc.dram_tensor("ident", [128, 128], F32, kind="ExternalInput")
    ones_h = nc.dram_tensor("ones", [1, 128], F32, kind="ExternalInput")
    r1n_h = nc.dram_tensor("r1n", [S, D], F32,
                           kind="ExternalOutput" if dbg else "Internal")
    y_h = nc.dram_tensor("y", [S, D], F32, kind="ExternalOutput")

    ctx = []

    def sem(name):
        cm = nc.semaphore(name)
        ctx.append(cm)
        return Ctr(cm.__enter__())

    def alloc(shape, dtype, name, ctx=ctx):
        cm = nc.sbuf_tensor(name, shape, dtype)
        ctx.append(cm)
        return cm.__enter__()

    s_in = sem("s_in")
    s_w = sem("s_w")
    s_out = sem("s_out")
    s_pe = sem("s_pe")
    s_vec = sem("s_vec")
    s_act = sem("s_act")

    ps_cm = nc.psum_tensor("ps", [128, 4096], F32)
    ctx.append(ps_cm)
    ps = ps_cm.__enter__()
    A = ps[:, 0:1024]
    B = ps[:, 1024:2048]
    C = ps[:, 2048:3072]
    Dr = ps[:, 3072:4096]

    ident_bf = alloc([128, 128], BF16, "ident_bf")
    ident_f = alloc([128, 128], F32, "ident_f")
    ones_bf = alloc([1, 128], BF16, "ones_bf")
    junk_bf = alloc([128, D], BF16, "junk_bf")
    lnst = alloc([128, 16], F32, "lnst")

    def lnv(p, a, b):
        return lnst[:, p * 8 + a:p * 8 + b]
    eps_sb = alloc([128, 1], F32, "eps_sb")
    x_f32 = alloc([128, 2, D], F32, "x_f32")
    x_bf = alloc([128, 1, D], BF16, "x_bf")

    actx = []
    xT = alloc([128, KD, S], BF16, "xT", actx)
    kT = alloc([128, S], BF16, "kT", actx)
    vT_bf = alloc([DK, S], BF16, "vT_bf", actx)
    v_sb = alloc([128, NT, DK], BF16, "v_sb", actx)
    wq_sb = alloc([128, KD, MD, 128], BF16, "wq_sb", actx)
    wo_sb = alloc([128, KD, MD, 128], BF16, "wo_sb", actx)
    wk_sb = alloc([128, KD, DK], BF16, "wk_sb", actx)
    wv_sb = alloc([128, KD, DK], BF16, "wv_sb", actx)
    bq_sb = alloc([128, MD], F32, "bq_sb", actx)
    bkv_sb = alloc([128, 2], F32, "bkv_sb", actx)
    bkv_sc = alloc([128, 2], F32, "bkv_sc", actx)
    bo_bf = alloc([1, D], BF16, "bo_bf", actx)
    qT = alloc([128, MD, ST], BF16, "qT", actx)
    pv_sb = alloc([128, KD, ST], BF16, "pv_sb", actx)
    P_sb = alloc([128, 2, S], BF16, "P_sb", actx)
    wT_sb = alloc([128, NT, ST], BF16, "wT_sb", actx)
    D_sb = alloc([128, 2, 4, 128], BF16, "D_sb", actx)
    nm4 = alloc([128, 2, 4], F32, "nm4", actx)
    sums4 = alloc([128, 2, 4], F32, "sums4", actx)
    st4 = alloc([128, 2, 4], F32, "st4", actx)
    al4 = alloc([128, 2, 4], F32, "al4", actx)
    nm1 = alloc([128, 2, 1], F32, "nm1", actx)
    sto = alloc([128, 2, 2], F32, "sto", actx)
    r1_f32 = alloc([128, 1, D], F32, "r1_f32", actx)
    r1n_f32 = alloc([128, 1, D], F32, "r1n_f32", actx)

    ph = Phase(nc)

    def gload(dst, src):
        v = s_w.inc(16)
        ph.add("gpsimd", lambda g, dst=dst, src=src:
               g.dma_start(dst, src).then_inc(s_w.sem, 16))
        return v

    def sload(dst, src):
        v = s_in.inc(16)
        ph.add("sync", lambda g, dst=dst, src=src:
               g.dma_start(dst, src).then_inc(s_in.sem, 16))
        return v

    gload(ident_bf[:], id_h[:])
    gload(ones_bf[:], ones_h[:])
    gload(bo_bf[:], bo_h[:])
    gload(wq_sb[:], wq_h[:])
    gload(wo_sb[:], wo_h[:])
    gload(wk_sb[:], wk_h[:])
    gload(wv_sb[:], wv_h[:])
    sload(ident_f[:], id_h[:])
    sload(bq_sb[:], bq_h[:])
    v_bkv = sload(bkv_sb[:], bkv_h[:])

    def _bkv(v, w=v_bkv):
        v.wait_ge(s_in.sem, w)
        v.memset(eps_sb[:], 1e-5)
        v.tensor_scalar(bkv_sc[:, 0:1], bkv_sb[:, 0:1], 0.125, None, ALU.mult)
        v.tensor_copy(bkv_sc[:, 1:2], bkv_sb[:, 1:2]).then_inc(s_vec.sem, 1)
    ph.add("vector", _bkv)
    s_vec.inc()
    w_const = s_w.n

    for t in range(NT):
        p = t % 2
        v_in = s_in.inc(16)
        ph.add("sync", lambda g, t=t, p=p, w=s_act.n:
               (g.wait_ge(s_act.sem, w),
                g.dma_start(x_f32[:, p, :], x_h[t * 128:(t + 1) * 128, :])
                .then_inc(s_in.sem, 16)))
        v_act = s_act.inc()
        ph.add("scalar", lambda sc, p=p, w=v_in, wpe=s_pe.n:
               (sc.wait_ge(s_in.sem, w),
                sc.wait_ge(s_pe.sem, wpe),
                sc.copy(x_bf[:, 0, :], x_f32[:, p, :]).then_inc(s_act.sem, 1)))
        for half in range(2):
            reg = A[:, half * 512:(half + 1) * 512]
            v_pe = None
            for f4 in range(4):
                f = half * 4 + f4
                last = (f4 == 3)
                if last:
                    v_pe = s_pe.inc()

                def mk(te, p=p, f=f, f4=f4, w=v_act, wc=w_const,
                       wv=s_vec.n, last=last, reg=reg):
                    te.wait_ge(s_act.sem, w)
                    te.wait_ge(s_w.sem, wc)
                    te.wait_ge(s_vec.sem, wv)
                    ins = te.matmul(reg[:, f4 * 128:(f4 + 1) * 128],
                                    x_bf[:, 0, f * 128:(f + 1) * 128],
                                    ident_bf[:], start=True, stop=True)
                    if last:
                        ins.then_inc(s_pe.sem, 1)
                ph.add("tensor", mk)
            s_vec.inc()
            ph.add("vector", lambda v, t=t, half=half, w=v_pe, reg=reg:
                   (v.wait_ge(s_pe.sem, w),
                    v.tensor_copy(
                        xT[:, half * 4:half * 4 + 4, t * 128:(t + 1) * 128],
                        reg.rearrange("p (a b) -> p a b", a=4))
                    .then_inc(s_vec.sem, 1)))
    ph.commit()

    ph = Phase(nc)
    NS512 = S // 512
    for which, wsb, dst in ((0, wk_sb, kT), (1, wv_sb, vT_bf)):
        for n in range(NS512):
            reg = A[:, (n % 2) * 512:((n % 2) + 1) * 512]
            v_pe = None
            for tp in ((0, (0, 0)), (1, (0, 64))) if which == 0 else ((0, (0, 0)),):
                ti, tpos = tp
                for k in range(KD):
                    last = (k == KD - 1 and ti == (1 if which == 0 else 0))
                    if last:
                        v_pe = s_pe.inc()

                    def mk(te, wsb=wsb, k=k, n=n, reg=reg, ti=ti, tpos=tpos,
                           wact=s_act.n, last=last):
                        te.wait_ge(s_act.sem, wact)
                        out = reg[ti * 64:(ti + 1) * 64, :] if wsb is wk_sb \
                            else reg[0:64, :]
                        ins = te.matmul(out, wsb[:, k, :],
                                        xT[:, k, n * 512:(n + 1) * 512],
                                        start=(k == 0), stop=(k == KD - 1),
                                        tile_position=tpos)
                        if last:
                            ins.then_inc(s_pe.sem, 1)
                    ph.add("tensor", mk)
            s_act.inc()
            if which == 0:
                ph.add("scalar", lambda sc, n=n, reg=reg, w=v_pe:
                       (sc.wait_ge(s_pe.sem, w),
                        sc.activation(kT[:, n * 512:(n + 1) * 512], reg,
                                      AF.Identity, bias=bkv_sc[:, 0:1], scale=0.125)
                        .then_inc(s_act.sem, 1)))
            else:
                ph.add("scalar", lambda sc, n=n, reg=reg, w=v_pe:
                       (sc.wait_ge(s_pe.sem, w),
                        sc.activation(vT_bf[:, n * 512:(n + 1) * 512],
                                      reg[0:64, :], AF.Identity,
                                      bias=bkv_sc[0:64, 1:2])
                        .then_inc(s_act.sem, 1)))
    for t2 in range((NT + 1) // 2):
        p = t2 % 2
        reg = B[:, p * 512:(p + 1) * 512]
        v_pe = None
        njt = min(2, NT - t2 * 2)
        for j in range(njt):
            t = t2 * 2 + j
            last = (j == njt - 1)
            if last:
                v_pe = s_pe.inc()

            def mk(te, t=t, j=j, reg=reg, w=s_act.n, wv=s_vec.n, last=last):
                te.wait_ge(s_act.sem, w)
                te.wait_ge(s_vec.sem, wv)
                ins = te.matmul(reg[:, j * 64:(j + 1) * 64],
                                vT_bf[:, t * 128:(t + 1) * 128],
                                ident_bf[0:64, 0:64], start=True, stop=True)
                if last:
                    ins.then_inc(s_pe.sem, 1)
            ph.add("tensor", mk)
        s_vec.inc()
        ph.add("vector", lambda v, t2=t2, reg=reg, w=v_pe, njt=njt:
               (v.wait_ge(s_pe.sem, w),
                v.tensor_copy(v_sb[:, t2 * 2:t2 * 2 + njt, :],
                              reg[:, 0:njt * 64]
                              .rearrange("p (a b) -> p a b", a=njt))
                .then_inc(s_vec.sem, 1)))
    ph.commit()

    if dbg:
        kT_d = nc.dram_tensor("kT_d", [128, S], BF16, kind="ExternalOutput")
        v_d = nc.dram_tensor("v_d", [128, NT * DK], BF16, kind="ExternalOutput")
        xT_d = nc.dram_tensor("xT_d", [128, KD * S], BF16, kind="ExternalOutput")
        ph = Phase(nc)
        s_out.inc(48)
        ph.add("sync", lambda g:
               (g.dma_start(kT_d[:], kT[:]).then_inc(s_out.sem, 16),
                g.dma_start(v_d[:], v_sb[:].rearrange("p a b -> p (a b)"))
                .then_inc(s_out.sem, 16),
                g.dma_start(xT_d[:], xT[:].rearrange("p a b -> p (a b)"))
                .then_inc(s_out.sem, 16),
                g.wait_ge(s_out.sem, s_out.n)))
        ph.commit()

    NC128 = S // 128
    CPQ = QS // 128
    r1n_dma_hist = {0: 0, 1: 0}
    out_dma_hist = {0: 0, 1: 0}
    tr_hist = {0: 0, 1: 0}
    ph = Phase(nc)
    for st in range(NST):
        exp_acts_prev = None
        if dbg and st > 0:
            ph = Phase(nc)
        for m in range(MD):
            reg = (A, B)[m % 2][:, 0:ST]
            v_pe = None
            for k in range(KD):
                last = (k == KD - 1)
                if last:
                    v_pe = s_pe.inc()

                def mk(te, m=m, k=k, reg=reg, wact=s_act.n, wv=s_vec.n,
                       last=last, st=st):
                    te.wait_ge(s_act.sem, wact)
                    te.wait_ge(s_vec.sem, wv)
                    ins = te.matmul(reg, wq_sb[:, k, m, :],
                                    xT[:, k, st * ST:(st + 1) * ST],
                                    start=(k == 0), stop=last)
                    if last:
                        ins.then_inc(s_pe.sem, 1)
                ph.add("tensor", mk)
            s_act.inc()
            ph.add("scalar", lambda sc, m=m, reg=reg, w=v_pe:
                   (sc.wait_ge(s_pe.sem, w),
                    sc.activation(qT[:, m, :], reg, AF.Identity,
                                  bias=bq_sb[:, m:m + 1]).then_inc(s_act.sem, 1)))

        def emit_tr_group(info, g4):
            pb_, vD_, qt_ = info["pb"], info["vD"], info["qt"]
            reg = C[:, (g4 % 2) * 512:((g4 % 2) + 1) * 512]
            v_pe = None
            for j in range(4):
                ci = g4 * 4 + j
                cq = ci // CPQ
                last = (j == 3)
                if last:
                    v_pe = s_pe.inc()

                def mk(te, pb=pb_, ci=ci, cq=cq, j=j, reg=reg,
                       w=vD_, wact=ev_hist["act"][g4 % 2],
                       wv=ev_hist["vec"][g4 % 2], last=last):
                    te.wait_ge(s_vec.sem, w)
                    te.wait_ge(s_act.sem, wact)
                    te.wait_ge(s_vec.sem, wv)
                    ins = te.matmul(reg[:, j * 128:(j + 1) * 128],
                                    P_sb[:, pb, ci * 128:(ci + 1) * 128],
                                    D_sb[:, pb, cq, :],
                                    start=True, stop=True)
                    if last:
                        ins.then_inc(s_pe.sem, 1)
                ph.add("tensor", mk)
            if g4 % 2 == 0:
                v_ev = s_act.inc()
                ph.add("scalar", lambda sc, g4=g4, qt=qt_, reg=reg, w=v_pe:
                       (sc.wait_ge(s_pe.sem, w),
                        sc.copy(wT_sb[:, g4 * 4:g4 * 4 + 4,
                                      qt * 128:(qt + 1) * 128],
                                reg.rearrange("p (a b) -> p a b", a=4))
                        .then_inc(s_act.sem, 1)))
                ev_hist["act"][0] = v_ev
            else:
                v_ev = s_vec.inc()
                ph.add("vector", lambda v, g4=g4, qt=qt_, reg=reg, w=v_pe:
                       (v.wait_ge(s_pe.sem, w),
                        v.tensor_copy(wT_sb[:, g4 * 4:g4 * 4 + 4,
                                            qt * 128:(qt + 1) * 128],
                                      reg.rearrange("p (a b) -> p a b", a=4))
                        .then_inc(s_vec.sem, 1)))
                ev_hist["vec"][1] = v_ev
            if g4 == NC128 // 4 - 1:
                tr_hist[pb_] = v_pe

        def emit_pv(hprev):
            hbp = (hprev % 2) * 64
            pvreg = Dr[hbp:hbp + DK, 0:ST]
            v_pe = None
            for ci in range(NT):
                last = (ci == NT - 1)
                if last:
                    v_pe = s_pe.inc()

                def mk(te, ci=ci, hbp=hbp, w=s_act.n, wv=s_vec.n, last=last,
                       pvreg=pvreg):
                    te.wait_ge(s_act.sem, w)
                    te.wait_ge(s_vec.sem, wv)
                    ins = te.matmul(pvreg, v_sb[:, ci, :], wT_sb[:, ci, :],
                                    start=(ci == 0), stop=last,
                                    tile_position=(0, hbp))
                    if last:
                        ins.then_inc(s_pe.sem, 1)
                ph.add("tensor", mk)
            s_vec.inc()
            ph.add("vector", lambda v, hprev=hprev, hbp=hbp, w=v_pe,
                   pvreg=pvreg:
                   (v.wait_ge(s_pe.sem, w),
                    v.tensor_copy(pv_sb[hbp:hbp + 64, hprev // 2, :], pvreg)
                    .then_inc(s_vec.sem, 1)))

        ev_hist = {"act": [s_act.n, s_act.n], "vec": [s_vec.n, s_vec.n]}
        pend = None
        for h in range(H):
            hb = (h % 2) * 64
            for qt in range(QT):
                pb = (h * QT + qt) % 2
                exp_acts = []
                for c in range(4):
                    sreg = (A, B)[c % 2]
                    v_pe = None
                    for nb in range(NB):
                        last = (nb == NB - 1)
                        if last:
                            v_pe = s_pe.inc()
                        if c >= 2:
                            wait_act = exp_acts[c - 2]
                        elif exp_acts_prev is None:
                            wait_act = s_act.n
                        else:
                            wait_act = exp_acts_prev[c + 2]

                        def mk(te, h=h, hb=hb, qt=qt, c=c, nb=nb, sreg=sreg,
                               wact=wait_act, last=last):
                            te.wait_ge(s_act.sem, wact)
                            ins = te.matmul(
                                sreg[:, nb * NW:(nb + 1) * NW],
                                qT[hb:hb + 64, h // 2, qt * 128:(qt + 1) * 128],
                                kT[hb:hb + 64, c * QS + nb * NW:
                                   c * QS + (nb + 1) * NW],
                                start=True, stop=True,
                                tile_position=(hb, 0))
                            if last:
                                ins.then_inc(s_pe.sem, 1)
                        ph.add("tensor", mk)
                    v_vec = s_vec.inc()
                    ph.add("vector", lambda v, c=c, pb=pb, sreg=sreg, w=v_pe:
                           (v.wait_ge(s_pe.sem, w),
                            v.tensor_reduce(nm4[:, pb, c:c + 1], sreg[:, 0:QS],
                                            AX.X, ALU.max, negate=True)
                            .then_inc(s_vec.sem, 1)))
                    v_act = s_act.inc()
                    exp_acts.append(v_act)
                    ph.add("scalar", lambda sc, c=c, pb=pb, sreg=sreg, w=v_vec,
                           wtr=tr_hist[pb]:
                           (sc.wait_ge(s_vec.sem, w),
                            sc.wait_ge(s_pe.sem, wtr),
                            sc.activation(P_sb[:, pb, c * QS:(c + 1) * QS],
                                          sreg[:, 0:QS], AF.Exp,
                                          bias=nm4[:, pb, c:c + 1],
                                          accum_out=sums4[:, pb, c:c + 1])
                            .then_inc(s_act.sem, 1)))
                    if pend is not None:
                        G = NC128 // 4
                        for g4 in range(G * c // 4, G * (c + 1) // 4):
                            emit_tr_group(pend, g4)
                exp_acts_prev = exp_acts

                v_vec = chain(ph, "vector", s_vec, [
                    lambda v, pb=pb: v.tensor_reduce(
                        nm1[:, pb, :], nm4[:, pb, 0:4], AX.X, ALU.min),
                    lambda v, pb=pb: v.tensor_scalar(
                        st4[:, pb, :], nm4[:, pb, :], -1.0, nm1[:, pb, :],
                        ALU.mult, ALU.add),
                ], waits=[(s_act.sem, exp_acts[3])])
                v_act = s_act.inc()
                ph.add("scalar", lambda sc, pb=pb, w=v_vec:
                       (sc.wait_ge(s_vec.sem, w),
                        sc.activation(al4[:, pb, :], st4[:, pb, :], AF.Exp)
                        .then_inc(s_act.sem, 1)))
                ch = chain(ph, "vector", s_vec, [
                    lambda v, pb=pb: v.scalar_tensor_tensor(
                        st4[:, pb, :], al4[:, pb, :], 1.0, sums4[:, pb, :],
                        ALU.mult, ALU.mult, accum_out=sto[:, pb, 0:1]),
                    lambda v, pb=pb: v.reciprocal(
                        sto[:, pb, 1:2], sto[:, pb, 0:1]),
                ], waits=[(s_act.sem, v_act)])
                v_vecD = s_vec.inc()
                ph.add("vector", lambda v, pb=pb, w=ch, wtr=tr_hist[pb]:
                       (v.wait_ge(s_vec.sem, w),
                        v.wait_ge(s_pe.sem, wtr),
                        v.tensor_scalar(D_sb[:, pb, 0, :], ident_bf[:],
                                        al4[:, pb, 0:1], sto[:, pb, 1:2],
                                        ALU.mult, ALU.mult),
                        v.tensor_scalar(D_sb[:, pb, 1, :], ident_bf[:],
                                        al4[:, pb, 1:2], sto[:, pb, 1:2],
                                        ALU.mult, ALU.mult),
                        v.tensor_scalar(D_sb[:, pb, 2, :], ident_bf[:],
                                        al4[:, pb, 2:3], sto[:, pb, 1:2],
                                        ALU.mult, ALU.mult),
                        v.tensor_scalar(D_sb[:, pb, 3, :], ident_bf[:],
                                        al4[:, pb, 3:4], sto[:, pb, 1:2],
                                        ALU.mult, ALU.mult)
                        .then_inc(s_vec.sem, 1)))

                if pend is not None and pend["head_end"]:
                    emit_pv(pend["h"])
                pend = {"pb": pb, "vD": v_vecD, "qt": qt,
                        "head_end": qt == QT - 1, "h": h}

        for g4 in range(NC128 // 4):
            emit_tr_group(pend, g4)
        emit_pv(pend["h"])
        pend = None

        for t in range(QT):
            tok = st * QT + t
            p = t % 2
            oreg = (A, B)[p]
            v_in = s_in.inc(16)
            ph.add("sync", lambda g, tok=tok, p=p, w=s_vec.n:
                   (g.wait_ge(s_vec.sem, w),
                    g.dma_start(x_f32[:, p, :], x_h[tok * 128:(tok + 1) * 128, :])
                    .then_inc(s_in.sem, 16)))
            v_pe = None
            for n in range(2):
                for k in range(KD):
                    ph.add("tensor", lambda te, t=t, n=n, k=k, oreg=oreg,
                           wv=s_vec.n, wact=s_act.n:
                           (te.wait_ge(s_vec.sem, wv),
                            te.wait_ge(s_act.sem, wact),
                            te.matmul(oreg[:, n * 512:(n + 1) * 512],
                                      pv_sb[:, k, t * 128:(t + 1) * 128],
                                      wo_sb[:, k, n * 4:(n + 1) * 4, :],
                                      start=(k == 0), stop=False)))
                last = (n == 1)
                if last:
                    v_pe = s_pe.inc()

                def mk(te, n=n, oreg=oreg, last=last):
                    ins = te.matmul(oreg[:, n * 512:(n + 1) * 512], ones_bf[:],
                                    bo_bf[:, n * 512:(n + 1) * 512],
                                    start=False, stop=True)
                    if last:
                        ins.then_inc(s_pe.sem, 1)
                ph.add("tensor", mk)
            v_vec = s_vec.inc()
            ph.add("vector", lambda v, p=p, oreg=oreg, w=v_pe, win=v_in:
                   (v.wait_ge(s_pe.sem, w),
                    v.wait_ge(s_in.sem, win),
                    v.scalar_tensor_tensor(r1_f32[:, 0, :], oreg, 1.0,
                                           x_f32[:, p, :], ALU.mult, ALU.add,
                                           accum_out=lnv(p, 0, 1))
                    .then_inc(s_vec.sem, 1)))
            v_act = s_act.inc()
            ph.add("scalar", lambda sc, p=p, w=v_vec:
                   (sc.wait_ge(s_vec.sem, w),
                    sc.activation(junk_bf[:], r1_f32[:, 0, :], AF.Square,
                                  accum_out=lnv(p, 1, 2)).then_inc(s_act.sem, 1)))
            v_vec2 = chain(ph, "vector", s_vec, [
                lambda v, p=p: v.tensor_scalar(lnv(p, 2, 3), lnv(p, 0, 1),
                                               1.0 / D, None, ALU.mult),
                lambda v, p=p: v.tensor_scalar(lnv(p, 3, 4), lnv(p, 1, 2),
                                               1.0 / D, None, ALU.mult),
                lambda v, p=p: v.tensor_mul(lnv(p, 4, 5), lnv(p, 2, 3),
                                            lnv(p, 2, 3)),
                lambda v, p=p: v.tensor_sub(lnv(p, 3, 4), lnv(p, 3, 4),
                                            lnv(p, 4, 5)),
            ], waits=[(s_act.sem, v_act)])
            v_act2 = chain(ph, "scalar", s_act, [
                lambda sc, p=p: sc.activation(lnv(p, 5, 6), lnv(p, 3, 4),
                                              AF.Ln, bias=eps_sb[:]),
                lambda sc, p=p: sc.activation(lnv(p, 6, 7), lnv(p, 5, 6),
                                              AF.Exp, scale=-0.5),
            ], waits=[(s_vec.sem, v_vec2)])
            v_vec3 = chain(ph, "vector", s_vec, [
                lambda v, p=p: v.tensor_mul(lnv(p, 7, 8), lnv(p, 2, 3),
                                            lnv(p, 6, 7)),
                lambda v, p=p: v.tensor_scalar(lnv(p, 7, 8), lnv(p, 7, 8),
                                               -1.0, None, ALU.mult),
                lambda v, p=p: v.tensor_scalar(r1n_f32[:, 0, :], r1_f32[:, 0, :],
                                               lnv(p, 6, 7), lnv(p, 7, 8),
                                               ALU.mult, ALU.add),
            ], waits=[(s_act.sem, v_act2), (s_out.sem, r1n_dma_hist[p])])
            r1n_dma_hist[p] = s_out.inc(16)
            ph.add("sync", lambda g, tok=tok, p=p, w=v_vec3:
                   (g.wait_ge(s_vec.sem, w),
                    g.dma_start(r1n_h[tok * 128:(tok + 1) * 128, :],
                                r1n_f32[:, 0, :]).then_inc(s_out.sem, 16)))
        if dbg:
            ph.commit()
        if dbg and st == 0:
            qT_d = nc.dram_tensor("qT_d", [128, MD * ST], BF16, kind="ExternalOutput")
            stt_d = nc.dram_tensor("stt_d", [128, 2 * (4 * 4 + 1 + 2)], F32, kind="ExternalOutput")
            Dd_d = nc.dram_tensor("Dd_d", [128, 2 * 4 * 128], BF16, kind="ExternalOutput")
            P_d = nc.dram_tensor("P_d", [128, S], BF16, kind="ExternalOutput")
            wT_d = nc.dram_tensor("wT_d", [128, NT * ST], BF16, kind="ExternalOutput")
            pv_d = nc.dram_tensor("pv_d", [128, KD * ST], BF16, kind="ExternalOutput")
            ph = Phase(nc)
            s_out.inc(96)
            ph.add("sync", lambda g:
                   (g.dma_start(stt_d[:, 0:8], nm4[:].rearrange("p a b -> p (a b)"))
                    .then_inc(s_out.sem, 16),
                    g.dma_start(stt_d[:, 8:16], sums4[:].rearrange("p a b -> p (a b)"))
                    .then_inc(s_out.sem, 16),
                    g.dma_start(stt_d[:, 16:24], al4[:].rearrange("p a b -> p (a b)"))
                    .then_inc(s_out.sem, 16),
                    g.dma_start(stt_d[:, 24:32], st4[:].rearrange("p a b -> p (a b)"))
                    .then_inc(s_out.sem, 16),
                    g.dma_start(stt_d[:, 32:36],
                                sto[:].rearrange("p a b -> p (a b)"))
                    .then_inc(s_out.sem, 16),
                    g.dma_start(Dd_d[:], D_sb[:].rearrange("p a b c -> p (a b c)"))
                    .then_inc(s_out.sem, 16),
                    g.dma_start(qT_d[:], qT[:].rearrange("p a b -> p (a b)"))
                    .then_inc(s_out.sem, 16),
                    g.dma_start(P_d[:], P_sb[:, 1, :]).then_inc(s_out.sem, 16),
                    g.dma_start(wT_d[:], wT_sb[:].rearrange("p a b -> p (a b)"))
                    .then_inc(s_out.sem, 16),
                    g.dma_start(pv_d[:], pv_sb[:].rearrange("p a b -> p (a b)"))
                    .then_inc(s_out.sem, 16),
                    g.wait_ge(s_out.sem, s_out.n)))
            ph.commit()

    if not dbg:
        ph.commit()
    for cm in reversed(actx):
        cm.__exit__(None, None, None)

    fctx = []
    gw_bf = alloc([128, KD, E], BF16, "gw_bf", fctx)
    rq_f32 = alloc([128, TQ, D], F32, "rq_f32", fctx)
    rqT = alloc([128, KD, QU], BF16, "rqT", fctx)
    acc = alloc([128, TQ, D], F32, "acc", fctx)
    su_sb = alloc([128, MF, QU], BF16, "su_sb", fctx)
    gu_sb = alloc([128, 4, KD, 128], BF16, "gu_sb", fctx)
    wd_sb = alloc([128, 2, MF, D], BF16, "wd_sb", fctx)
    gst = alloc([128, QU], BF16, "gst", fctx)
    lg_fm = alloc([E, QU], F32, "lg_fm", fctx)
    lg_tm = alloc([128, TQ, E], F32, "lg_tm", fctx)
    comb = alloc([128, TQ, E], F32, "comb", fctx)
    cwk = alloc([128, 4 * E], F32, "cwk", fctx)
    out_sb = alloc([128, 2, D], F32, "out_sb", fctx)
    rq_bf = alloc([128, 2, D], BF16, "rq_bf", fctx)

    r1n_done = s_out.n
    ln2_v3 = {t: 0 for t in range(TQ)}

    first_q = True
    ph = Phase(nc)
    for qu in range(NU):
        if first_q:
            gload(gw_bf[:], gw_h[:])
            first_q = False
        for t in range(TQ):
            p = t % 2
            v_in = s_in.inc(16)
            ph.add("sync", lambda g, qu=qu, t=t, w=r1n_done,
                   wv=ln2_v3[t]:
                   (g.wait_ge(s_out.sem, w),
                    g.wait_ge(s_vec.sem, wv),
                    g.dma_start(rq_f32[:, t, :],
                                r1n_h[(qu * TQ + t) * 128:
                                      (qu * TQ + t + 1) * 128, :])
                    .then_inc(s_in.sem, 16)))
            v_act = s_act.inc()
            ph.add("scalar", lambda sc, t=t, p=p, w=v_in, wpe=s_pe.n:
                   (sc.wait_ge(s_in.sem, w),
                    sc.wait_ge(s_pe.sem, wpe),
                    sc.copy(rq_bf[:, p, :], rq_f32[:, t, :])
                    .then_inc(s_act.sem, 1)))
            for half in range(2):
                reg = C[:, half * 512:(half + 1) * 512]
                v_pe = None
                for f4 in range(4):
                    f = half * 4 + f4
                    last = (f4 == 3)
                    if last:
                        v_pe = s_pe.inc()

                    def mk(te, p=p, f=f, f4=f4, reg=reg, w=v_act,
                           wv=s_vec.n, wg=s_w.n, last=last):
                        te.wait_ge(s_act.sem, w)
                        te.wait_ge(s_vec.sem, wv)
                        te.wait_ge(s_w.sem, wg)
                        ins = te.matmul(reg[:, f4 * 128:(f4 + 1) * 128],
                                        rq_bf[:, p, f * 128:(f + 1) * 128],
                                        ident_bf[:], start=True, stop=True)
                        if last:
                            ins.then_inc(s_pe.sem, 1)
                    ph.add("tensor", mk)
                s_vec.inc()
                ph.add("vector", lambda v, t=t, half=half, reg=reg, w=v_pe:
                       (v.wait_ge(s_pe.sem, w),
                        v.tensor_copy(
                            rqT[:, half * 4:half * 4 + 4, t * 128:(t + 1) * 128],
                            reg.rearrange("p (a b) -> p a b", a=4))
                        .then_inc(s_vec.sem, 1)))

        for n in range(NQ512):
            lreg = Dr[0:E, n * QW:(n + 1) * QW]
            v_pe = None
            for k in range(KD):
                last = (k == KD - 1)
                if last:
                    v_pe = s_pe.inc()

                def mk(te, n=n, k=k, lreg=lreg, wv=s_vec.n, wg=s_w.n, last=last):
                    te.wait_ge(s_vec.sem, wv)
                    te.wait_ge(s_w.sem, wg)
                    ins = te.matmul(lreg, gw_bf[:, k, :],
                                    rqT[:, k, n * QW:(n + 1) * QW],
                                    start=(k == 0), stop=last)
                    if last:
                        ins.then_inc(s_pe.sem, 1)
                ph.add("tensor", mk)
            s_vec.inc()
            ph.add("vector", lambda v, n=n, lreg=lreg, w=v_pe:
                   (v.wait_ge(s_pe.sem, w),
                    v.tensor_copy(lg_fm[:, n * QW:(n + 1) * QW], lreg)
                    .then_inc(s_vec.sem, 1)))
        for t in range(TQ):
            treg = C[:, (t % 2) * 512:(t % 2) * 512 + 128]
            v_pe = s_pe.inc()
            ph.add("tensor", lambda te, t=t, treg=treg, w=s_vec.n:
                   (te.wait_ge(s_vec.sem, w),
                    te.matmul(treg[:, 0:E], lg_fm[:, t * 128:(t + 1) * 128],
                              ident_f[0:E, 0:E], start=True, stop=True)
                    .then_inc(s_pe.sem, 1)))
            v_vec = s_vec.inc()
            ph.add("vector", lambda v, t=t, treg=treg, w=v_pe:
                   (v.wait_ge(s_pe.sem, w),
                    v.tensor_copy(lg_tm[:, t, :], treg[:, 0:E])
                    .then_inc(s_vec.sem, 1)))
            c0, c1, c2 = cwk[:, 0:E], cwk[:, E:2 * E], cwk[:, 2 * E:3 * E]
            m1, m2, dw = lnst[:, 0:1], lnst[:, 1:2], lnst[:, 2:3]
            w1, w2 = lnst[:, 3:4], lnst[:, 4:5]
            v_vec2 = chain(ph, "vector", s_vec, [
                lambda v, t=t, m1=m1: v.tensor_reduce(
                    m1, lg_tm[:, t, :], AX.X, ALU.max),
                lambda v, t=t, c0=c0, m1=m1: v.tensor_scalar(
                    c0, lg_tm[:, t, :], m1, None, ALU.is_equal),
                lambda v, t=t, c0=c0, c1=c1: v.scalar_tensor_tensor(
                    c1, c0, -1e30, lg_tm[:, t, :], ALU.mult, ALU.add),
                lambda v, c1=c1, m2=m2: v.tensor_reduce(m2, c1, AX.X, ALU.max),
                lambda v, m1=m1, m2=m2, dw=dw: v.tensor_sub(dw, m1, m2),
            ], waits=[(s_vec.sem, v_vec)])
            v_act = s_act.inc()
            ph.add("scalar", lambda sc, w=v_vec2, w1=w1, dw=dw:
                   (sc.wait_ge(s_vec.sem, w),
                    sc.activation(w1, dw, AF.Exp, scale=-1.0)
                    .then_inc(s_act.sem, 1)))
            chain(ph, "vector", s_vec, [
                lambda v, w1=w1: v.tensor_scalar(
                    w1, w1, 1.0, None, ALU.add),
                lambda v, w1=w1: v.reciprocal(w1, w1),
                lambda v, w1=w1, w2=w2: v.tensor_scalar(
                    w2, w1, -1.0, 1.0, ALU.mult, ALU.add),
                lambda v, c1=c1, c2=c2, m2=m2: v.tensor_scalar(
                    c2, c1, m2, None, ALU.is_equal),
                lambda v, c0=c0, w1=w1: v.tensor_scalar(
                    c0, c0, w1, None, ALU.mult),
                lambda v, t=t, c0=c0, c2=c2, w2=w2: v.scalar_tensor_tensor(
                    comb[:, t, :], c2, w2, c0, ALU.mult, ALU.add),
            ], waits=[(s_act.sem, v_act)])

        lg_done = s_vec.n
        gu_pe_hist = [0] * 4
        rd = {k: (s_vec.sem, lg_done) for k in "ABCD"}
        regs = {"A": A, "B": B, "C": C, "D": Dr}
        acc_hist = {0: 0, 1: 0}
        gst_hist = 0
        wd_v = {}
        for ei in range(NEXP):
            if ei == 0:
                wd_v[0] = s_w.inc(16)
                ph.add("gpsimd", lambda g, w=s_pe.n:
                       (g.wait_ge(s_pe.sem, w),
                        g.dma_start(wd_sb[:, 0, :, :], wd_h[0])
                        .then_inc(s_w.sem, 16)))
            v_wd = wd_v[ei]
            su_v = {}
            for mp in range(MF):
                v_pe = None
                v_pe_g = None
                for gi in range(2):
                    ms = mp * 2 + gi
                    bufi = ms % 4
                    v_g = s_w.inc(16)
                    ph.add("gpsimd", lambda g, ei=ei, ms=ms, bufi=bufi,
                           w=gu_pe_hist[bufi]:
                           (g.wait_ge(s_pe.sem, w),
                            g.dma_start(gu_sb[:, bufi, :, :], gu_h[ei, ms])
                            .then_inc(s_w.sem, 16)))
                    if mp == 3 and gi == 0 and ei + 1 < NEXP:
                        wd_v[ei + 1] = s_w.inc(16)
                        ph.add("gpsimd", lambda g, ei=ei, w=s_pe.n:
                               (g.wait_ge(s_pe.sem, w),
                                g.dma_start(wd_sb[:, (ei + 1) % 2, :, :],
                                            wd_h[ei + 1]).then_inc(s_w.sem, 16)))
                    rname = ("A", "C")[mp % 2] if gi == 0 else ("B", "D")[mp % 2]
                    greg = regs[rname]
                    rsem, rval = rd[rname]
                    for n in range(NQ512):
                        for k in range(KD):
                            last = (k == KD - 1 and n == NQ512 - 1)
                            if last:
                                v_pe = s_pe.inc()
                                if gi == 0:
                                    v_pe_g = v_pe

                            def mk(te, bufi=bufi, n=n, k=k, greg=greg,
                                   w=v_g, rsem=rsem, rval=rval, last=last):
                                te.wait_ge(s_w.sem, w)
                                te.wait_ge(rsem, rval)
                                ins = te.matmul(
                                    greg[:, n * QW:(n + 1) * QW],
                                    gu_sb[:, bufi, k, :],
                                    rqT[:, k, n * QW:(n + 1) * QW],
                                    start=(k == 0), stop=(k == KD - 1))
                                if last:
                                    ins.then_inc(s_pe.sem, 1)
                            ph.add("tensor", mk)
                    gu_pe_hist[bufi] = v_pe
                rname_g = ("A", "C")[mp % 2]
                rname_u = ("B", "D")[mp % 2]
                v_act = s_act.inc()
                ph.add("scalar", lambda sc, w=v_pe_g, wg=gst_hist,
                       greg_g=regs[rname_g]:
                       (sc.wait_ge(s_pe.sem, w),
                        sc.wait_ge(s_vec.sem, wg),
                        sc.activation(gst[:], greg_g[:, 0:QU], AF.Silu)
                        .then_inc(s_act.sem, 1)))
                rd[rname_g] = (s_act.sem, v_act)
                v_vec = s_vec.inc()
                ph.add("vector", lambda v, mp=mp, w=v_act, wpe=v_pe,
                       greg_u=regs[rname_u]:
                       (v.wait_ge(s_act.sem, w),
                        v.wait_ge(s_pe.sem, wpe),
                        v.tensor_mul(su_sb[:, mp, :], greg_u[:, 0:QU], gst[:])
                        .then_inc(s_vec.sem, 1)))
                rd[rname_u] = (s_vec.sem, v_vec)
                gst_hist = v_vec
                su_v[mp] = v_vec
            su_done = s_vec.n
            for t in range(TQ):
                rname = ("A", "B")[t % 2]
                yreg = regs[rname]
                rsem, rval = rd[rname]
                v_pe = None
                for k in range(MF):
                    for n in range(2):
                        last = (k == MF - 1 and n == 1)
                        if last:
                            v_pe = s_pe.inc()

                        def mk(te, t=t, k=k, n=n, ei=ei, yreg=yreg,
                               w=max(su_v[k], acc_hist[t % 2]), wwd=v_wd,
                               rsem=rsem, rval=rval, last=last):
                            te.wait_ge(s_vec.sem, w)
                            te.wait_ge(rsem, rval)
                            te.wait_ge(s_w.sem, wwd)
                            ins = te.matmul(yreg[:, n * 512:(n + 1) * 512],
                                            su_sb[:, k, t * 128:(t + 1) * 128],
                                            wd_sb[:, ei % 2, k,
                                                  n * 512:(n + 1) * 512],
                                            start=(k == 0), stop=(k == MF - 1))
                            if last:
                                ins.then_inc(s_pe.sem, 1)
                        ph.add("tensor", mk)
                v_y = s_vec.inc()
                if ei == 0:
                    ph.add("vector", lambda v, t=t, yreg=yreg, w=v_pe:
                           (v.wait_ge(s_pe.sem, w),
                            v.tensor_copy(acc[:, t, :], yreg)
                            .then_inc(s_vec.sem, 1)))
                else:
                    ph.add("vector", lambda v, t=t, yreg=yreg, ei=ei, w=v_pe:
                           (v.wait_ge(s_pe.sem, w),
                            v.scalar_tensor_tensor(acc[:, t, :], yreg,
                                                   comb[:, t, ei - 1:ei],
                                                   acc[:, t, :],
                                                   ALU.mult, ALU.add)
                            .then_inc(s_vec.sem, 1)))
                acc_hist[t % 2] = v_y
                rd[rname] = (s_vec.sem, v_y)

        acc_last = s_vec.n
        for t in range(TQ):
            p = t % 2
            v_vec = s_vec.inc()
            ph.add("vector", lambda v, t=t, wa=acc_last:
                   (v.wait_ge(s_vec.sem, wa),
                    v.scalar_tensor_tensor(rq_f32[:, t, :], acc[:, t, :], 1.0,
                                           rq_f32[:, t, :], ALU.mult, ALU.add,
                                           accum_out=lnv(p, 0, 1))
                    .then_inc(s_vec.sem, 1)))
            v_act = s_act.inc()
            ph.add("scalar", lambda sc, t=t, p=p, w=v_vec:
                   (sc.wait_ge(s_vec.sem, w),
                    sc.activation(junk_bf[:], rq_f32[:, t, :], AF.Square,
                                  accum_out=lnv(p, 1, 2)).then_inc(s_act.sem, 1)))
            v_vec2 = chain(ph, "vector", s_vec, [
                lambda v, p=p: v.tensor_scalar(lnv(p, 2, 3), lnv(p, 0, 1),
                                               1.0 / D, None, ALU.mult),
                lambda v, p=p: v.tensor_scalar(lnv(p, 3, 4), lnv(p, 1, 2),
                                               1.0 / D, None, ALU.mult),
                lambda v, p=p: v.tensor_mul(lnv(p, 4, 5), lnv(p, 2, 3),
                                            lnv(p, 2, 3)),
                lambda v, p=p: v.tensor_sub(lnv(p, 3, 4), lnv(p, 3, 4),
                                            lnv(p, 4, 5)),
            ], waits=[(s_act.sem, v_act)])
            v_act2 = chain(ph, "scalar", s_act, [
                lambda sc, p=p: sc.activation(lnv(p, 5, 6), lnv(p, 3, 4),
                                              AF.Ln, bias=eps_sb[:]),
                lambda sc, p=p: sc.activation(lnv(p, 6, 7), lnv(p, 5, 6),
                                              AF.Exp, scale=-0.5),
            ], waits=[(s_vec.sem, v_vec2)])
            v_vec3 = chain(ph, "vector", s_vec, [
                lambda v, p=p: v.tensor_mul(lnv(p, 7, 8), lnv(p, 2, 3),
                                            lnv(p, 6, 7)),
                lambda v, p=p: v.tensor_scalar(lnv(p, 7, 8), lnv(p, 7, 8),
                                               -1.0, None, ALU.mult),
                lambda v, t=t, p=p: v.tensor_scalar(
                    out_sb[:, p, :], rq_f32[:, t, :], lnv(p, 6, 7),
                    lnv(p, 7, 8), ALU.mult, ALU.add),
            ], waits=[(s_act.sem, v_act2), (s_out.sem, out_dma_hist[p])])
            ln2_v3[t] = v_vec3
            out_dma_hist[p] = s_out.inc(16)
            ph.add("sync", lambda g, qu=qu, t=t, p=p, w=v_vec3:
                   (g.wait_ge(s_vec.sem, w),
                    g.dma_start(y_h[(qu * TQ + t) * 128:
                                    (qu * TQ + t + 1) * 128, :],
                                out_sb[:, p, :]).then_inc(s_out.sem, 16)))
    ph.commit()

    ph = Phase(nc)
    ph.add("sync", lambda g, w=s_out.n: g.wait_ge(s_out.sem, w))
    ph.commit()

    for cm in reversed(fctx):
        cm.__exit__(None, None, None)
    for cm in reversed(ctx):
        cm.__exit__(None, None, None)
    return nc


def prep_core_inputs(inputs, S):
    f = np.ascontiguousarray
    Wq = np.asarray(inputs["L0_attn_Wq"], dtype=np.float32)
    Wk = np.asarray(inputs["L0_attn_Wk"], dtype=np.float32)
    Wv = np.asarray(inputs["L0_attn_Wv"], dtype=np.float32)
    Wo = np.asarray(inputs["L0_attn_Wo"], dtype=np.float32)
    bq = np.asarray(inputs["L0_attn_bq"], dtype=np.float32)
    bk = np.asarray(inputs["L0_attn_bk"], dtype=np.float32)
    bv = np.asarray(inputs["L0_attn_bv"], dtype=np.float32)
    bo = np.asarray(inputs["L0_attn_bo"], dtype=np.float32)
    gate_w = np.asarray(inputs["L0_ffn_gate_w"], dtype=np.float32)
    w_up = np.asarray(inputs["L0_ffn_w_up"], dtype=np.float32)
    w_down = np.asarray(inputs["L0_ffn_w_down"], dtype=np.float32)
    sg_gate = np.asarray(inputs["L0_ffn_sg_gate"], dtype=np.float32)
    sg_up = np.asarray(inputs["L0_ffn_sg_up"], dtype=np.float32)
    sg_down = np.asarray(inputs["L0_ffn_sg_down"], dtype=np.float32)

    def tile_lhst(W):
        i_, o_ = W.shape
        return f(W.reshape(i_ // 128, 128, o_ // 128, 128).transpose(1, 0, 2, 3))

    bkv = np.stack([np.concatenate([bk, bk]), np.concatenate([bv, bv])], axis=1)
    common = {
        "wq": tile_lhst(Wq),
        "wo": tile_lhst(Wo),
        "wk": f(Wk.reshape(KD, 128, DK).transpose(1, 0, 2)),
        "wv": f(Wv.reshape(KD, 128, DK).transpose(1, 0, 2)),
        "bq": f(bq.reshape(MD, 128).T),
        "bkv": f(bkv.astype(np.float32)),
        "bo": bo.reshape(1, D),
        "gw": f(gate_w.T.reshape(KD, 128, E).transpose(1, 0, 2)),
        "ident": np.eye(128, dtype=np.float32),
        "ones": np.ones((1, 128), dtype=np.float32),
    }

    gu_list, wd_list = [], []
    order = np.empty(2 * MF, dtype=np.int64)
    order[0::2] = np.arange(MF)
    order[1::2] = np.arange(MF) + MF
    for e in range(NEXP):
        if e == 0:
            wu = np.concatenate([sg_gate, sg_up], axis=0)
            wdn = sg_down
        else:
            wu = w_up[e - 1]
            wdn = w_down[e - 1]
        g4 = wu.reshape(2 * MF, 128, KD, 128).transpose(0, 3, 2, 1)
        gu_list.append(g4[order])
        wd_list.append(wdn.T.reshape(MF, 128, D).transpose(1, 0, 2))
    common["gu"] = f(np.stack(gu_list, axis=0))
    common["wd"] = f(np.stack(wd_list, axis=0))

    x = np.asarray(inputs["L0_attn_x"], dtype=np.float32)
    in_maps = []
    for c in range(x.shape[0]):
        m = dict(common)
        m["x"] = f(x[c, :S, :])
        in_maps.append(m)
    return in_maps


def kernel(**inputs):
    S = 4096
    nc = build_nc(S)
    in_maps = prep_core_inputs(inputs, S)
    res = run_bass_kernel_spmd(nc, in_maps, core_ids=list(range(8)))
    out = np.stack([np.asarray(res.results[c]["y"]) for c in range(8)], axis=0)
    return out


# version 28
# speedup vs baseline: 1.0422x; 1.0056x over previous
import numpy as np
import concourse.bass as bass
import concourse.mybir as mybir
from concourse.bass_utils import run_bass_kernel_spmd

F32 = mybir.dt.float32
BF16 = mybir.dt.bfloat16
AF = mybir.ActivationFunctionType
ALU = mybir.AluOpType
AX = mybir.AxisListType

D = 1024
H = 16
DK = 64
E = 8
HALF = 1536
KD = D // 128
MD = D // 128
MF = HALF // 128
NEXP = E + 1


class Ctr:
    __slots__ = ("sem", "n")

    def __init__(self, sem):
        self.sem = sem
        self.n = 0

    def inc(self, k=1):
        self.n += k
        return self.n


class WaitFilter:

    def __init__(self, eng, state):
        self._e = eng
        self._state = state

    def wait_ge(self, sem, val):
        key = sem.num
        if self._state.get(key, -1) >= val:
            return None
        self._state[key] = val
        return self._e.wait_ge(sem, val)

    def __getattr__(self, k):
        return getattr(self._e, k)


class Phase:

    WSTATE = {}

    def __init__(self, nc):
        self.nc = nc
        self.q = {k: [] for k in ("tensor", "vector", "scalar", "sync", "gpsimd")}

    def add(self, eng, fn):
        self.q[eng].append(fn)

    def commit(self):
        with self.nc.Block() as blk:
            for name, items in self.q.items():
                if not items:
                    continue

                def runner(e, items=items, name=name):
                    wf = WaitFilter(e, Phase.WSTATE.setdefault((id(self.nc), name), {}))
                    for fn in items:
                        fn(wf)
                getattr(blk, name)(runner)


def chain(ph, eng, ctr, builders, waits=None):
    vals = [ctr.inc() for _ in builders]

    def run(e, builders=builders, vals=vals, waits=waits):
        if waits:
            for sm, v in waits:
                e.wait_ge(sm, v)
        last = len(builders) - 1
        for i, (f, val) in enumerate(zip(builders, vals)):
            f(e).then_inc(ctr.sem, 1)
            if i < last:
                e.wait_ge(ctr.sem, val)
    ph.add(eng, run)
    return vals[-1]


def build_nc(S, dbg=False):
    NT = S // 128
    QS = S // 4
    NW = min(512, QS)
    NB = QS // NW
    ST = min(512, S)
    NST = S // ST
    QT = ST // 128
    QU = S // 4
    TQ = QU // 128
    NU = S // QU
    NQ512 = max(1, QU // 512)
    QW = min(512, QU)

    nc = bass.Bass("TRN2", target_bir_lowering=False, detect_race_conditions=False)

    x_h = nc.dram_tensor("x", [S, D], F32, kind="ExternalInput")
    wq_h = nc.dram_tensor("wq", [128, KD, MD, 128], F32, kind="ExternalInput")
    wk_h = nc.dram_tensor("wk", [128, KD, DK], F32, kind="ExternalInput")
    wv_h = nc.dram_tensor("wv", [128, KD, DK], F32, kind="ExternalInput")
    wo_h = nc.dram_tensor("wo", [128, KD, MD, 128], F32, kind="ExternalInput")
    bq_h = nc.dram_tensor("bq", [128, MD], F32, kind="ExternalInput")
    bkv_h = nc.dram_tensor("bkv", [128, 2], F32, kind="ExternalInput")
    bo_h = nc.dram_tensor("bo", [1, D], F32, kind="ExternalInput")
    gw_h = nc.dram_tensor("gw", [128, KD, E], F32, kind="ExternalInput")
    gu_h = nc.dram_tensor("gu", [NEXP, 2 * MF, 128, KD, 128], F32,
                          kind="ExternalInput")
    wd_h = nc.dram_tensor("wd", [NEXP, 128, MF, D], F32, kind="ExternalInput")
    id_h = nc.dram_tensor("ident", [128, 128], F32, kind="ExternalInput")
    ones_h = nc.dram_tensor("ones", [1, 128], F32, kind="ExternalInput")
    r1n_h = nc.dram_tensor("r1n", [S, D], F32,
                           kind="ExternalOutput" if dbg else "Internal")
    y_h = nc.dram_tensor("y", [S, D], F32, kind="ExternalOutput")

    ctx = []

    def sem(name):
        cm = nc.semaphore(name)
        ctx.append(cm)
        return Ctr(cm.__enter__())

    def alloc(shape, dtype, name, ctx=ctx):
        cm = nc.sbuf_tensor(name, shape, dtype)
        ctx.append(cm)
        return cm.__enter__()

    s_in = sem("s_in")
    s_w = sem("s_w")
    s_out = sem("s_out")
    s_pe = sem("s_pe")
    s_vec = sem("s_vec")
    s_act = sem("s_act")

    ps_cm = nc.psum_tensor("ps", [128, 4096], F32)
    ctx.append(ps_cm)
    ps = ps_cm.__enter__()
    A = ps[:, 0:1024]
    B = ps[:, 1024:2048]
    C = ps[:, 2048:3072]
    Dr = ps[:, 3072:4096]

    ident_bf = alloc([128, 128], BF16, "ident_bf")
    ident_f = alloc([128, 128], F32, "ident_f")
    ones_bf = alloc([1, 128], BF16, "ones_bf")
    junk_bf = alloc([128, D], BF16, "junk_bf")
    lnst = alloc([128, 16], F32, "lnst")

    def lnv(p, a, b):
        return lnst[:, p * 8 + a:p * 8 + b]
    eps_sb = alloc([128, 1], F32, "eps_sb")
    x_f32 = alloc([128, 2, D], F32, "x_f32")
    x_bf = alloc([128, 1, D], BF16, "x_bf")

    actx = []
    xT = alloc([128, KD, S], BF16, "xT", actx)
    kT = alloc([128, S], BF16, "kT", actx)
    vT_bf = alloc([DK, S], BF16, "vT_bf", actx)
    v_sb = alloc([128, NT, DK], BF16, "v_sb", actx)
    wq_sb = alloc([128, KD, MD, 128], BF16, "wq_sb", actx)
    wo_sb = alloc([128, KD, MD, 128], BF16, "wo_sb", actx)
    wk_sb = alloc([128, KD, DK], BF16, "wk_sb", actx)
    wv_sb = alloc([128, KD, DK], BF16, "wv_sb", actx)
    bq_sb = alloc([128, MD], F32, "bq_sb", actx)
    bkv_sb = alloc([128, 2], F32, "bkv_sb", actx)
    bkv_sc = alloc([128, 2], F32, "bkv_sc", actx)
    bo_bf = alloc([1, D], BF16, "bo_bf", actx)
    qT = alloc([128, MD, ST], BF16, "qT", actx)
    pv_sb = alloc([128, KD, ST], BF16, "pv_sb", actx)
    P_sb = alloc([128, 2, S], BF16, "P_sb", actx)
    wT_sb = alloc([128, NT, ST], BF16, "wT_sb", actx)
    D_sb = alloc([128, 2, 4, 128], BF16, "D_sb", actx)
    nm4 = alloc([128, 2, 4], F32, "nm4", actx)
    sums4 = alloc([128, 2, 4], F32, "sums4", actx)
    st4 = alloc([128, 2, 4], F32, "st4", actx)
    al4 = alloc([128, 2, 4], F32, "al4", actx)
    nm1 = alloc([128, 2, 1], F32, "nm1", actx)
    sto = alloc([128, 2, 2], F32, "sto", actx)
    r1_f32 = alloc([128, 1, D], F32, "r1_f32", actx)
    r1n_f32 = alloc([128, 1, D], F32, "r1n_f32", actx)

    ph = Phase(nc)

    def gload(dst, src):
        v = s_w.inc(16)
        ph.add("gpsimd", lambda g, dst=dst, src=src:
               g.dma_start(dst, src).then_inc(s_w.sem, 16))
        return v

    def sload(dst, src):
        v = s_in.inc(16)
        ph.add("sync", lambda g, dst=dst, src=src:
               g.dma_start(dst, src).then_inc(s_in.sem, 16))
        return v

    gload(ident_bf[:], id_h[:])
    gload(ones_bf[:], ones_h[:])
    gload(bo_bf[:], bo_h[:])
    gload(wq_sb[:], wq_h[:])
    gload(wo_sb[:], wo_h[:])
    gload(wk_sb[:], wk_h[:])
    gload(wv_sb[:], wv_h[:])
    sload(ident_f[:], id_h[:])
    sload(bq_sb[:], bq_h[:])
    v_bkv = sload(bkv_sb[:], bkv_h[:])

    def _bkv(v, w=v_bkv):
        v.wait_ge(s_in.sem, w)
        v.memset(eps_sb[:], 1e-5)
        v.tensor_scalar(bkv_sc[:, 0:1], bkv_sb[:, 0:1], 0.125, None, ALU.mult)
        v.tensor_copy(bkv_sc[:, 1:2], bkv_sb[:, 1:2]).then_inc(s_vec.sem, 1)
    ph.add("vector", _bkv)
    s_vec.inc()
    w_const = s_w.n

    for t in range(NT):
        p = t % 2
        v_in = s_in.inc(16)
        ph.add("sync", lambda g, t=t, p=p, w=s_act.n:
               (g.wait_ge(s_act.sem, w),
                g.dma_start(x_f32[:, p, :], x_h[t * 128:(t + 1) * 128, :])
                .then_inc(s_in.sem, 16)))
        v_act = s_act.inc()
        ph.add("scalar", lambda sc, p=p, w=v_in, wpe=s_pe.n:
               (sc.wait_ge(s_in.sem, w),
                sc.wait_ge(s_pe.sem, wpe),
                sc.copy(x_bf[:, 0, :], x_f32[:, p, :]).then_inc(s_act.sem, 1)))
        for half in range(2):
            reg = A[:, half * 512:(half + 1) * 512]
            v_pe = None
            for f4 in range(4):
                f = half * 4 + f4
                last = (f4 == 3)
                if last:
                    v_pe = s_pe.inc()

                def mk(te, p=p, f=f, f4=f4, w=v_act, wc=w_const,
                       wv=s_vec.n, last=last, reg=reg):
                    te.wait_ge(s_act.sem, w)
                    te.wait_ge(s_w.sem, wc)
                    te.wait_ge(s_vec.sem, wv)
                    ins = te.matmul(reg[:, f4 * 128:(f4 + 1) * 128],
                                    x_bf[:, 0, f * 128:(f + 1) * 128],
                                    ident_bf[:], start=True, stop=True)
                    if last:
                        ins.then_inc(s_pe.sem, 1)
                ph.add("tensor", mk)
            s_vec.inc()
            ph.add("vector", lambda v, t=t, half=half, w=v_pe, reg=reg:
                   (v.wait_ge(s_pe.sem, w),
                    v.tensor_copy(
                        xT[:, half * 4:half * 4 + 4, t * 128:(t + 1) * 128],
                        reg.rearrange("p (a b) -> p a b", a=4))
                    .then_inc(s_vec.sem, 1)))
    ph.commit()

    ph = Phase(nc)
    NS512 = S // 512
    for which, wsb, dst in ((0, wk_sb, kT), (1, wv_sb, vT_bf)):
        for n in range(NS512):
            reg = A[:, (n % 2) * 512:((n % 2) + 1) * 512]
            v_pe = None
            for tp in ((0, (0, 0)), (1, (0, 64))) if which == 0 else ((0, (0, 0)),):
                ti, tpos = tp
                for k in range(KD):
                    last = (k == KD - 1 and ti == (1 if which == 0 else 0))
                    if last:
                        v_pe = s_pe.inc()

                    def mk(te, wsb=wsb, k=k, n=n, reg=reg, ti=ti, tpos=tpos,
                           wact=s_act.n, last=last):
                        te.wait_ge(s_act.sem, wact)
                        out = reg[ti * 64:(ti + 1) * 64, :] if wsb is wk_sb \
                            else reg[0:64, :]
                        ins = te.matmul(out, wsb[:, k, :],
                                        xT[:, k, n * 512:(n + 1) * 512],
                                        start=(k == 0), stop=(k == KD - 1),
                                        tile_position=tpos)
                        if last:
                            ins.then_inc(s_pe.sem, 1)
                    ph.add("tensor", mk)
            s_act.inc()
            if which == 0:
                ph.add("scalar", lambda sc, n=n, reg=reg, w=v_pe:
                       (sc.wait_ge(s_pe.sem, w),
                        sc.activation(kT[:, n * 512:(n + 1) * 512], reg,
                                      AF.Identity, bias=bkv_sc[:, 0:1], scale=0.125)
                        .then_inc(s_act.sem, 1)))
            else:
                ph.add("scalar", lambda sc, n=n, reg=reg, w=v_pe:
                       (sc.wait_ge(s_pe.sem, w),
                        sc.activation(vT_bf[:, n * 512:(n + 1) * 512],
                                      reg[0:64, :], AF.Identity,
                                      bias=bkv_sc[0:64, 1:2])
                        .then_inc(s_act.sem, 1)))
    for t2 in range((NT + 1) // 2):
        p = t2 % 2
        reg = B[:, p * 512:(p + 1) * 512]
        v_pe = None
        njt = min(2, NT - t2 * 2)
        for j in range(njt):
            t = t2 * 2 + j
            last = (j == njt - 1)
            if last:
                v_pe = s_pe.inc()

            def mk(te, t=t, j=j, reg=reg, w=s_act.n, wv=s_vec.n, last=last):
                te.wait_ge(s_act.sem, w)
                te.wait_ge(s_vec.sem, wv)
                ins = te.matmul(reg[:, j * 64:(j + 1) * 64],
                                vT_bf[:, t * 128:(t + 1) * 128],
                                ident_bf[0:64, 0:64], start=True, stop=True)
                if last:
                    ins.then_inc(s_pe.sem, 1)
            ph.add("tensor", mk)
        s_vec.inc()
        ph.add("vector", lambda v, t2=t2, reg=reg, w=v_pe, njt=njt:
               (v.wait_ge(s_pe.sem, w),
                v.tensor_copy(v_sb[:, t2 * 2:t2 * 2 + njt, :],
                              reg[:, 0:njt * 64]
                              .rearrange("p (a b) -> p a b", a=njt))
                .then_inc(s_vec.sem, 1)))
    ph.commit()

    if dbg:
        kT_d = nc.dram_tensor("kT_d", [128, S], BF16, kind="ExternalOutput")
        v_d = nc.dram_tensor("v_d", [128, NT * DK], BF16, kind="ExternalOutput")
        xT_d = nc.dram_tensor("xT_d", [128, KD * S], BF16, kind="ExternalOutput")
        ph = Phase(nc)
        s_out.inc(48)
        ph.add("sync", lambda g:
               (g.dma_start(kT_d[:], kT[:]).then_inc(s_out.sem, 16),
                g.dma_start(v_d[:], v_sb[:].rearrange("p a b -> p (a b)"))
                .then_inc(s_out.sem, 16),
                g.dma_start(xT_d[:], xT[:].rearrange("p a b -> p (a b)"))
                .then_inc(s_out.sem, 16),
                g.wait_ge(s_out.sem, s_out.n)))
        ph.commit()

    NC128 = S // 128
    CPQ = QS // 128
    r1n_dma_hist = {0: 0, 1: 0}
    out_dma_hist = {0: 0, 1: 0}
    tr_hist = {0: 0, 1: 0}
    ph = Phase(nc)
    for st in range(NST):
        exp_acts_prev = None
        if dbg and st > 0:
            ph = Phase(nc)
        for m in range(MD):
            reg = (A, B)[m % 2][:, 0:ST]
            v_pe = None
            for k in range(KD):
                last = (k == KD - 1)
                if last:
                    v_pe = s_pe.inc()

                def mk(te, m=m, k=k, reg=reg, wact=s_act.n, wv=s_vec.n,
                       last=last, st=st):
                    te.wait_ge(s_act.sem, wact)
                    te.wait_ge(s_vec.sem, wv)
                    ins = te.matmul(reg, wq_sb[:, k, m, :],
                                    xT[:, k, st * ST:(st + 1) * ST],
                                    start=(k == 0), stop=last)
                    if last:
                        ins.then_inc(s_pe.sem, 1)
                ph.add("tensor", mk)
            s_act.inc()
            ph.add("scalar", lambda sc, m=m, reg=reg, w=v_pe:
                   (sc.wait_ge(s_pe.sem, w),
                    sc.activation(qT[:, m, :], reg, AF.Identity,
                                  bias=bq_sb[:, m:m + 1]).then_inc(s_act.sem, 1)))

        def emit_tr_group(info, g4):
            pb_, vD_, qt_ = info["pb"], info["vD"], info["qt"]
            reg = C[:, (g4 % 2) * 512:((g4 % 2) + 1) * 512]
            v_pe = None
            for j in range(4):
                ci = g4 * 4 + j
                cq = ci // CPQ
                last = (j == 3)
                if last:
                    v_pe = s_pe.inc()

                def mk(te, pb=pb_, ci=ci, cq=cq, j=j, reg=reg,
                       w=vD_, wact=ev_hist["act"][g4 % 2],
                       wv=ev_hist["vec"][g4 % 2], last=last):
                    te.wait_ge(s_vec.sem, w)
                    te.wait_ge(s_act.sem, wact)
                    te.wait_ge(s_vec.sem, wv)
                    ins = te.matmul(reg[:, j * 128:(j + 1) * 128],
                                    P_sb[:, pb, ci * 128:(ci + 1) * 128],
                                    D_sb[:, pb, cq, :],
                                    start=True, stop=True)
                    if last:
                        ins.then_inc(s_pe.sem, 1)
                ph.add("tensor", mk)
            if g4 % 2 == 0:
                v_ev = s_act.inc()
                ph.add("scalar", lambda sc, g4=g4, qt=qt_, reg=reg, w=v_pe:
                       (sc.wait_ge(s_pe.sem, w),
                        sc.copy(wT_sb[:, g4 * 4:g4 * 4 + 4,
                                      qt * 128:(qt + 1) * 128],
                                reg.rearrange("p (a b) -> p a b", a=4))
                        .then_inc(s_act.sem, 1)))
                ev_hist["act"][0] = v_ev
            else:
                v_ev = s_vec.inc()
                ph.add("vector", lambda v, g4=g4, qt=qt_, reg=reg, w=v_pe:
                       (v.wait_ge(s_pe.sem, w),
                        v.tensor_copy(wT_sb[:, g4 * 4:g4 * 4 + 4,
                                            qt * 128:(qt + 1) * 128],
                                      reg.rearrange("p (a b) -> p a b", a=4))
                        .then_inc(s_vec.sem, 1)))
                ev_hist["vec"][1] = v_ev
            if g4 == NC128 // 4 - 1:
                tr_hist[pb_] = v_pe
                pair_marks[(info["h"], info["qt"])] = (s_act.n, s_vec.n)

        def emit_pv(hprev):
            hbp = (hprev % 2) * 64
            pvreg = Dr[hbp:hbp + DK, 0:ST]
            mA = pair_marks[(hprev, 0)]
            mB = pair_marks[(hprev, QT - 1)]
            wact_pv = max(mA[0], mB[0])
            wvec_pv = max(mA[1], mB[1], pv_prev_ev[0])
            v_pe = None
            for ci in range(NT):
                last = (ci == NT - 1)
                if last:
                    v_pe = s_pe.inc()

                def mk(te, ci=ci, hbp=hbp, w=wact_pv, wv=wvec_pv, last=last,
                       pvreg=pvreg):
                    te.wait_ge(s_act.sem, w)
                    te.wait_ge(s_vec.sem, wv)
                    ins = te.matmul(pvreg, v_sb[:, ci, :], wT_sb[:, ci, :],
                                    start=(ci == 0), stop=last,
                                    tile_position=(0, hbp))
                    if last:
                        ins.then_inc(s_pe.sem, 1)
                ph.add("tensor", mk)
            pv_prev_ev[0] = s_vec.inc()
            ph.add("vector", lambda v, hprev=hprev, hbp=hbp, w=v_pe,
                   pvreg=pvreg:
                   (v.wait_ge(s_pe.sem, w),
                    v.tensor_copy(pv_sb[hbp:hbp + 64, hprev // 2, :], pvreg)
                    .then_inc(s_vec.sem, 1)))

        ev_hist = {"act": [s_act.n, s_act.n], "vec": [s_vec.n, s_vec.n]}
        pair_marks = {}
        pv_prev_ev = [0]
        pend = None
        for h in range(H):
            hb = (h % 2) * 64
            for qt in range(QT):
                pb = (h * QT + qt) % 2
                exp_acts = []
                for c in range(4):
                    sreg = (A, B)[c % 2]
                    v_pe = None
                    for nb in range(NB):
                        last = (nb == NB - 1)
                        if last:
                            v_pe = s_pe.inc()
                        if c >= 2:
                            wait_act = exp_acts[c - 2]
                        elif exp_acts_prev is None:
                            wait_act = s_act.n
                        else:
                            wait_act = exp_acts_prev[c + 2]

                        def mk(te, h=h, hb=hb, qt=qt, c=c, nb=nb, sreg=sreg,
                               wact=wait_act, last=last):
                            te.wait_ge(s_act.sem, wact)
                            ins = te.matmul(
                                sreg[:, nb * NW:(nb + 1) * NW],
                                qT[hb:hb + 64, h // 2, qt * 128:(qt + 1) * 128],
                                kT[hb:hb + 64, c * QS + nb * NW:
                                   c * QS + (nb + 1) * NW],
                                start=True, stop=True,
                                tile_position=(hb, 0))
                            if last:
                                ins.then_inc(s_pe.sem, 1)
                        ph.add("tensor", mk)
                    v_vec = s_vec.inc()
                    ph.add("vector", lambda v, c=c, pb=pb, sreg=sreg, w=v_pe:
                           (v.wait_ge(s_pe.sem, w),
                            v.tensor_reduce(nm4[:, pb, c:c + 1], sreg[:, 0:QS],
                                            AX.X, ALU.max, negate=True)
                            .then_inc(s_vec.sem, 1)))
                    v_act = s_act.inc()
                    exp_acts.append(v_act)
                    ph.add("scalar", lambda sc, c=c, pb=pb, sreg=sreg, w=v_vec,
                           wtr=tr_hist[pb]:
                           (sc.wait_ge(s_vec.sem, w),
                            sc.wait_ge(s_pe.sem, wtr),
                            sc.activation(P_sb[:, pb, c * QS:(c + 1) * QS],
                                          sreg[:, 0:QS], AF.Exp,
                                          bias=nm4[:, pb, c:c + 1],
                                          accum_out=sums4[:, pb, c:c + 1])
                            .then_inc(s_act.sem, 1)))
                    if pend is not None:
                        G = NC128 // 4
                        for g4 in range(G * c // 4, G * (c + 1) // 4):
                            emit_tr_group(pend, g4)
                exp_acts_prev = exp_acts

                v_vec = chain(ph, "vector", s_vec, [
                    lambda v, pb=pb: v.tensor_reduce(
                        nm1[:, pb, :], nm4[:, pb, 0:4], AX.X, ALU.min),
                    lambda v, pb=pb: v.tensor_scalar(
                        st4[:, pb, :], nm4[:, pb, :], -1.0, nm1[:, pb, :],
                        ALU.mult, ALU.add),
                ], waits=[(s_act.sem, exp_acts[3])])
                v_act = s_act.inc()
                ph.add("scalar", lambda sc, pb=pb, w=v_vec:
                       (sc.wait_ge(s_vec.sem, w),
                        sc.activation(al4[:, pb, :], st4[:, pb, :], AF.Exp)
                        .then_inc(s_act.sem, 1)))
                ch = chain(ph, "vector", s_vec, [
                    lambda v, pb=pb: v.scalar_tensor_tensor(
                        st4[:, pb, :], al4[:, pb, :], 1.0, sums4[:, pb, :],
                        ALU.mult, ALU.mult, accum_out=sto[:, pb, 0:1]),
                    lambda v, pb=pb: v.reciprocal(
                        sto[:, pb, 1:2], sto[:, pb, 0:1]),
                ], waits=[(s_act.sem, v_act)])
                v_vecD = s_vec.inc()
                ph.add("vector", lambda v, pb=pb, w=ch, wtr=tr_hist[pb]:
                       (v.wait_ge(s_vec.sem, w),
                        v.wait_ge(s_pe.sem, wtr),
                        v.tensor_scalar(D_sb[:, pb, 0, :], ident_bf[:],
                                        al4[:, pb, 0:1], sto[:, pb, 1:2],
                                        ALU.mult, ALU.mult),
                        v.tensor_scalar(D_sb[:, pb, 1, :], ident_bf[:],
                                        al4[:, pb, 1:2], sto[:, pb, 1:2],
                                        ALU.mult, ALU.mult),
                        v.tensor_scalar(D_sb[:, pb, 2, :], ident_bf[:],
                                        al4[:, pb, 2:3], sto[:, pb, 1:2],
                                        ALU.mult, ALU.mult),
                        v.tensor_scalar(D_sb[:, pb, 3, :], ident_bf[:],
                                        al4[:, pb, 3:4], sto[:, pb, 1:2],
                                        ALU.mult, ALU.mult)
                        .then_inc(s_vec.sem, 1)))

                if pend is not None and pend["head_end"]:
                    emit_pv(pend["h"])
                pend = {"pb": pb, "vD": v_vecD, "qt": qt,
                        "head_end": qt == QT - 1, "h": h}

        for g4 in range(NC128 // 4):
            emit_tr_group(pend, g4)
        emit_pv(pend["h"])
        pend = None

        for t in range(QT):
            tok = st * QT + t
            p = t % 2
            oreg = (A, B)[p]
            v_in = s_in.inc(16)
            ph.add("sync", lambda g, tok=tok, p=p, w=s_vec.n:
                   (g.wait_ge(s_vec.sem, w),
                    g.dma_start(x_f32[:, p, :], x_h[tok * 128:(tok + 1) * 128, :])
                    .then_inc(s_in.sem, 16)))
            v_pe = None
            for n in range(2):
                for k in range(KD):
                    ph.add("tensor", lambda te, t=t, n=n, k=k, oreg=oreg,
                           wv=s_vec.n, wact=s_act.n:
                           (te.wait_ge(s_vec.sem, wv),
                            te.wait_ge(s_act.sem, wact),
                            te.matmul(oreg[:, n * 512:(n + 1) * 512],
                                      pv_sb[:, k, t * 128:(t + 1) * 128],
                                      wo_sb[:, k, n * 4:(n + 1) * 4, :],
                                      start=(k == 0), stop=False)))
                last = (n == 1)
                if last:
                    v_pe = s_pe.inc()

                def mk(te, n=n, oreg=oreg, last=last):
                    ins = te.matmul(oreg[:, n * 512:(n + 1) * 512], ones_bf[:],
                                    bo_bf[:, n * 512:(n + 1) * 512],
                                    start=False, stop=True)
                    if last:
                        ins.then_inc(s_pe.sem, 1)
                ph.add("tensor", mk)
            v_vec = s_vec.inc()
            ph.add("vector", lambda v, p=p, oreg=oreg, w=v_pe, win=v_in:
                   (v.wait_ge(s_pe.sem, w),
                    v.wait_ge(s_in.sem, win),
                    v.scalar_tensor_tensor(r1_f32[:, 0, :], oreg, 1.0,
                                           x_f32[:, p, :], ALU.mult, ALU.add,
                                           accum_out=lnv(p, 0, 1))
                    .then_inc(s_vec.sem, 1)))
            v_act = s_act.inc()
            ph.add("scalar", lambda sc, p=p, w=v_vec:
                   (sc.wait_ge(s_vec.sem, w),
                    sc.activation(junk_bf[:], r1_f32[:, 0, :], AF.Square,
                                  accum_out=lnv(p, 1, 2)).then_inc(s_act.sem, 1)))
            v_vec2 = chain(ph, "vector", s_vec, [
                lambda v, p=p: v.tensor_scalar(lnv(p, 2, 3), lnv(p, 0, 1),
                                               1.0 / D, None, ALU.mult),
                lambda v, p=p: v.tensor_scalar(lnv(p, 3, 4), lnv(p, 1, 2),
                                               1.0 / D, None, ALU.mult),
                lambda v, p=p: v.tensor_mul(lnv(p, 4, 5), lnv(p, 2, 3),
                                            lnv(p, 2, 3)),
                lambda v, p=p: v.tensor_sub(lnv(p, 3, 4), lnv(p, 3, 4),
                                            lnv(p, 4, 5)),
            ], waits=[(s_act.sem, v_act)])
            v_act2 = chain(ph, "scalar", s_act, [
                lambda sc, p=p: sc.activation(lnv(p, 5, 6), lnv(p, 3, 4),
                                              AF.Ln, bias=eps_sb[:]),
                lambda sc, p=p: sc.activation(lnv(p, 6, 7), lnv(p, 5, 6),
                                              AF.Exp, scale=-0.5),
            ], waits=[(s_vec.sem, v_vec2)])
            v_vec3 = chain(ph, "vector", s_vec, [
                lambda v, p=p: v.tensor_mul(lnv(p, 7, 8), lnv(p, 2, 3),
                                            lnv(p, 6, 7)),
                lambda v, p=p: v.tensor_scalar(lnv(p, 7, 8), lnv(p, 7, 8),
                                               -1.0, None, ALU.mult),
                lambda v, p=p: v.tensor_scalar(r1n_f32[:, 0, :], r1_f32[:, 0, :],
                                               lnv(p, 6, 7), lnv(p, 7, 8),
                                               ALU.mult, ALU.add),
            ], waits=[(s_act.sem, v_act2), (s_out.sem, r1n_dma_hist[p])])
            r1n_dma_hist[p] = s_out.inc(16)
            ph.add("sync", lambda g, tok=tok, p=p, w=v_vec3:
                   (g.wait_ge(s_vec.sem, w),
                    g.dma_start(r1n_h[tok * 128:(tok + 1) * 128, :],
                                r1n_f32[:, 0, :]).then_inc(s_out.sem, 16)))
        if dbg:
            ph.commit()
        if dbg and st == 0:
            qT_d = nc.dram_tensor("qT_d", [128, MD * ST], BF16, kind="ExternalOutput")
            stt_d = nc.dram_tensor("stt_d", [128, 2 * (4 * 4 + 1 + 2)], F32, kind="ExternalOutput")
            Dd_d = nc.dram_tensor("Dd_d", [128, 2 * 4 * 128], BF16, kind="ExternalOutput")
            P_d = nc.dram_tensor("P_d", [128, S], BF16, kind="ExternalOutput")
            wT_d = nc.dram_tensor("wT_d", [128, NT * ST], BF16, kind="ExternalOutput")
            pv_d = nc.dram_tensor("pv_d", [128, KD * ST], BF16, kind="ExternalOutput")
            ph = Phase(nc)
            s_out.inc(96)
            ph.add("sync", lambda g:
                   (g.dma_start(stt_d[:, 0:8], nm4[:].rearrange("p a b -> p (a b)"))
                    .then_inc(s_out.sem, 16),
                    g.dma_start(stt_d[:, 8:16], sums4[:].rearrange("p a b -> p (a b)"))
                    .then_inc(s_out.sem, 16),
                    g.dma_start(stt_d[:, 16:24], al4[:].rearrange("p a b -> p (a b)"))
                    .then_inc(s_out.sem, 16),
                    g.dma_start(stt_d[:, 24:32], st4[:].rearrange("p a b -> p (a b)"))
                    .then_inc(s_out.sem, 16),
                    g.dma_start(stt_d[:, 32:36],
                                sto[:].rearrange("p a b -> p (a b)"))
                    .then_inc(s_out.sem, 16),
                    g.dma_start(Dd_d[:], D_sb[:].rearrange("p a b c -> p (a b c)"))
                    .then_inc(s_out.sem, 16),
                    g.dma_start(qT_d[:], qT[:].rearrange("p a b -> p (a b)"))
                    .then_inc(s_out.sem, 16),
                    g.dma_start(P_d[:], P_sb[:, 1, :]).then_inc(s_out.sem, 16),
                    g.dma_start(wT_d[:], wT_sb[:].rearrange("p a b -> p (a b)"))
                    .then_inc(s_out.sem, 16),
                    g.dma_start(pv_d[:], pv_sb[:].rearrange("p a b -> p (a b)"))
                    .then_inc(s_out.sem, 16),
                    g.wait_ge(s_out.sem, s_out.n)))
            ph.commit()

    if not dbg:
        ph.commit()
    for cm in reversed(actx):
        cm.__exit__(None, None, None)

    fctx = []
    gw_bf = alloc([128, KD, E], BF16, "gw_bf", fctx)
    rq_f32 = alloc([128, TQ, D], F32, "rq_f32", fctx)
    rqT = alloc([128, KD, QU], BF16, "rqT", fctx)
    acc = alloc([128, TQ, D], F32, "acc", fctx)
    su_sb = alloc([128, MF, QU], BF16, "su_sb", fctx)
    gu_sb = alloc([128, 4, KD, 128], BF16, "gu_sb", fctx)
    wd_sb = alloc([128, 2, MF, D], BF16, "wd_sb", fctx)
    gst = alloc([128, QU], BF16, "gst", fctx)
    lg_fm = alloc([E, QU], F32, "lg_fm", fctx)
    lg_tm = alloc([128, TQ, E], F32, "lg_tm", fctx)
    comb = alloc([128, TQ, E], F32, "comb", fctx)
    cwk = alloc([128, 4 * E], F32, "cwk", fctx)
    out_sb = alloc([128, 2, D], F32, "out_sb", fctx)
    rq_bf = alloc([128, 2, D], BF16, "rq_bf", fctx)

    r1n_done = s_out.n
    ln2_v3 = {t: 0 for t in range(TQ)}

    first_q = True
    ph = Phase(nc)
    for qu in range(NU):
        if first_q:
            gload(gw_bf[:], gw_h[:])
            first_q = False
        for t in range(TQ):
            p = t % 2
            v_in = s_in.inc(16)
            ph.add("sync", lambda g, qu=qu, t=t, w=r1n_done,
                   wv=ln2_v3[t]:
                   (g.wait_ge(s_out.sem, w),
                    g.wait_ge(s_vec.sem, wv),
                    g.dma_start(rq_f32[:, t, :],
                                r1n_h[(qu * TQ + t) * 128:
                                      (qu * TQ + t + 1) * 128, :])
                    .then_inc(s_in.sem, 16)))
            v_act = s_act.inc()
            ph.add("scalar", lambda sc, t=t, p=p, w=v_in, wpe=s_pe.n:
                   (sc.wait_ge(s_in.sem, w),
                    sc.wait_ge(s_pe.sem, wpe),
                    sc.copy(rq_bf[:, p, :], rq_f32[:, t, :])
                    .then_inc(s_act.sem, 1)))
            for half in range(2):
                reg = C[:, half * 512:(half + 1) * 512]
                v_pe = None
                for f4 in range(4):
                    f = half * 4 + f4
                    last = (f4 == 3)
                    if last:
                        v_pe = s_pe.inc()

                    def mk(te, p=p, f=f, f4=f4, reg=reg, w=v_act,
                           wv=s_vec.n, wg=s_w.n, last=last):
                        te.wait_ge(s_act.sem, w)
                        te.wait_ge(s_vec.sem, wv)
                        te.wait_ge(s_w.sem, wg)
                        ins = te.matmul(reg[:, f4 * 128:(f4 + 1) * 128],
                                        rq_bf[:, p, f * 128:(f + 1) * 128],
                                        ident_bf[:], start=True, stop=True)
                        if last:
                            ins.then_inc(s_pe.sem, 1)
                    ph.add("tensor", mk)
                s_vec.inc()
                ph.add("vector", lambda v, t=t, half=half, reg=reg, w=v_pe:
                       (v.wait_ge(s_pe.sem, w),
                        v.tensor_copy(
                            rqT[:, half * 4:half * 4 + 4, t * 128:(t + 1) * 128],
                            reg.rearrange("p (a b) -> p a b", a=4))
                        .then_inc(s_vec.sem, 1)))

        for n in range(NQ512):
            lreg = Dr[0:E, n * QW:(n + 1) * QW]
            v_pe = None
            for k in range(KD):
                last = (k == KD - 1)
                if last:
                    v_pe = s_pe.inc()

                def mk(te, n=n, k=k, lreg=lreg, wv=s_vec.n, wg=s_w.n, last=last):
                    te.wait_ge(s_vec.sem, wv)
                    te.wait_ge(s_w.sem, wg)
                    ins = te.matmul(lreg, gw_bf[:, k, :],
                                    rqT[:, k, n * QW:(n + 1) * QW],
                                    start=(k == 0), stop=last)
                    if last:
                        ins.then_inc(s_pe.sem, 1)
                ph.add("tensor", mk)
            s_vec.inc()
            ph.add("vector", lambda v, n=n, lreg=lreg, w=v_pe:
                   (v.wait_ge(s_pe.sem, w),
                    v.tensor_copy(lg_fm[:, n * QW:(n + 1) * QW], lreg)
                    .then_inc(s_vec.sem, 1)))
        for t in range(TQ):
            treg = C[:, (t % 2) * 512:(t % 2) * 512 + 128]
            v_pe = s_pe.inc()
            ph.add("tensor", lambda te, t=t, treg=treg, w=s_vec.n:
                   (te.wait_ge(s_vec.sem, w),
                    te.matmul(treg[:, 0:E], lg_fm[:, t * 128:(t + 1) * 128],
                              ident_f[0:E, 0:E], start=True, stop=True)
                    .then_inc(s_pe.sem, 1)))
            v_vec = s_vec.inc()
            ph.add("vector", lambda v, t=t, treg=treg, w=v_pe:
                   (v.wait_ge(s_pe.sem, w),
                    v.tensor_copy(lg_tm[:, t, :], treg[:, 0:E])
                    .then_inc(s_vec.sem, 1)))
            c0, c1, c2 = cwk[:, 0:E], cwk[:, E:2 * E], cwk[:, 2 * E:3 * E]
            m1, m2, dw = lnst[:, 0:1], lnst[:, 1:2], lnst[:, 2:3]
            w1, w2 = lnst[:, 3:4], lnst[:, 4:5]
            v_vec2 = chain(ph, "vector", s_vec, [
                lambda v, t=t, m1=m1: v.tensor_reduce(
                    m1, lg_tm[:, t, :], AX.X, ALU.max),
                lambda v, t=t, c0=c0, m1=m1: v.tensor_scalar(
                    c0, lg_tm[:, t, :], m1, None, ALU.is_equal),
                lambda v, t=t, c0=c0, c1=c1: v.scalar_tensor_tensor(
                    c1, c0, -1e30, lg_tm[:, t, :], ALU.mult, ALU.add),
                lambda v, c1=c1, m2=m2: v.tensor_reduce(m2, c1, AX.X, ALU.max),
                lambda v, m1=m1, m2=m2, dw=dw: v.tensor_sub(dw, m1, m2),
            ], waits=[(s_vec.sem, v_vec)])
            v_act = s_act.inc()
            ph.add("scalar", lambda sc, w=v_vec2, w1=w1, dw=dw:
                   (sc.wait_ge(s_vec.sem, w),
                    sc.activation(w1, dw, AF.Exp, scale=-1.0)
                    .then_inc(s_act.sem, 1)))
            chain(ph, "vector", s_vec, [
                lambda v, w1=w1: v.tensor_scalar(
                    w1, w1, 1.0, None, ALU.add),
                lambda v, w1=w1: v.reciprocal(w1, w1),
                lambda v, w1=w1, w2=w2: v.tensor_scalar(
                    w2, w1, -1.0, 1.0, ALU.mult, ALU.add),
                lambda v, c1=c1, c2=c2, m2=m2: v.tensor_scalar(
                    c2, c1, m2, None, ALU.is_equal),
                lambda v, c0=c0, w1=w1: v.tensor_scalar(
                    c0, c0, w1, None, ALU.mult),
                lambda v, t=t, c0=c0, c2=c2, w2=w2: v.scalar_tensor_tensor(
                    comb[:, t, :], c2, w2, c0, ALU.mult, ALU.add),
            ], waits=[(s_act.sem, v_act)])

        lg_done = s_vec.n
        gu_pe_hist = [0] * 4
        rd = {k: (s_vec.sem, lg_done) for k in "ABCD"}
        regs = {"A": A, "B": B, "C": C, "D": Dr}
        acc_hist = {0: 0, 1: 0}
        gst_hist = 0
        wd_v = {}
        for ei in range(NEXP):
            if ei == 0:
                wd_v[0] = s_w.inc(16)
                ph.add("gpsimd", lambda g, w=s_pe.n:
                       (g.wait_ge(s_pe.sem, w),
                        g.dma_start(wd_sb[:, 0, :, :], wd_h[0])
                        .then_inc(s_w.sem, 16)))
            v_wd = wd_v[ei]
            su_v = {}
            for mp in range(MF):
                v_pe = None
                v_pe_g = None
                for gi in range(2):
                    ms = mp * 2 + gi
                    bufi = ms % 4
                    v_g = s_w.inc(16)
                    ph.add("gpsimd", lambda g, ei=ei, ms=ms, bufi=bufi,
                           w=gu_pe_hist[bufi]:
                           (g.wait_ge(s_pe.sem, w),
                            g.dma_start(gu_sb[:, bufi, :, :], gu_h[ei, ms])
                            .then_inc(s_w.sem, 16)))
                    if mp == 3 and gi == 0 and ei + 1 < NEXP:
                        wd_v[ei + 1] = s_w.inc(16)
                        ph.add("gpsimd", lambda g, ei=ei, w=s_pe.n:
                               (g.wait_ge(s_pe.sem, w),
                                g.dma_start(wd_sb[:, (ei + 1) % 2, :, :],
                                            wd_h[ei + 1]).then_inc(s_w.sem, 16)))
                    rname = ("A", "C")[mp % 2] if gi == 0 else ("B", "D")[mp % 2]
                    greg = regs[rname]
                    rsem, rval = rd[rname]
                    for n in range(NQ512):
                        for k in range(KD):
                            last = (k == KD - 1 and n == NQ512 - 1)
                            if last:
                                v_pe = s_pe.inc()
                                if gi == 0:
                                    v_pe_g = v_pe

                            def mk(te, bufi=bufi, n=n, k=k, greg=greg,
                                   w=v_g, rsem=rsem, rval=rval, last=last):
                                te.wait_ge(s_w.sem, w)
                                te.wait_ge(rsem, rval)
                                ins = te.matmul(
                                    greg[:, n * QW:(n + 1) * QW],
                                    gu_sb[:, bufi, k, :],
                                    rqT[:, k, n * QW:(n + 1) * QW],
                                    start=(k == 0), stop=(k == KD - 1))
                                if last:
                                    ins.then_inc(s_pe.sem, 1)
                            ph.add("tensor", mk)
                    gu_pe_hist[bufi] = v_pe
                rname_g = ("A", "C")[mp % 2]
                rname_u = ("B", "D")[mp % 2]
                v_act = s_act.inc()
                ph.add("scalar", lambda sc, w=v_pe_g, wg=gst_hist,
                       greg_g=regs[rname_g]:
                       (sc.wait_ge(s_pe.sem, w),
                        sc.wait_ge(s_vec.sem, wg),
                        sc.activation(gst[:], greg_g[:, 0:QU], AF.Silu)
                        .then_inc(s_act.sem, 1)))
                rd[rname_g] = (s_act.sem, v_act)
                v_vec = s_vec.inc()
                ph.add("vector", lambda v, mp=mp, w=v_act, wpe=v_pe,
                       greg_u=regs[rname_u]:
                       (v.wait_ge(s_act.sem, w),
                        v.wait_ge(s_pe.sem, wpe),
                        v.tensor_mul(su_sb[:, mp, :], greg_u[:, 0:QU], gst[:])
                        .then_inc(s_vec.sem, 1)))
                rd[rname_u] = (s_vec.sem, v_vec)
                gst_hist = v_vec
                su_v[mp] = v_vec
            su_done = s_vec.n
            for t in range(TQ):
                rname = ("A", "B")[t % 2]
                yreg = regs[rname]
                rsem, rval = rd[rname]
                v_pe = None
                for k in range(MF):
                    for n in range(2):
                        last = (k == MF - 1 and n == 1)
                        if last:
                            v_pe = s_pe.inc()

                        def mk(te, t=t, k=k, n=n, ei=ei, yreg=yreg,
                               w=max(su_v[k], acc_hist[t % 2]), wwd=v_wd,
                               rsem=rsem, rval=rval, last=last):
                            te.wait_ge(s_vec.sem, w)
                            te.wait_ge(rsem, rval)
                            te.wait_ge(s_w.sem, wwd)
                            ins = te.matmul(yreg[:, n * 512:(n + 1) * 512],
                                            su_sb[:, k, t * 128:(t + 1) * 128],
                                            wd_sb[:, ei % 2, k,
                                                  n * 512:(n + 1) * 512],
                                            start=(k == 0), stop=(k == MF - 1))
                            if last:
                                ins.then_inc(s_pe.sem, 1)
                        ph.add("tensor", mk)
                v_y = s_vec.inc()
                if ei == 0:
                    ph.add("vector", lambda v, t=t, yreg=yreg, w=v_pe:
                           (v.wait_ge(s_pe.sem, w),
                            v.tensor_copy(acc[:, t, :], yreg)
                            .then_inc(s_vec.sem, 1)))
                else:
                    ph.add("vector", lambda v, t=t, yreg=yreg, ei=ei, w=v_pe:
                           (v.wait_ge(s_pe.sem, w),
                            v.scalar_tensor_tensor(acc[:, t, :], yreg,
                                                   comb[:, t, ei - 1:ei],
                                                   acc[:, t, :],
                                                   ALU.mult, ALU.add)
                            .then_inc(s_vec.sem, 1)))
                acc_hist[t % 2] = v_y
                rd[rname] = (s_vec.sem, v_y)

        acc_last = s_vec.n
        for t in range(TQ):
            p = t % 2
            v_vec = s_vec.inc()
            ph.add("vector", lambda v, t=t, wa=acc_last:
                   (v.wait_ge(s_vec.sem, wa),
                    v.scalar_tensor_tensor(rq_f32[:, t, :], acc[:, t, :], 1.0,
                                           rq_f32[:, t, :], ALU.mult, ALU.add,
                                           accum_out=lnv(p, 0, 1))
                    .then_inc(s_vec.sem, 1)))
            v_act = s_act.inc()
            ph.add("scalar", lambda sc, t=t, p=p, w=v_vec:
                   (sc.wait_ge(s_vec.sem, w),
                    sc.activation(junk_bf[:], rq_f32[:, t, :], AF.Square,
                                  accum_out=lnv(p, 1, 2)).then_inc(s_act.sem, 1)))
            v_vec2 = chain(ph, "vector", s_vec, [
                lambda v, p=p: v.tensor_scalar(lnv(p, 2, 3), lnv(p, 0, 1),
                                               1.0 / D, None, ALU.mult),
                lambda v, p=p: v.tensor_scalar(lnv(p, 3, 4), lnv(p, 1, 2),
                                               1.0 / D, None, ALU.mult),
                lambda v, p=p: v.tensor_mul(lnv(p, 4, 5), lnv(p, 2, 3),
                                            lnv(p, 2, 3)),
                lambda v, p=p: v.tensor_sub(lnv(p, 3, 4), lnv(p, 3, 4),
                                            lnv(p, 4, 5)),
            ], waits=[(s_act.sem, v_act)])
            v_act2 = chain(ph, "scalar", s_act, [
                lambda sc, p=p: sc.activation(lnv(p, 5, 6), lnv(p, 3, 4),
                                              AF.Ln, bias=eps_sb[:]),
                lambda sc, p=p: sc.activation(lnv(p, 6, 7), lnv(p, 5, 6),
                                              AF.Exp, scale=-0.5),
            ], waits=[(s_vec.sem, v_vec2)])
            v_vec3 = chain(ph, "vector", s_vec, [
                lambda v, p=p: v.tensor_mul(lnv(p, 7, 8), lnv(p, 2, 3),
                                            lnv(p, 6, 7)),
                lambda v, p=p: v.tensor_scalar(lnv(p, 7, 8), lnv(p, 7, 8),
                                               -1.0, None, ALU.mult),
                lambda v, t=t, p=p: v.tensor_scalar(
                    out_sb[:, p, :], rq_f32[:, t, :], lnv(p, 6, 7),
                    lnv(p, 7, 8), ALU.mult, ALU.add),
            ], waits=[(s_act.sem, v_act2), (s_out.sem, out_dma_hist[p])])
            ln2_v3[t] = v_vec3
            out_dma_hist[p] = s_out.inc(16)
            ph.add("sync", lambda g, qu=qu, t=t, p=p, w=v_vec3:
                   (g.wait_ge(s_vec.sem, w),
                    g.dma_start(y_h[(qu * TQ + t) * 128:
                                    (qu * TQ + t + 1) * 128, :],
                                out_sb[:, p, :]).then_inc(s_out.sem, 16)))
    ph.commit()

    ph = Phase(nc)
    ph.add("sync", lambda g, w=s_out.n: g.wait_ge(s_out.sem, w))
    ph.commit()

    for cm in reversed(fctx):
        cm.__exit__(None, None, None)
    for cm in reversed(ctx):
        cm.__exit__(None, None, None)
    return nc


def prep_core_inputs(inputs, S):
    f = np.ascontiguousarray
    Wq = np.asarray(inputs["L0_attn_Wq"], dtype=np.float32)
    Wk = np.asarray(inputs["L0_attn_Wk"], dtype=np.float32)
    Wv = np.asarray(inputs["L0_attn_Wv"], dtype=np.float32)
    Wo = np.asarray(inputs["L0_attn_Wo"], dtype=np.float32)
    bq = np.asarray(inputs["L0_attn_bq"], dtype=np.float32)
    bk = np.asarray(inputs["L0_attn_bk"], dtype=np.float32)
    bv = np.asarray(inputs["L0_attn_bv"], dtype=np.float32)
    bo = np.asarray(inputs["L0_attn_bo"], dtype=np.float32)
    gate_w = np.asarray(inputs["L0_ffn_gate_w"], dtype=np.float32)
    w_up = np.asarray(inputs["L0_ffn_w_up"], dtype=np.float32)
    w_down = np.asarray(inputs["L0_ffn_w_down"], dtype=np.float32)
    sg_gate = np.asarray(inputs["L0_ffn_sg_gate"], dtype=np.float32)
    sg_up = np.asarray(inputs["L0_ffn_sg_up"], dtype=np.float32)
    sg_down = np.asarray(inputs["L0_ffn_sg_down"], dtype=np.float32)

    def tile_lhst(W):
        i_, o_ = W.shape
        return f(W.reshape(i_ // 128, 128, o_ // 128, 128).transpose(1, 0, 2, 3))

    bkv = np.stack([np.concatenate([bk, bk]), np.concatenate([bv, bv])], axis=1)
    common = {
        "wq": tile_lhst(Wq),
        "wo": tile_lhst(Wo),
        "wk": f(Wk.reshape(KD, 128, DK).transpose(1, 0, 2)),
        "wv": f(Wv.reshape(KD, 128, DK).transpose(1, 0, 2)),
        "bq": f(bq.reshape(MD, 128).T),
        "bkv": f(bkv.astype(np.float32)),
        "bo": bo.reshape(1, D),
        "gw": f(gate_w.T.reshape(KD, 128, E).transpose(1, 0, 2)),
        "ident": np.eye(128, dtype=np.float32),
        "ones": np.ones((1, 128), dtype=np.float32),
    }

    gu_list, wd_list = [], []
    order = np.empty(2 * MF, dtype=np.int64)
    order[0::2] = np.arange(MF)
    order[1::2] = np.arange(MF) + MF
    for e in range(NEXP):
        if e == 0:
            wu = np.concatenate([sg_gate, sg_up], axis=0)
            wdn = sg_down
        else:
            wu = w_up[e - 1]
            wdn = w_down[e - 1]
        g4 = wu.reshape(2 * MF, 128, KD, 128).transpose(0, 3, 2, 1)
        gu_list.append(g4[order])
        wd_list.append(wdn.T.reshape(MF, 128, D).transpose(1, 0, 2))
    common["gu"] = f(np.stack(gu_list, axis=0))
    common["wd"] = f(np.stack(wd_list, axis=0))

    x = np.asarray(inputs["L0_attn_x"], dtype=np.float32)
    in_maps = []
    for c in range(x.shape[0]):
        m = dict(common)
        m["x"] = f(x[c, :S, :])
        in_maps.append(m)
    return in_maps


def kernel(**inputs):
    S = 4096
    nc = build_nc(S)
    in_maps = prep_core_inputs(inputs, S)
    res = run_bass_kernel_spmd(nc, in_maps, core_ids=list(range(8)))
    out = np.stack([np.asarray(res.results[c]["y"]) for c in range(8)], axis=0)
    return out


# version 29 (final)
# speedup vs baseline: 1.0518x; 1.0092x over previous
import numpy as np
import concourse.bass as bass
import concourse.mybir as mybir
from concourse.bass_utils import run_bass_kernel_spmd

F32 = mybir.dt.float32
BF16 = mybir.dt.bfloat16
AF = mybir.ActivationFunctionType
ALU = mybir.AluOpType
AX = mybir.AxisListType

D = 1024
H = 16
DK = 64
E = 8
HALF = 1536
KD = D // 128
MD = D // 128
MF = HALF // 128
NEXP = E + 1


class Ctr:
    __slots__ = ("sem", "n")

    def __init__(self, sem):
        self.sem = sem
        self.n = 0

    def inc(self, k=1):
        self.n += k
        return self.n


class WaitFilter:

    def __init__(self, eng, state):
        self._e = eng
        self._state = state

    def wait_ge(self, sem, val):
        key = sem.num
        if self._state.get(key, -1) >= val:
            return None
        self._state[key] = val
        return self._e.wait_ge(sem, val)

    def __getattr__(self, k):
        return getattr(self._e, k)


class Phase:

    WSTATE = {}

    def __init__(self, nc):
        self.nc = nc
        self.q = {k: [] for k in ("tensor", "vector", "scalar", "sync", "gpsimd")}

    def add(self, eng, fn):
        self.q[eng].append(fn)

    def commit(self):
        with self.nc.Block() as blk:
            for name, items in self.q.items():
                if not items:
                    continue

                def runner(e, items=items, name=name):
                    wf = WaitFilter(e, Phase.WSTATE.setdefault((id(self.nc), name), {}))
                    for fn in items:
                        fn(wf)
                getattr(blk, name)(runner)


def chain(ph, eng, ctr, builders, waits=None):
    vals = [ctr.inc() for _ in builders]

    def run(e, builders=builders, vals=vals, waits=waits):
        if waits:
            for sm, v in waits:
                e.wait_ge(sm, v)
        last = len(builders) - 1
        for i, (f, val) in enumerate(zip(builders, vals)):
            f(e).then_inc(ctr.sem, 1)
            if i < last:
                e.wait_ge(ctr.sem, val)
    ph.add(eng, run)
    return vals[-1]


def build_nc(S, dbg=False):
    NT = S // 128
    QS = S // 4
    NW = min(512, QS)
    NB = QS // NW
    ST = min(512, S)
    NST = S // ST
    QT = ST // 128
    QU = S // 4
    TQ = QU // 128
    NU = S // QU
    NQ512 = max(1, QU // 512)
    QW = min(512, QU)

    nc = bass.Bass("TRN2", target_bir_lowering=False, detect_race_conditions=False)

    x_h = nc.dram_tensor("x", [S, D], F32, kind="ExternalInput")
    wq_h = nc.dram_tensor("wq", [128, KD, MD, 128], F32, kind="ExternalInput")
    wk_h = nc.dram_tensor("wk", [128, KD, DK], F32, kind="ExternalInput")
    wv_h = nc.dram_tensor("wv", [128, KD, DK], F32, kind="ExternalInput")
    wo_h = nc.dram_tensor("wo", [128, KD, MD, 128], F32, kind="ExternalInput")
    bq_h = nc.dram_tensor("bq", [128, MD], F32, kind="ExternalInput")
    bkv_h = nc.dram_tensor("bkv", [128, 2], F32, kind="ExternalInput")
    bo_h = nc.dram_tensor("bo", [1, D], F32, kind="ExternalInput")
    gw_h = nc.dram_tensor("gw", [128, KD, E], F32, kind="ExternalInput")
    gu_h = nc.dram_tensor("gu", [NEXP, 2 * MF, 128, KD, 128], F32,
                          kind="ExternalInput")
    wd_h = nc.dram_tensor("wd", [NEXP, 128, MF, D], F32, kind="ExternalInput")
    id_h = nc.dram_tensor("ident", [128, 128], F32, kind="ExternalInput")
    ones_h = nc.dram_tensor("ones", [1, 128], F32, kind="ExternalInput")
    r1n_h = nc.dram_tensor("r1n", [S, D], F32,
                           kind="ExternalOutput" if dbg else "Internal")
    y_h = nc.dram_tensor("y", [S, D], F32, kind="ExternalOutput")

    ctx = []

    def sem(name):
        cm = nc.semaphore(name)
        ctx.append(cm)
        return Ctr(cm.__enter__())

    def alloc(shape, dtype, name, ctx=ctx):
        cm = nc.sbuf_tensor(name, shape, dtype)
        ctx.append(cm)
        return cm.__enter__()

    s_in = sem("s_in")
    s_w = sem("s_w")
    s_out = sem("s_out")
    s_pe = sem("s_pe")
    s_vec = sem("s_vec")
    s_act = sem("s_act")

    ps_cm = nc.psum_tensor("ps", [128, 4096], F32)
    ctx.append(ps_cm)
    ps = ps_cm.__enter__()
    A = ps[:, 0:1024]
    B = ps[:, 1024:2048]
    C = ps[:, 2048:3072]
    Dr = ps[:, 3072:4096]

    ident_bf = alloc([128, 128], BF16, "ident_bf")
    ident_f = alloc([128, 128], F32, "ident_f")
    ones_bf = alloc([1, 128], BF16, "ones_bf")
    junk_bf = alloc([128, D], BF16, "junk_bf")
    lnst = alloc([128, 16], F32, "lnst")

    def lnv(p, a, b):
        return lnst[:, p * 8 + a:p * 8 + b]
    eps_sb = alloc([128, 1], F32, "eps_sb")
    x_f32 = alloc([128, 2, D], F32, "x_f32")
    x_bf = alloc([128, 1, D], BF16, "x_bf")

    actx = []
    xT = alloc([128, KD, S], BF16, "xT", actx)
    kT = alloc([128, S], BF16, "kT", actx)
    vT_bf = alloc([DK, S], BF16, "vT_bf", actx)
    v_sb = alloc([128, NT, DK], BF16, "v_sb", actx)
    wq_sb = alloc([128, KD, MD, 128], BF16, "wq_sb", actx)
    wo_sb = alloc([128, KD, MD, 128], BF16, "wo_sb", actx)
    wk_sb = alloc([128, KD, DK], BF16, "wk_sb", actx)
    wv_sb = alloc([128, KD, DK], BF16, "wv_sb", actx)
    bq_sb = alloc([128, MD], F32, "bq_sb", actx)
    bkv_sb = alloc([128, 2], F32, "bkv_sb", actx)
    bkv_sc = alloc([128, 2], F32, "bkv_sc", actx)
    bo_bf = alloc([1, D], BF16, "bo_bf", actx)
    qT = alloc([128, MD, ST], BF16, "qT", actx)
    pv_sb = alloc([128, KD, ST], BF16, "pv_sb", actx)
    P_sb = alloc([128, 2, S], BF16, "P_sb", actx)
    wT_sb = alloc([128, NT, ST], BF16, "wT_sb", actx)
    D_sb = alloc([128, 2, 4, 128], BF16, "D_sb", actx)
    nm4 = alloc([128, 2, 4], F32, "nm4", actx)
    sums4 = alloc([128, 2, 4], F32, "sums4", actx)
    st4 = alloc([128, 2, 4], F32, "st4", actx)
    al4 = alloc([128, 2, 4], F32, "al4", actx)
    nm1 = alloc([128, 2, 1], F32, "nm1", actx)
    sto = alloc([128, 2, 2], F32, "sto", actx)
    r1_f32 = alloc([128, 1, D], F32, "r1_f32", actx)
    r1n_f32 = alloc([128, 1, D], F32, "r1n_f32", actx)

    ph = Phase(nc)

    def gload(dst, src):
        v = s_w.inc(16)
        ph.add("gpsimd", lambda g, dst=dst, src=src:
               g.dma_start(dst, src).then_inc(s_w.sem, 16))
        return v

    def sload(dst, src):
        v = s_in.inc(16)
        ph.add("sync", lambda g, dst=dst, src=src:
               g.dma_start(dst, src).then_inc(s_in.sem, 16))
        return v

    gload(ident_bf[:], id_h[:])
    gload(ones_bf[:], ones_h[:])
    gload(bo_bf[:], bo_h[:])
    gload(wq_sb[:], wq_h[:])
    gload(wo_sb[:], wo_h[:])
    gload(wk_sb[:], wk_h[:])
    gload(wv_sb[:], wv_h[:])
    sload(ident_f[:], id_h[:])
    sload(bq_sb[:], bq_h[:])
    v_bkv = sload(bkv_sb[:], bkv_h[:])

    def _bkv(v, w=v_bkv):
        v.wait_ge(s_in.sem, w)
        v.memset(eps_sb[:], 1e-5)
        v.tensor_scalar(bkv_sc[:, 0:1], bkv_sb[:, 0:1], 0.125, None, ALU.mult)
        v.tensor_copy(bkv_sc[:, 1:2], bkv_sb[:, 1:2]).then_inc(s_vec.sem, 1)
    ph.add("vector", _bkv)
    s_vec.inc()
    w_const = s_w.n

    for t in range(NT):
        p = t % 2
        v_in = s_in.inc(16)
        ph.add("sync", lambda g, t=t, p=p, w=s_act.n:
               (g.wait_ge(s_act.sem, w),
                g.dma_start(x_f32[:, p, :], x_h[t * 128:(t + 1) * 128, :])
                .then_inc(s_in.sem, 16)))
        v_act = s_act.inc()
        ph.add("scalar", lambda sc, p=p, w=v_in, wpe=s_pe.n:
               (sc.wait_ge(s_in.sem, w),
                sc.wait_ge(s_pe.sem, wpe),
                sc.copy(x_bf[:, 0, :], x_f32[:, p, :]).then_inc(s_act.sem, 1)))
        for half in range(2):
            reg = A[:, half * 512:(half + 1) * 512]
            v_pe = None
            for f4 in range(4):
                f = half * 4 + f4
                last = (f4 == 3)
                if last:
                    v_pe = s_pe.inc()

                def mk(te, p=p, f=f, f4=f4, w=v_act, wc=w_const,
                       wv=s_vec.n, last=last, reg=reg):
                    te.wait_ge(s_act.sem, w)
                    te.wait_ge(s_w.sem, wc)
                    te.wait_ge(s_vec.sem, wv)
                    ins = te.matmul(reg[:, f4 * 128:(f4 + 1) * 128],
                                    x_bf[:, 0, f * 128:(f + 1) * 128],
                                    ident_bf[:], start=True, stop=True)
                    if last:
                        ins.then_inc(s_pe.sem, 1)
                ph.add("tensor", mk)
            s_vec.inc()
            ph.add("vector", lambda v, t=t, half=half, w=v_pe, reg=reg:
                   (v.wait_ge(s_pe.sem, w),
                    v.tensor_copy(
                        xT[:, half * 4:half * 4 + 4, t * 128:(t + 1) * 128],
                        reg.rearrange("p (a b) -> p a b", a=4))
                    .then_inc(s_vec.sem, 1)))
    ph.commit()

    ph = Phase(nc)
    NS512 = S // 512
    for which, wsb, dst in ((0, wk_sb, kT), (1, wv_sb, vT_bf)):
        for n in range(NS512):
            reg = A[:, (n % 2) * 512:((n % 2) + 1) * 512]
            v_pe = None
            for tp in ((0, (0, 0)), (1, (0, 64))) if which == 0 else ((0, (0, 0)),):
                ti, tpos = tp
                for k in range(KD):
                    last = (k == KD - 1 and ti == (1 if which == 0 else 0))
                    if last:
                        v_pe = s_pe.inc()

                    def mk(te, wsb=wsb, k=k, n=n, reg=reg, ti=ti, tpos=tpos,
                           wact=s_act.n, last=last):
                        te.wait_ge(s_act.sem, wact)
                        out = reg[ti * 64:(ti + 1) * 64, :] if wsb is wk_sb \
                            else reg[0:64, :]
                        ins = te.matmul(out, wsb[:, k, :],
                                        xT[:, k, n * 512:(n + 1) * 512],
                                        start=(k == 0), stop=(k == KD - 1),
                                        tile_position=tpos)
                        if last:
                            ins.then_inc(s_pe.sem, 1)
                    ph.add("tensor", mk)
            s_act.inc()
            if which == 0:
                ph.add("scalar", lambda sc, n=n, reg=reg, w=v_pe:
                       (sc.wait_ge(s_pe.sem, w),
                        sc.activation(kT[:, n * 512:(n + 1) * 512], reg,
                                      AF.Identity, bias=bkv_sc[:, 0:1], scale=0.125)
                        .then_inc(s_act.sem, 1)))
            else:
                ph.add("scalar", lambda sc, n=n, reg=reg, w=v_pe:
                       (sc.wait_ge(s_pe.sem, w),
                        sc.activation(vT_bf[:, n * 512:(n + 1) * 512],
                                      reg[0:64, :], AF.Identity,
                                      bias=bkv_sc[0:64, 1:2])
                        .then_inc(s_act.sem, 1)))
    for t2 in range((NT + 1) // 2):
        p = t2 % 2
        reg = B[:, p * 512:(p + 1) * 512]
        v_pe = None
        njt = min(2, NT - t2 * 2)
        for j in range(njt):
            t = t2 * 2 + j
            last = (j == njt - 1)
            if last:
                v_pe = s_pe.inc()

            def mk(te, t=t, j=j, reg=reg, w=s_act.n, wv=s_vec.n, last=last):
                te.wait_ge(s_act.sem, w)
                te.wait_ge(s_vec.sem, wv)
                ins = te.matmul(reg[:, j * 64:(j + 1) * 64],
                                vT_bf[:, t * 128:(t + 1) * 128],
                                ident_bf[0:64, 0:64], start=True, stop=True)
                if last:
                    ins.then_inc(s_pe.sem, 1)
            ph.add("tensor", mk)
        s_vec.inc()
        ph.add("vector", lambda v, t2=t2, reg=reg, w=v_pe, njt=njt:
               (v.wait_ge(s_pe.sem, w),
                v.tensor_copy(v_sb[:, t2 * 2:t2 * 2 + njt, :],
                              reg[:, 0:njt * 64]
                              .rearrange("p (a b) -> p a b", a=njt))
                .then_inc(s_vec.sem, 1)))
    ph.commit()

    if dbg:
        kT_d = nc.dram_tensor("kT_d", [128, S], BF16, kind="ExternalOutput")
        v_d = nc.dram_tensor("v_d", [128, NT * DK], BF16, kind="ExternalOutput")
        xT_d = nc.dram_tensor("xT_d", [128, KD * S], BF16, kind="ExternalOutput")
        ph = Phase(nc)
        s_out.inc(48)
        ph.add("sync", lambda g:
               (g.dma_start(kT_d[:], kT[:]).then_inc(s_out.sem, 16),
                g.dma_start(v_d[:], v_sb[:].rearrange("p a b -> p (a b)"))
                .then_inc(s_out.sem, 16),
                g.dma_start(xT_d[:], xT[:].rearrange("p a b -> p (a b)"))
                .then_inc(s_out.sem, 16),
                g.wait_ge(s_out.sem, s_out.n)))
        ph.commit()

    NC128 = S // 128
    CPQ = QS // 128
    r1n_dma_hist = {0: 0, 1: 0}
    out_dma_hist = {0: 0, 1: 0}
    xread_hist = {0: 0, 1: 0}
    tr_hist = {0: 0, 1: 0}
    ph = Phase(nc)
    for st in range(NST):
        exp_acts_prev = None
        if dbg and st > 0:
            ph = Phase(nc)
        for m in range(MD):
            reg = (A, B)[m % 2][:, 0:ST]
            v_pe = None
            for k in range(KD):
                last = (k == KD - 1)
                if last:
                    v_pe = s_pe.inc()

                def mk(te, m=m, k=k, reg=reg, wact=s_act.n, wv=s_vec.n,
                       last=last, st=st):
                    te.wait_ge(s_act.sem, wact)
                    te.wait_ge(s_vec.sem, wv)
                    ins = te.matmul(reg, wq_sb[:, k, m, :],
                                    xT[:, k, st * ST:(st + 1) * ST],
                                    start=(k == 0), stop=last)
                    if last:
                        ins.then_inc(s_pe.sem, 1)
                ph.add("tensor", mk)
            s_act.inc()
            ph.add("scalar", lambda sc, m=m, reg=reg, w=v_pe:
                   (sc.wait_ge(s_pe.sem, w),
                    sc.activation(qT[:, m, :], reg, AF.Identity,
                                  bias=bq_sb[:, m:m + 1]).then_inc(s_act.sem, 1)))

        def emit_tr_group(info, g4):
            pb_, vD_, qt_ = info["pb"], info["vD"], info["qt"]
            reg = C[:, (g4 % 2) * 512:((g4 % 2) + 1) * 512]
            v_pe = None
            for j in range(4):
                ci = g4 * 4 + j
                cq = ci // CPQ
                last = (j == 3)
                if last:
                    v_pe = s_pe.inc()

                def mk(te, pb=pb_, ci=ci, cq=cq, j=j, reg=reg,
                       w=vD_, wact=ev_hist["act"][g4 % 2],
                       wv=ev_hist["vec"][g4 % 2], last=last):
                    te.wait_ge(s_vec.sem, w)
                    te.wait_ge(s_act.sem, wact)
                    te.wait_ge(s_vec.sem, wv)
                    ins = te.matmul(reg[:, j * 128:(j + 1) * 128],
                                    P_sb[:, pb, ci * 128:(ci + 1) * 128],
                                    D_sb[:, pb, cq, :],
                                    start=True, stop=True)
                    if last:
                        ins.then_inc(s_pe.sem, 1)
                ph.add("tensor", mk)
            if g4 % 2 == 0:
                v_ev = s_act.inc()
                ph.add("scalar", lambda sc, g4=g4, qt=qt_, reg=reg, w=v_pe:
                       (sc.wait_ge(s_pe.sem, w),
                        sc.copy(wT_sb[:, g4 * 4:g4 * 4 + 4,
                                      qt * 128:(qt + 1) * 128],
                                reg.rearrange("p (a b) -> p a b", a=4))
                        .then_inc(s_act.sem, 1)))
                ev_hist["act"][0] = v_ev
            else:
                v_ev = s_vec.inc()
                ph.add("vector", lambda v, g4=g4, qt=qt_, reg=reg, w=v_pe:
                       (v.wait_ge(s_pe.sem, w),
                        v.tensor_copy(wT_sb[:, g4 * 4:g4 * 4 + 4,
                                            qt * 128:(qt + 1) * 128],
                                      reg.rearrange("p (a b) -> p a b", a=4))
                        .then_inc(s_vec.sem, 1)))
                ev_hist["vec"][1] = v_ev
            if g4 == NC128 // 4 - 1:
                tr_hist[pb_] = v_pe
                pair_marks[(info["h"], info["qt"])] = (s_act.n, s_vec.n)

        def emit_pv(hprev):
            hbp = (hprev % 2) * 64
            pvreg = Dr[hbp:hbp + DK, 0:ST]
            mA = pair_marks[(hprev, 0)]
            mB = pair_marks[(hprev, QT - 1)]
            wact_pv = max(mA[0], mB[0])
            wvec_pv = max(mA[1], mB[1], pv_prev_ev[0])
            v_pe = None
            for ci in range(NT):
                last = (ci == NT - 1)
                if last:
                    v_pe = s_pe.inc()

                def mk(te, ci=ci, hbp=hbp, w=wact_pv, wv=wvec_pv, last=last,
                       pvreg=pvreg):
                    te.wait_ge(s_act.sem, w)
                    te.wait_ge(s_vec.sem, wv)
                    ins = te.matmul(pvreg, v_sb[:, ci, :], wT_sb[:, ci, :],
                                    start=(ci == 0), stop=last,
                                    tile_position=(0, hbp))
                    if last:
                        ins.then_inc(s_pe.sem, 1)
                ph.add("tensor", mk)
            pv_prev_ev[0] = s_vec.inc()
            ph.add("vector", lambda v, hprev=hprev, hbp=hbp, w=v_pe,
                   pvreg=pvreg:
                   (v.wait_ge(s_pe.sem, w),
                    v.tensor_copy(pv_sb[hbp:hbp + 64, hprev // 2, :], pvreg)
                    .then_inc(s_vec.sem, 1)))

        ev_hist = {"act": [s_act.n, s_act.n], "vec": [s_vec.n, s_vec.n]}
        pair_marks = {}
        pv_prev_ev = [0]
        pend = None
        for h in range(H):
            hb = (h % 2) * 64
            for qt in range(QT):
                pb = (h * QT + qt) % 2
                exp_acts = []
                for c in range(4):
                    sreg = (A, B)[c % 2]
                    v_pe = None
                    for nb in range(NB):
                        last = (nb == NB - 1)
                        if last:
                            v_pe = s_pe.inc()
                        if c >= 2:
                            wait_act = exp_acts[c - 2]
                        elif exp_acts_prev is None:
                            wait_act = s_act.n
                        else:
                            wait_act = exp_acts_prev[c + 2]

                        def mk(te, h=h, hb=hb, qt=qt, c=c, nb=nb, sreg=sreg,
                               wact=wait_act, last=last):
                            te.wait_ge(s_act.sem, wact)
                            ins = te.matmul(
                                sreg[:, nb * NW:(nb + 1) * NW],
                                qT[hb:hb + 64, h // 2, qt * 128:(qt + 1) * 128],
                                kT[hb:hb + 64, c * QS + nb * NW:
                                   c * QS + (nb + 1) * NW],
                                start=True, stop=True,
                                tile_position=(hb, 0))
                            if last:
                                ins.then_inc(s_pe.sem, 1)
                        ph.add("tensor", mk)
                    v_vec = s_vec.inc()
                    ph.add("vector", lambda v, c=c, pb=pb, sreg=sreg, w=v_pe:
                           (v.wait_ge(s_pe.sem, w),
                            v.tensor_reduce(nm4[:, pb, c:c + 1], sreg[:, 0:QS],
                                            AX.X, ALU.max, negate=True)
                            .then_inc(s_vec.sem, 1)))
                    v_act = s_act.inc()
                    exp_acts.append(v_act)
                    ph.add("scalar", lambda sc, c=c, pb=pb, sreg=sreg, w=v_vec,
                           wtr=tr_hist[pb]:
                           (sc.wait_ge(s_vec.sem, w),
                            sc.wait_ge(s_pe.sem, wtr),
                            sc.activation(P_sb[:, pb, c * QS:(c + 1) * QS],
                                          sreg[:, 0:QS], AF.Exp,
                                          bias=nm4[:, pb, c:c + 1],
                                          accum_out=sums4[:, pb, c:c + 1])
                            .then_inc(s_act.sem, 1)))
                    if pend is not None:
                        G = NC128 // 4
                        for g4 in range(G * c // 4, G * (c + 1) // 4):
                            emit_tr_group(pend, g4)
                exp_acts_prev = exp_acts

                v_vec = chain(ph, "vector", s_vec, [
                    lambda v, pb=pb: v.tensor_reduce(
                        nm1[:, pb, :], nm4[:, pb, 0:4], AX.X, ALU.min),
                    lambda v, pb=pb: v.tensor_scalar(
                        st4[:, pb, :], nm4[:, pb, :], -1.0, nm1[:, pb, :],
                        ALU.mult, ALU.add),
                ], waits=[(s_act.sem, exp_acts[3])])
                v_act = s_act.inc()
                ph.add("scalar", lambda sc, pb=pb, w=v_vec:
                       (sc.wait_ge(s_vec.sem, w),
                        sc.activation(al4[:, pb, :], st4[:, pb, :], AF.Exp)
                        .then_inc(s_act.sem, 1)))
                ch = chain(ph, "vector", s_vec, [
                    lambda v, pb=pb: v.scalar_tensor_tensor(
                        st4[:, pb, :], al4[:, pb, :], 1.0, sums4[:, pb, :],
                        ALU.mult, ALU.mult, accum_out=sto[:, pb, 0:1]),
                    lambda v, pb=pb: v.reciprocal(
                        sto[:, pb, 1:2], sto[:, pb, 0:1]),
                ], waits=[(s_act.sem, v_act)])
                v_vecD = s_vec.inc()
                ph.add("vector", lambda v, pb=pb, w=ch, wtr=tr_hist[pb]:
                       (v.wait_ge(s_vec.sem, w),
                        v.wait_ge(s_pe.sem, wtr),
                        v.tensor_scalar(D_sb[:, pb, 0, :], ident_bf[:],
                                        al4[:, pb, 0:1], sto[:, pb, 1:2],
                                        ALU.mult, ALU.mult),
                        v.tensor_scalar(D_sb[:, pb, 1, :], ident_bf[:],
                                        al4[:, pb, 1:2], sto[:, pb, 1:2],
                                        ALU.mult, ALU.mult),
                        v.tensor_scalar(D_sb[:, pb, 2, :], ident_bf[:],
                                        al4[:, pb, 2:3], sto[:, pb, 1:2],
                                        ALU.mult, ALU.mult),
                        v.tensor_scalar(D_sb[:, pb, 3, :], ident_bf[:],
                                        al4[:, pb, 3:4], sto[:, pb, 1:2],
                                        ALU.mult, ALU.mult)
                        .then_inc(s_vec.sem, 1)))

                if pend is not None and pend["head_end"]:
                    emit_pv(pend["h"])
                pend = {"pb": pb, "vD": v_vecD, "qt": qt,
                        "head_end": qt == QT - 1, "h": h}

        for g4 in range(NC128 // 4):
            emit_tr_group(pend, g4)
        emit_pv(pend["h"])
        pend = None

        for t in range(QT):
            tok = st * QT + t
            p = t % 2
            oreg = (A, B)[p]
            v_in = s_in.inc(16)
            ph.add("sync", lambda g, tok=tok, p=p, w=xread_hist[p]:
                   (g.wait_ge(s_vec.sem, w),
                    g.dma_start(x_f32[:, p, :], x_h[tok * 128:(tok + 1) * 128, :])
                    .then_inc(s_in.sem, 16)))
            v_pe = None
            for n in range(2):
                for k in range(KD):
                    ph.add("tensor", lambda te, t=t, n=n, k=k, oreg=oreg,
                           wv=s_vec.n, wact=s_act.n:
                           (te.wait_ge(s_vec.sem, wv),
                            te.wait_ge(s_act.sem, wact),
                            te.matmul(oreg[:, n * 512:(n + 1) * 512],
                                      pv_sb[:, k, t * 128:(t + 1) * 128],
                                      wo_sb[:, k, n * 4:(n + 1) * 4, :],
                                      start=(k == 0), stop=False)))
                last = (n == 1)
                if last:
                    v_pe = s_pe.inc()

                def mk(te, n=n, oreg=oreg, last=last):
                    ins = te.matmul(oreg[:, n * 512:(n + 1) * 512], ones_bf[:],
                                    bo_bf[:, n * 512:(n + 1) * 512],
                                    start=False, stop=True)
                    if last:
                        ins.then_inc(s_pe.sem, 1)
                ph.add("tensor", mk)
            v_vec = s_vec.inc()
            ph.add("vector", lambda v, p=p, oreg=oreg, w=v_pe, win=v_in:
                   (v.wait_ge(s_pe.sem, w),
                    v.wait_ge(s_in.sem, win),
                    v.scalar_tensor_tensor(r1_f32[:, 0, :], oreg, 1.0,
                                           x_f32[:, p, :], ALU.mult, ALU.add,
                                           accum_out=lnv(p, 0, 1))
                    .then_inc(s_vec.sem, 1)))
            xread_hist[p] = v_vec
            v_act = s_act.inc()
            ph.add("scalar", lambda sc, p=p, w=v_vec:
                   (sc.wait_ge(s_vec.sem, w),
                    sc.activation(junk_bf[:], r1_f32[:, 0, :], AF.Square,
                                  accum_out=lnv(p, 1, 2)).then_inc(s_act.sem, 1)))
            v_vec2 = chain(ph, "vector", s_vec, [
                lambda v, p=p: v.tensor_scalar(lnv(p, 2, 3), lnv(p, 0, 1),
                                               1.0 / D, None, ALU.mult),
                lambda v, p=p: v.tensor_scalar(lnv(p, 3, 4), lnv(p, 1, 2),
                                               1.0 / D, None, ALU.mult),
                lambda v, p=p: v.tensor_mul(lnv(p, 4, 5), lnv(p, 2, 3),
                                            lnv(p, 2, 3)),
                lambda v, p=p: v.tensor_sub(lnv(p, 3, 4), lnv(p, 3, 4),
                                            lnv(p, 4, 5)),
            ], waits=[(s_act.sem, v_act)])
            v_act2 = chain(ph, "scalar", s_act, [
                lambda sc, p=p: sc.activation(lnv(p, 5, 6), lnv(p, 3, 4),
                                              AF.Ln, bias=eps_sb[:]),
                lambda sc, p=p: sc.activation(lnv(p, 6, 7), lnv(p, 5, 6),
                                              AF.Exp, scale=-0.5),
            ], waits=[(s_vec.sem, v_vec2)])
            v_vec3 = chain(ph, "vector", s_vec, [
                lambda v, p=p: v.tensor_mul(lnv(p, 7, 8), lnv(p, 2, 3),
                                            lnv(p, 6, 7)),
                lambda v, p=p: v.tensor_scalar(lnv(p, 7, 8), lnv(p, 7, 8),
                                               -1.0, None, ALU.mult),
                lambda v, p=p: v.tensor_scalar(r1n_f32[:, 0, :], r1_f32[:, 0, :],
                                               lnv(p, 6, 7), lnv(p, 7, 8),
                                               ALU.mult, ALU.add),
            ], waits=[(s_act.sem, v_act2), (s_out.sem, r1n_dma_hist[p])])
            r1n_dma_hist[p] = s_out.inc(16)
            ph.add("sync", lambda g, tok=tok, p=p, w=v_vec3:
                   (g.wait_ge(s_vec.sem, w),
                    g.dma_start(r1n_h[tok * 128:(tok + 1) * 128, :],
                                r1n_f32[:, 0, :]).then_inc(s_out.sem, 16)))
        if dbg:
            ph.commit()
        if dbg and st == 0:
            qT_d = nc.dram_tensor("qT_d", [128, MD * ST], BF16, kind="ExternalOutput")
            stt_d = nc.dram_tensor("stt_d", [128, 2 * (4 * 4 + 1 + 2)], F32, kind="ExternalOutput")
            Dd_d = nc.dram_tensor("Dd_d", [128, 2 * 4 * 128], BF16, kind="ExternalOutput")
            P_d = nc.dram_tensor("P_d", [128, S], BF16, kind="ExternalOutput")
            wT_d = nc.dram_tensor("wT_d", [128, NT * ST], BF16, kind="ExternalOutput")
            pv_d = nc.dram_tensor("pv_d", [128, KD * ST], BF16, kind="ExternalOutput")
            ph = Phase(nc)
            s_out.inc(96)
            ph.add("sync", lambda g:
                   (g.dma_start(stt_d[:, 0:8], nm4[:].rearrange("p a b -> p (a b)"))
                    .then_inc(s_out.sem, 16),
                    g.dma_start(stt_d[:, 8:16], sums4[:].rearrange("p a b -> p (a b)"))
                    .then_inc(s_out.sem, 16),
                    g.dma_start(stt_d[:, 16:24], al4[:].rearrange("p a b -> p (a b)"))
                    .then_inc(s_out.sem, 16),
                    g.dma_start(stt_d[:, 24:32], st4[:].rearrange("p a b -> p (a b)"))
                    .then_inc(s_out.sem, 16),
                    g.dma_start(stt_d[:, 32:36],
                                sto[:].rearrange("p a b -> p (a b)"))
                    .then_inc(s_out.sem, 16),
                    g.dma_start(Dd_d[:], D_sb[:].rearrange("p a b c -> p (a b c)"))
                    .then_inc(s_out.sem, 16),
                    g.dma_start(qT_d[:], qT[:].rearrange("p a b -> p (a b)"))
                    .then_inc(s_out.sem, 16),
                    g.dma_start(P_d[:], P_sb[:, 1, :]).then_inc(s_out.sem, 16),
                    g.dma_start(wT_d[:], wT_sb[:].rearrange("p a b -> p (a b)"))
                    .then_inc(s_out.sem, 16),
                    g.dma_start(pv_d[:], pv_sb[:].rearrange("p a b -> p (a b)"))
                    .then_inc(s_out.sem, 16),
                    g.wait_ge(s_out.sem, s_out.n)))
            ph.commit()

    if not dbg:
        ph.commit()
    for cm in reversed(actx):
        cm.__exit__(None, None, None)

    fctx = []
    gw_bf = alloc([128, KD, E], BF16, "gw_bf", fctx)
    rq_f32 = alloc([128, TQ, D], F32, "rq_f32", fctx)
    rqT = alloc([128, KD, QU], BF16, "rqT", fctx)
    acc = alloc([128, TQ, D], F32, "acc", fctx)
    su_sb = alloc([128, MF, QU], BF16, "su_sb", fctx)
    gu_sb = alloc([128, 4, KD, 128], BF16, "gu_sb", fctx)
    wd_sb = alloc([128, 2, MF, D], BF16, "wd_sb", fctx)
    gst = alloc([128, QU], BF16, "gst", fctx)
    lg_fm = alloc([E, QU], F32, "lg_fm", fctx)
    lg_tm = alloc([128, TQ, E], F32, "lg_tm", fctx)
    comb = alloc([128, TQ, E], F32, "comb", fctx)
    cwk = alloc([128, 4 * E], F32, "cwk", fctx)
    out_sb = alloc([128, 2, D], F32, "out_sb", fctx)
    rq_bf = alloc([128, 2, D], BF16, "rq_bf", fctx)

    r1n_done = s_out.n
    ln2_v3 = {t: 0 for t in range(TQ)}

    first_q = True
    ph = Phase(nc)
    for qu in range(NU):
        if first_q:
            gload(gw_bf[:], gw_h[:])
            first_q = False
        for t in range(TQ):
            p = t % 2
            v_in = s_in.inc(16)
            ph.add("sync", lambda g, qu=qu, t=t, w=r1n_done,
                   wv=ln2_v3[t]:
                   (g.wait_ge(s_out.sem, w),
                    g.wait_ge(s_vec.sem, wv),
                    g.dma_start(rq_f32[:, t, :],
                                r1n_h[(qu * TQ + t) * 128:
                                      (qu * TQ + t + 1) * 128, :])
                    .then_inc(s_in.sem, 16)))
            v_act = s_act.inc()
            ph.add("scalar", lambda sc, t=t, p=p, w=v_in, wpe=s_pe.n:
                   (sc.wait_ge(s_in.sem, w),
                    sc.wait_ge(s_pe.sem, wpe),
                    sc.copy(rq_bf[:, p, :], rq_f32[:, t, :])
                    .then_inc(s_act.sem, 1)))
            for half in range(2):
                reg = C[:, half * 512:(half + 1) * 512]
                v_pe = None
                for f4 in range(4):
                    f = half * 4 + f4
                    last = (f4 == 3)
                    if last:
                        v_pe = s_pe.inc()

                    def mk(te, p=p, f=f, f4=f4, reg=reg, w=v_act,
                           wv=s_vec.n, wg=s_w.n, last=last):
                        te.wait_ge(s_act.sem, w)
                        te.wait_ge(s_vec.sem, wv)
                        te.wait_ge(s_w.sem, wg)
                        ins = te.matmul(reg[:, f4 * 128:(f4 + 1) * 128],
                                        rq_bf[:, p, f * 128:(f + 1) * 128],
                                        ident_bf[:], start=True, stop=True)
                        if last:
                            ins.then_inc(s_pe.sem, 1)
                    ph.add("tensor", mk)
                s_vec.inc()
                ph.add("vector", lambda v, t=t, half=half, reg=reg, w=v_pe:
                       (v.wait_ge(s_pe.sem, w),
                        v.tensor_copy(
                            rqT[:, half * 4:half * 4 + 4, t * 128:(t + 1) * 128],
                            reg.rearrange("p (a b) -> p a b", a=4))
                        .then_inc(s_vec.sem, 1)))

        for n in range(NQ512):
            lreg = Dr[0:E, n * QW:(n + 1) * QW]
            v_pe = None
            for k in range(KD):
                last = (k == KD - 1)
                if last:
                    v_pe = s_pe.inc()

                def mk(te, n=n, k=k, lreg=lreg, wv=s_vec.n, wg=s_w.n, last=last):
                    te.wait_ge(s_vec.sem, wv)
                    te.wait_ge(s_w.sem, wg)
                    ins = te.matmul(lreg, gw_bf[:, k, :],
                                    rqT[:, k, n * QW:(n + 1) * QW],
                                    start=(k == 0), stop=last)
                    if last:
                        ins.then_inc(s_pe.sem, 1)
                ph.add("tensor", mk)
            s_vec.inc()
            ph.add("vector", lambda v, n=n, lreg=lreg, w=v_pe:
                   (v.wait_ge(s_pe.sem, w),
                    v.tensor_copy(lg_fm[:, n * QW:(n + 1) * QW], lreg)
                    .then_inc(s_vec.sem, 1)))
        for t in range(TQ):
            treg = C[:, (t % 2) * 512:(t % 2) * 512 + 128]
            v_pe = s_pe.inc()
            ph.add("tensor", lambda te, t=t, treg=treg, w=s_vec.n:
                   (te.wait_ge(s_vec.sem, w),
                    te.matmul(treg[:, 0:E], lg_fm[:, t * 128:(t + 1) * 128],
                              ident_f[0:E, 0:E], start=True, stop=True)
                    .then_inc(s_pe.sem, 1)))
            v_vec = s_vec.inc()
            ph.add("vector", lambda v, t=t, treg=treg, w=v_pe:
                   (v.wait_ge(s_pe.sem, w),
                    v.tensor_copy(lg_tm[:, t, :], treg[:, 0:E])
                    .then_inc(s_vec.sem, 1)))
            c0, c1, c2 = cwk[:, 0:E], cwk[:, E:2 * E], cwk[:, 2 * E:3 * E]
            m1, m2, dw = lnst[:, 0:1], lnst[:, 1:2], lnst[:, 2:3]
            w1, w2 = lnst[:, 3:4], lnst[:, 4:5]
            v_vec2 = chain(ph, "vector", s_vec, [
                lambda v, t=t, m1=m1: v.tensor_reduce(
                    m1, lg_tm[:, t, :], AX.X, ALU.max),
                lambda v, t=t, c0=c0, m1=m1: v.tensor_scalar(
                    c0, lg_tm[:, t, :], m1, None, ALU.is_equal),
                lambda v, t=t, c0=c0, c1=c1: v.scalar_tensor_tensor(
                    c1, c0, -1e30, lg_tm[:, t, :], ALU.mult, ALU.add),
                lambda v, c1=c1, m2=m2: v.tensor_reduce(m2, c1, AX.X, ALU.max),
                lambda v, m1=m1, m2=m2, dw=dw: v.tensor_sub(dw, m1, m2),
            ], waits=[(s_vec.sem, v_vec)])
            v_act = s_act.inc()
            ph.add("scalar", lambda sc, w=v_vec2, w1=w1, dw=dw:
                   (sc.wait_ge(s_vec.sem, w),
                    sc.activation(w1, dw, AF.Exp, scale=-1.0)
                    .then_inc(s_act.sem, 1)))
            chain(ph, "vector", s_vec, [
                lambda v, w1=w1: v.tensor_scalar(
                    w1, w1, 1.0, None, ALU.add),
                lambda v, w1=w1: v.reciprocal(w1, w1),
                lambda v, w1=w1, w2=w2: v.tensor_scalar(
                    w2, w1, -1.0, 1.0, ALU.mult, ALU.add),
                lambda v, c1=c1, c2=c2, m2=m2: v.tensor_scalar(
                    c2, c1, m2, None, ALU.is_equal),
                lambda v, c0=c0, w1=w1: v.tensor_scalar(
                    c0, c0, w1, None, ALU.mult),
                lambda v, t=t, c0=c0, c2=c2, w2=w2: v.scalar_tensor_tensor(
                    comb[:, t, :], c2, w2, c0, ALU.mult, ALU.add),
            ], waits=[(s_act.sem, v_act)])

        lg_done = s_vec.n
        gu_pe_hist = [0] * 4
        rd = {k: (s_vec.sem, lg_done) for k in "ABCD"}
        regs = {"A": A, "B": B, "C": C, "D": Dr}
        acc_hist = {0: 0, 1: 0}
        gst_hist = 0
        wd_v = {}
        for ei in range(NEXP):
            if ei == 0:
                wd_v[0] = s_w.inc(16)
                ph.add("gpsimd", lambda g, w=s_pe.n:
                       (g.wait_ge(s_pe.sem, w),
                        g.dma_start(wd_sb[:, 0, :, :], wd_h[0])
                        .then_inc(s_w.sem, 16)))
            v_wd = wd_v[ei]
            su_v = {}
            for mp in range(MF):
                v_pe = None
                v_pe_g = None
                for gi in range(2):
                    ms = mp * 2 + gi
                    bufi = ms % 4
                    v_g = s_w.inc(16)
                    ph.add("gpsimd", lambda g, ei=ei, ms=ms, bufi=bufi,
                           w=gu_pe_hist[bufi]:
                           (g.wait_ge(s_pe.sem, w),
                            g.dma_start(gu_sb[:, bufi, :, :], gu_h[ei, ms])
                            .then_inc(s_w.sem, 16)))
                    if mp == 3 and gi == 0 and ei + 1 < NEXP:
                        wd_v[ei + 1] = s_w.inc(16)
                        ph.add("gpsimd", lambda g, ei=ei, w=s_pe.n:
                               (g.wait_ge(s_pe.sem, w),
                                g.dma_start(wd_sb[:, (ei + 1) % 2, :, :],
                                            wd_h[ei + 1]).then_inc(s_w.sem, 16)))
                    rname = ("A", "C")[mp % 2] if gi == 0 else ("B", "D")[mp % 2]
                    greg = regs[rname]
                    rsem, rval = rd[rname]
                    for n in range(NQ512):
                        for k in range(KD):
                            last = (k == KD - 1 and n == NQ512 - 1)
                            if last:
                                v_pe = s_pe.inc()
                                if gi == 0:
                                    v_pe_g = v_pe

                            def mk(te, bufi=bufi, n=n, k=k, greg=greg,
                                   w=v_g, rsem=rsem, rval=rval, last=last):
                                te.wait_ge(s_w.sem, w)
                                te.wait_ge(rsem, rval)
                                ins = te.matmul(
                                    greg[:, n * QW:(n + 1) * QW],
                                    gu_sb[:, bufi, k, :],
                                    rqT[:, k, n * QW:(n + 1) * QW],
                                    start=(k == 0), stop=(k == KD - 1))
                                if last:
                                    ins.then_inc(s_pe.sem, 1)
                            ph.add("tensor", mk)
                    gu_pe_hist[bufi] = v_pe
                rname_g = ("A", "C")[mp % 2]
                rname_u = ("B", "D")[mp % 2]
                v_act = s_act.inc()
                ph.add("scalar", lambda sc, w=v_pe_g, wg=gst_hist,
                       greg_g=regs[rname_g]:
                       (sc.wait_ge(s_pe.sem, w),
                        sc.wait_ge(s_vec.sem, wg),
                        sc.activation(gst[:], greg_g[:, 0:QU], AF.Silu)
                        .then_inc(s_act.sem, 1)))
                rd[rname_g] = (s_act.sem, v_act)
                v_vec = s_vec.inc()
                ph.add("vector", lambda v, mp=mp, w=v_act, wpe=v_pe,
                       greg_u=regs[rname_u]:
                       (v.wait_ge(s_act.sem, w),
                        v.wait_ge(s_pe.sem, wpe),
                        v.tensor_mul(su_sb[:, mp, :], greg_u[:, 0:QU], gst[:])
                        .then_inc(s_vec.sem, 1)))
                rd[rname_u] = (s_vec.sem, v_vec)
                gst_hist = v_vec
                su_v[mp] = v_vec
            su_done = s_vec.n
            for t in range(TQ):
                rname = ("A", "B")[t % 2]
                yreg = regs[rname]
                rsem, rval = rd[rname]
                v_pe = None
                for k in range(MF):
                    for n in range(2):
                        last = (k == MF - 1 and n == 1)
                        if last:
                            v_pe = s_pe.inc()

                        def mk(te, t=t, k=k, n=n, ei=ei, yreg=yreg,
                               w=max(su_v[k], acc_hist[t % 2]), wwd=v_wd,
                               rsem=rsem, rval=rval, last=last):
                            te.wait_ge(s_vec.sem, w)
                            te.wait_ge(rsem, rval)
                            te.wait_ge(s_w.sem, wwd)
                            ins = te.matmul(yreg[:, n * 512:(n + 1) * 512],
                                            su_sb[:, k, t * 128:(t + 1) * 128],
                                            wd_sb[:, ei % 2, k,
                                                  n * 512:(n + 1) * 512],
                                            start=(k == 0), stop=(k == MF - 1))
                            if last:
                                ins.then_inc(s_pe.sem, 1)
                        ph.add("tensor", mk)
                v_y = s_vec.inc()
                if ei == 0:
                    ph.add("vector", lambda v, t=t, yreg=yreg, w=v_pe:
                           (v.wait_ge(s_pe.sem, w),
                            v.tensor_copy(acc[:, t, :], yreg)
                            .then_inc(s_vec.sem, 1)))
                else:
                    ph.add("vector", lambda v, t=t, yreg=yreg, ei=ei, w=v_pe:
                           (v.wait_ge(s_pe.sem, w),
                            v.scalar_tensor_tensor(acc[:, t, :], yreg,
                                                   comb[:, t, ei - 1:ei],
                                                   acc[:, t, :],
                                                   ALU.mult, ALU.add)
                            .then_inc(s_vec.sem, 1)))
                acc_hist[t % 2] = v_y
                rd[rname] = (s_vec.sem, v_y)

        acc_last = s_vec.n
        for t in range(TQ):
            p = t % 2
            v_vec = s_vec.inc()
            ph.add("vector", lambda v, t=t, wa=acc_last:
                   (v.wait_ge(s_vec.sem, wa),
                    v.scalar_tensor_tensor(rq_f32[:, t, :], acc[:, t, :], 1.0,
                                           rq_f32[:, t, :], ALU.mult, ALU.add,
                                           accum_out=lnv(p, 0, 1))
                    .then_inc(s_vec.sem, 1)))
            v_act = s_act.inc()
            ph.add("scalar", lambda sc, t=t, p=p, w=v_vec:
                   (sc.wait_ge(s_vec.sem, w),
                    sc.activation(junk_bf[:], rq_f32[:, t, :], AF.Square,
                                  accum_out=lnv(p, 1, 2)).then_inc(s_act.sem, 1)))
            v_vec2 = chain(ph, "vector", s_vec, [
                lambda v, p=p: v.tensor_scalar(lnv(p, 2, 3), lnv(p, 0, 1),
                                               1.0 / D, None, ALU.mult),
                lambda v, p=p: v.tensor_scalar(lnv(p, 3, 4), lnv(p, 1, 2),
                                               1.0 / D, None, ALU.mult),
                lambda v, p=p: v.tensor_mul(lnv(p, 4, 5), lnv(p, 2, 3),
                                            lnv(p, 2, 3)),
                lambda v, p=p: v.tensor_sub(lnv(p, 3, 4), lnv(p, 3, 4),
                                            lnv(p, 4, 5)),
            ], waits=[(s_act.sem, v_act)])
            v_act2 = chain(ph, "scalar", s_act, [
                lambda sc, p=p: sc.activation(lnv(p, 5, 6), lnv(p, 3, 4),
                                              AF.Ln, bias=eps_sb[:]),
                lambda sc, p=p: sc.activation(lnv(p, 6, 7), lnv(p, 5, 6),
                                              AF.Exp, scale=-0.5),
            ], waits=[(s_vec.sem, v_vec2)])
            v_vec3 = chain(ph, "vector", s_vec, [
                lambda v, p=p: v.tensor_mul(lnv(p, 7, 8), lnv(p, 2, 3),
                                            lnv(p, 6, 7)),
                lambda v, p=p: v.tensor_scalar(lnv(p, 7, 8), lnv(p, 7, 8),
                                               -1.0, None, ALU.mult),
                lambda v, t=t, p=p: v.tensor_scalar(
                    out_sb[:, p, :], rq_f32[:, t, :], lnv(p, 6, 7),
                    lnv(p, 7, 8), ALU.mult, ALU.add),
            ], waits=[(s_act.sem, v_act2), (s_out.sem, out_dma_hist[p])])
            ln2_v3[t] = v_vec3
            out_dma_hist[p] = s_out.inc(16)
            ph.add("sync", lambda g, qu=qu, t=t, p=p, w=v_vec3:
                   (g.wait_ge(s_vec.sem, w),
                    g.dma_start(y_h[(qu * TQ + t) * 128:
                                    (qu * TQ + t + 1) * 128, :],
                                out_sb[:, p, :]).then_inc(s_out.sem, 16)))
    ph.commit()

    ph = Phase(nc)
    ph.add("sync", lambda g, w=s_out.n: g.wait_ge(s_out.sem, w))
    ph.commit()

    for cm in reversed(fctx):
        cm.__exit__(None, None, None)
    for cm in reversed(ctx):
        cm.__exit__(None, None, None)
    return nc


def prep_core_inputs(inputs, S):
    f = np.ascontiguousarray
    Wq = np.asarray(inputs["L0_attn_Wq"], dtype=np.float32)
    Wk = np.asarray(inputs["L0_attn_Wk"], dtype=np.float32)
    Wv = np.asarray(inputs["L0_attn_Wv"], dtype=np.float32)
    Wo = np.asarray(inputs["L0_attn_Wo"], dtype=np.float32)
    bq = np.asarray(inputs["L0_attn_bq"], dtype=np.float32)
    bk = np.asarray(inputs["L0_attn_bk"], dtype=np.float32)
    bv = np.asarray(inputs["L0_attn_bv"], dtype=np.float32)
    bo = np.asarray(inputs["L0_attn_bo"], dtype=np.float32)
    gate_w = np.asarray(inputs["L0_ffn_gate_w"], dtype=np.float32)
    w_up = np.asarray(inputs["L0_ffn_w_up"], dtype=np.float32)
    w_down = np.asarray(inputs["L0_ffn_w_down"], dtype=np.float32)
    sg_gate = np.asarray(inputs["L0_ffn_sg_gate"], dtype=np.float32)
    sg_up = np.asarray(inputs["L0_ffn_sg_up"], dtype=np.float32)
    sg_down = np.asarray(inputs["L0_ffn_sg_down"], dtype=np.float32)

    def tile_lhst(W):
        i_, o_ = W.shape
        return f(W.reshape(i_ // 128, 128, o_ // 128, 128).transpose(1, 0, 2, 3))

    bkv = np.stack([np.concatenate([bk, bk]), np.concatenate([bv, bv])], axis=1)
    common = {
        "wq": tile_lhst(Wq),
        "wo": tile_lhst(Wo),
        "wk": f(Wk.reshape(KD, 128, DK).transpose(1, 0, 2)),
        "wv": f(Wv.reshape(KD, 128, DK).transpose(1, 0, 2)),
        "bq": f(bq.reshape(MD, 128).T),
        "bkv": f(bkv.astype(np.float32)),
        "bo": bo.reshape(1, D),
        "gw": f(gate_w.T.reshape(KD, 128, E).transpose(1, 0, 2)),
        "ident": np.eye(128, dtype=np.float32),
        "ones": np.ones((1, 128), dtype=np.float32),
    }

    gu_list, wd_list = [], []
    order = np.empty(2 * MF, dtype=np.int64)
    order[0::2] = np.arange(MF)
    order[1::2] = np.arange(MF) + MF
    for e in range(NEXP):
        if e == 0:
            wu = np.concatenate([sg_gate, sg_up], axis=0)
            wdn = sg_down
        else:
            wu = w_up[e - 1]
            wdn = w_down[e - 1]
        g4 = wu.reshape(2 * MF, 128, KD, 128).transpose(0, 3, 2, 1)
        gu_list.append(g4[order])
        wd_list.append(wdn.T.reshape(MF, 128, D).transpose(1, 0, 2))
    common["gu"] = f(np.stack(gu_list, axis=0))
    common["wd"] = f(np.stack(wd_list, axis=0))

    x = np.asarray(inputs["L0_attn_x"], dtype=np.float32)
    in_maps = []
    for c in range(x.shape[0]):
        m = dict(common)
        m["x"] = f(x[c, :S, :])
        in_maps.append(m)
    return in_maps


def kernel(**inputs):
    S = 4096
    nc = build_nc(S)
    in_maps = prep_core_inputs(inputs, S)
    res = run_bass_kernel_spmd(nc, in_maps, core_ids=list(range(8)))
    out = np.stack([np.asarray(res.results[c]["y"]) for c in range(8)], axis=0)
    return out
